# version 01
# baseline (speedup 1.0000x reference)
import numpy as np
import ml_dtypes
import concourse.bass as bass
import concourse.mybir as mybir
from concourse.bass_utils import run_bass_kernel_spmd
from contextlib import ExitStack

F32 = mybir.dt.float32
BF16 = mybir.dt.bfloat16
AF = mybir.ActivationFunctionType
ALU = mybir.AluOpType
AX = mybir.AxisListType

D = 1024
S = 4096
NH = 16
HD = 64
FF = 2048
NE = 8
EPS_LN = 1e-5

SEMS = ("L", "Y", "P", "A", "V")
PHASES = ("pc", "A", "B", "C", "D", "E", "F")
DBG_EXPERTS = None


class Plan:
    def __init__(self):
        self.q = {k: [] for k in ("te", "sc", "v", "g", "sy")}
        self.n = dict.fromkeys(SEMS, 0)
        self.bank = {}

    def on(self, eng):
        def deco(fn):
            self.q[eng].append(fn)
        return deco

    def inc(self, s, by=1):
        self.n[s] += by
        return self.n[s]

    def free_at(self, res, sem, val):
        self.bank[res] = (sem, val)

    def take(self, res):
        w = self.bank.pop(res, None)
        return [w] if w else []

    def barrier(self):
        tgt = dict(self.n)
        for e in self.q:
            @self.on(e)
            def _(eng, SM, tgt=tgt):
                for s, v in tgt.items():
                    if v:
                        eng.wait_ge(SM[s], v)


def op(pl, ek, fn, wait=(), inc=None):
    tgt = pl.inc(inc, 1) if inc else None
    wait = tuple(wait)

    @pl.on(ek)
    def _(eng, SM, fn=fn, wait=wait, tgt=tgt, inc=inc):
        for s, v in wait:
            if v:
                eng.wait_ge(SM[s], v)
        ins = fn(eng)
        if tgt is not None:
            ins.then_inc(SM[inc], 1)
    return tgt


def dma(pl, ek, sem, out_ap, in_ap, wait=()):
    tgt = pl.inc(sem, 16)
    wait = tuple(wait)

    @pl.on(ek)
    def _(eng, SM, out_ap=out_ap, in_ap=in_ap, wait=wait, sem=sem):
        for s, v in wait:
            if v:
                eng.wait_ge(SM[s], v)
        eng.dma_start(out=out_ap, in_=in_ap).then_inc(SM[sem], 16)
    return tgt


def vchain(pl, steps, wait=()):
    prev = None
    first = True
    for fn in steps:
        w = list(wait) if first else [("V", prev)]
        prev = op(pl, "v", fn, wait=w, inc="V")
        first = False
    return prev


def mmg(pl, out_ap, ops, wait=(), inc=True, start=True, stop=True):
    tgt = pl.inc("P", 1) if inc else None
    wait = tuple(wait)
    ops = tuple(ops)

    @pl.on("te")
    def _(eng, SM, out_ap=out_ap, ops=ops, wait=wait, tgt=tgt, start=start, stop=stop):
        for s, v in wait:
            if v:
                eng.wait_ge(SM[s], v)
        n = len(ops)
        ins = None
        for i, (l, r) in enumerate(ops):
            ins = eng.matmul(out_ap, l, r, start=(start and i == 0),
                             stop=(stop and i == n - 1))
        if tgt is not None:
            ins.then_inc(SM["P"], 1)
    return tgt


DECLARED = []


def build_program(phases=None):
    global DECLARED
    if phases is None:
        phases = PHASES
    nc = bass.Bass("TRN2", target_bir_lowering=False, debug=False)
    DECLARED = []

    def inp(name, shape, dt=F32):
        DECLARED.append(name)
        return nc.dram_tensor(name, list(shape), dt, kind="ExternalInput").ap()

    need_l0ffn = ("A" in phases) or ("C" in phases)
    need_l0att = "B" in phases
    need_l1ffn = ("D" in phases) or ("F" in phases)
    need_l1att = "E" in phases

    xT = inp("xT", (D, S))
    if need_l0att:
        wqT = inp("wqT", (D, D)); wkT = inp("wkT", (D, D))
        wvT = inp("wvT", (D, D)); woT = inp("woT", (D, D))
        bq = inp("bq", (D,)); bk = inp("bk", (D,)); bv = inp("bv", (D,)); bo = inp("bo", (D,))
        pw1T = inp("pw1T", (3, D, 512)); pb1 = inp("pb1", (3, 512))
        pw2T = inp("pw2T", (3, 512, NH)); pb2 = inp("pb2", (3, NH))
        sw1T = inp("sw1T", (D, 512)); sb1 = inp("sb1", (512,))
        sw2T = inp("sw2T", (512, 3)); sb2 = inp("sb2", (3,))
    if need_l0ffn:
        sgT = inp("sgT", (D, FF)); suT = inp("suT", (D, FF)); sdT = inp("sdT", (FF, D))
        gw = inp("gw", (NE, D, FF)); uw = inp("uw", (NE, D, FF)); dw = inp("dw", (NE, FF, D))
        rwT = inp("rwT", (D, NE)); rb = inp("rb", (NE,))
    if need_l1att:
        w2qT = inp("w2qT", (D, D)); w2kT = inp("w2kT", (D, D))
        w2vT = inp("w2vT", (D, D)); w2oT = inp("w2oT", (D, D))
    if need_l1ffn:
        W1 = inp("W1", (NE, D, FF)); W3 = inp("W3", (NE, D, FF)); W2 = inp("W2", (NE, FF, D))
        b1 = inp("b1", (NE, FF)); b3 = inp("b3", (NE, FF)); b2 = inp("b2", (NE, D))
        Wg = inp("Wg", (D, NE)); bg = inp("bg", (NE,))
        Wf = inp("Wf", (D, D)); bf = inp("bf", (D,))
    identF = inp("identF", (128, 128))
    identB = inp("identB", (128, 128), BF16)
    maskL0 = inp("maskL0", (4, 128, 512), BF16)
    mask128 = inp("mask128", (128, 128), BF16)

    y = nc.dram_tensor("y", [D, S], F32, kind="ExternalOutput").ap()

    def scratch(name, shape, dt):
        return nc.dram_tensor(name, list(shape), dt, kind="Internal").ap()

    resid = scratch("resid", (D, S), F32)
    qTd = scratch("qTd", (D, S), BF16)
    kTd = scratch("kTd", (D, S), BF16)
    vTd = scratch("vTd", (D, S), BF16)
    aTd = scratch("aTd", (D, S), BF16)
    c_sg = scratch("c_sg", (D, FF), BF16); c_su = scratch("c_su", (D, FF), BF16)
    c_sd = scratch("c_sd", (FF, D), BF16)
    c_gw = scratch("c_gw", (NE, D, FF), BF16); c_uw = scratch("c_uw", (NE, D, FF), BF16)
    c_dw = scratch("c_dw", (NE, FF, D), BF16)
    c_W1 = scratch("c_W1", (NE, D, FF), BF16); c_W3 = scratch("c_W3", (NE, D, FF), BF16)
    c_W2 = scratch("c_W2", (NE, FF, D), BF16)
    c_Wf = scratch("c_Wf", (D, D), BF16)

    ctx = ExitStack()
    with ctx:
        def sb(name, shape, dt):
            return ctx.enter_context(nc.sbuf_tensor(name, list(shape), dt)).ap()

        xs = sb("xs", (128, 8192), F32)
        xh = sb("xh", (128, 8192), BF16)
        wsl = sb("wsl", (128, 49152), BF16)
        hb = sb("hb", (128, 12288), BF16)
        scr = sb("scr", (128, 2048), BF16)
        scf = sb("scf", (128, 512), F32)
        cmask = sb("cmask", (128, 2176), BF16)
        idF = sb("idF", (128, 128), F32)
        idB = sb("idB", (128, 128), BF16)
        onec_b = sb("onec_b", (128, 1), BF16)
        oner_b = sb("oner_b", (1, 512), BF16)
        oner_b128 = sb("oner_b128", (1, 128), BF16)
        one64_b = sb("one64_b", (64, 1), BF16)
        one_b = sb("one_b", (1, 1), BF16)
        smf = sb("smf", (128, 128), F32)
        smb = sb("smb", (128, 64), BF16)
        bcol = sb("bcol", (128, 64), F32)
        oner_f = sb("oner_f", (1, 128), F32)
        polsm = sb("polsm", (1, 64), F32)
        r_px = sb("r_px", (1, 128), F32)
        rw_sb = sb("rw_sb", (128, 64), BF16)
        rb_sb = sb("rb_sb", (8, 1), F32)
        bg_sb = sb("bg_sb", (8, 1), F32)
        b2_sb = sb("b2_sb", (8, 1024), BF16)
        wT_sb = sb("wT_sb", (8, 1024), BF16)
        polv = sb("polv", (16, 8), F32)
        pw2_sb = sb("pw2_sb", (128, 256), BF16)
        r_ms = sb("r_ms", (1, 1024), BF16)
        r_t0 = sb("r_t0", (1, 1024), BF16)
        rb_nm = sb("rb_nm", (1, 1024), BF16)
        rb_r = sb("rb_r", (1, 1024), BF16)
        w_row1 = sb("w_row1", (1, 1024), BF16)
        bst = sb("bst", (16, 128), BF16)
        b1col = sb("b1col", (128, 16), F32)
        b3col = sb("b3col", (128, 16), F32)
        bfcol = sb("bfcol", (128, 8), F32)
        pbcol = sb("pbcol", (128, 16), F32)
        pb2c = sb("pb2c", (16, 4), F32)
        r_pol = sb("r_pol", (1, 64), F32)
        r_hg = sb("r_hg", (1, 16), F32)
        r_sc = sb("r_sc", (1, 16), F32)
        kv_sb = sb("kv_sb", (64, 64), F32)
        ks_sb = sb("ks_sb", (64, 1), F32)
        at_sb = sb("at_sb", (128, 128), BF16)
        kt_sb = sb("kt_sb", (128, 64), BF16)
        vt_sb = sb("vt_sb", (128, 64), BF16)
        ps = [ctx.enter_context(nc.psum_tensor("ps%d" % i, [128, 512], F32)).ap()
              for i in range(6)]
        pt0 = ctx.enter_context(nc.psum_tensor("pt0", [128, 1024], BF16)).ap()
        pt1 = ctx.enter_context(nc.psum_tensor("pt1", [128, 1024], BF16)).ap()

        SM = {s_: ctx.enter_context(nc.semaphore(s_)) for s_ in SEMS}
        pl = Plan()

        def slotv(i, cols=16384):
            return wsl[:, i * 16384:i * 16384 + cols]

        def r3(ap_, cols):
            return ap_.rearrange("(c p) t -> p c t", p=128)

        cpool = sb("cpool", (128, 2), F32)
        nc.const_aps.register(F32, EPS_LN, cpool[:, 0:1])
        nc.const_aps.register(F32, 1e-6, cpool[:, 1:2])
        op(pl, "v", lambda e: e.memset(cpool[:, 0:1], EPS_LN))
        op(pl, "v", lambda e: e.memset(cpool[:, 1:2], 1e-6))
        op(pl, "v", lambda e: e.memset(onec_b, 1.0))
        op(pl, "v", lambda e: e.memset(oner_b, 1.0))
        op(pl, "v", lambda e: e.memset(oner_b128, 1.0))
        op(pl, "v", lambda e: e.memset(one64_b, 1.0))
        op(pl, "v", lambda e: e.memset(oner_f, 1.0))
        op(pl, "v", lambda e: e.memset(one_b, 1.0), inc="V")
        dma(pl, "sy", "Y", idF[:, :], identF[:, :])
        dma(pl, "sy", "Y", idB[:, :], identB[:, :])
        dma(pl, "sy", "Y", cmask[:, 0:2048].rearrange("p (v t) -> p v t", v=4),
            maskL0.rearrange("v p t -> p v t"))
        dma(pl, "sy", "Y", cmask[:, 2048:2176], mask128[:, :])
        pl.barrier()

        if "pc" in phases:
            pairs = []
            if need_l0ffn:
                pairs += [(sgT, c_sg), (suT, c_su), (sdT, c_sd)]
                for e_ in range(NE):
                    pairs += [(gw[e_], c_gw[e_]), (uw[e_], c_uw[e_]), (dw[e_], c_dw[e_])]
            if need_l1ffn:
                for e_ in range(NE):
                    pairs += [(W1[e_], c_W1[e_]), (W3[e_], c_W3[e_]), (W2[e_], c_W2[e_])]
                pairs += [(Wf, c_Wf)]
            nbuf = 0
            lastst = {}
            for (src, dst) in pairs:
                rows, cols = src.shape
                for r0 in range(0, rows, 128):
                    b = nbuf % 6
                    nbuf += 1
                    view = hb[:, b * 2048:b * 2048 + cols]
                    lt = dma(pl, "g", "L", view, src[r0:r0 + 128, :],
                             wait=([("Y", lastst[b])] if b in lastst else []))
                    lastst[b] = dma(pl, "sy", "Y", dst[r0:r0 + 128, :], view,
                                    wait=[("L", lt)])
            pl.barrier()

        def ln_stripe(src, t0, TS):
            nsub = TS // 512
            lt = dma(pl, "sy", "Y",
                     xs.rearrange("p (c t) -> p c t", t=1024)[:, :, 0:TS],
                     r3(src, S)[:, :, t0:t0 + TS],
                     wait=pl.take("xs") + pl.take("hb") + pl.take("scr") + pl.take("xh"))
            xs3 = xs.rearrange("p (c t) -> p c t", t=1024)[:, :, 0:TS]
            xb3 = hb.rearrange("p (c t) -> p c t", t=1024)[:, 0:8, 0:TS]
            sq3 = xh.rearrange("p (c t) -> p c t", t=1024)[:, 0:8, 0:TS]
            a_xb = op(pl, "sc", lambda e: e.copy(xb3, xs3), wait=[("Y", lt)], inc="A")
            v_sq = op(pl, "v", lambda e: e.tensor_mul(sq3, xs3, xs3),
                      wait=[("Y", lt)], inc="V")
            for sub in range(nsub):
                o = sub * 512
                p_m = mmg(pl, ps[4][0:1, :],
                          [(onec_b, hb[:, c * 1024 + o:c * 1024 + o + 512]) for c in range(8)],
                          wait=[("A", a_xb)] + pl.take("ps4"))
                p_s = mmg(pl, ps[5][0:1, :],
                          [(onec_b, xh[:, c * 1024 + o:c * 1024 + o + 512])
                           for c in range(8)],
                          wait=[("V", v_sq)] + pl.take("ps5"))
                v_r = vchain(pl, [
                    lambda e, o=o: e.tensor_scalar(
                        rb_nm[:, o:o + 512], ps[4][0:1, :], -1.0 / D, None, ALU.mult),
                    lambda e, o=o: e.tensor_scalar(
                        r_ms[:, o:o + 512], ps[5][0:1, :], 1.0 / D, None, ALU.mult),
                ], wait=[("P", p_m), ("P", p_s)])
                pl.free_at("ps4", "V", v_r)
                pl.free_at("ps5", "V", v_r)
            op(pl, "v", lambda e: e.tensor_mul(r_t0[:, 0:TS], rb_nm[:, 0:TS], rb_nm[:, 0:TS]),
               wait=[("V", pl.n["V"])])
            v_var = op(pl, "v", lambda e: e.tensor_sub(
                r_t0[:, 0:TS], r_ms[:, 0:TS], r_t0[:, 0:TS]),
                wait=[("V", pl.n["V"])], inc="V")
            a_std = op(pl, "sc", lambda e: e.activation(
                r_ms[:, 0:TS], r_t0[:, 0:TS], AF.Sqrt, bias=EPS_LN),
                wait=[("V", v_var)], inc="A")
            def _recip(e):
                with nc.allow_low_precision(reason="LN rstd rows in bf16; tolerance allows"):
                    return e.reciprocal(rb_r[:, 0:TS], r_ms[:, 0:TS])
            v_rows = vchain(pl, [_recip], wait=[("A", a_std)])
            for sub in range(nsub):
                o = sub * 512
                p_b1 = mmg(pl, ps[4][:, :], [(oner_b128, rb_nm[0:1, o:o + 512])],
                           wait=[("V", v_rows)] + pl.take("ps4"))
                p_b2 = mmg(pl, ps[5][:, :], [(oner_b128, rb_r[0:1, o:o + 512])],
                           wait=pl.take("ps5"))
                steps = [lambda e: e.tensor_copy(scf[:, 0:512], ps[5][:, :])]
                for c in range(8):
                    steps.append(lambda e, c=c, o=o: e.tensor_add(
                        ps[5][:, :],
                        xs[:, c * 1024 + o:c * 1024 + o + 512], ps[4][:, :]))
                    steps.append(lambda e, c=c, o=o: e.tensor_mul(
                        xh[:, c * 1024 + o:c * 1024 + o + 512],
                        ps[5][:, :], scf[:, 0:512]))
                vv = vchain(pl, steps, wait=[("P", p_b2)])
                pl.free_at("ps4", "V", vv)
                pl.free_at("ps5", "V", vv)
            return pl.n["V"]

        def store_resid(dst, t0, TS, wait_v):
            st = dma(pl, "sy", "Y", r3(dst, S)[:, :, t0:t0 + TS],
                     xs.rearrange("p (c t) -> p c t", t=1024)[:, :, 0:TS],
                     wait=[("V", wait_v)])
            pl.free_at("xs", "Y", st)
            return st

        slot_state = {0: None, 1: None, 2: None}
        slot_freeP = {0: 0, 1: 0, 2: 0}

        def load_slot(i, key, src_ap, rows, cols):
            if slot_state[i] == key:
                return 0
            slot_state[i] = key
            view = wsl[:, i * 16384:i * 16384 + (rows // 128) * cols]
            return dma(pl, "g", "L", view.rearrange("p (c t) -> p c t", t=cols),
                       r3(src_ap, cols), wait=[("P", slot_freeP[i])])

        def invalidate_slots():
            for i in range(3):
                slot_state[i] = None

        def ffn_l0(src, dst, load_router):
            TS = 1024
            if load_router:
                dma(pl, "g", "L", rw_sb.rearrange("p (c e) -> p c e", e=8), r3(rwT, 8))
                dma(pl, "g", "L", rb_sb, rb.rearrange("(p o) -> p o", o=1))
            for st_i in range(S // TS):
                t0 = st_i * TS
                v_ln = ln_stripe(src, t0, TS)

                def bc(small, like):
                    return bass.broadcast_tensor_aps(like, small)[1]

                ptw_prev = 0
                for sub in range(2):
                    o = sub * 512
                    pr = mmg(pl, ps[4][0:8, :],
                             [(rw_sb[:, c * 8:(c + 1) * 8],
                               xh[:, c * 1024 + o:c * 1024 + o + 512]) for c in range(8)],
                             wait=[("V", v_ln), ("L", pl.n["L"])] + pl.take("ps4"))
                    a_pr = op(pl, "sc", lambda e: e.activation(
                        scf[0:8, 0:512], ps[4][0:8, :], AF.Sigmoid,
                        bias=rb_sb[:, 0:1]),
                        wait=[("P", pr)] + pl.take("scf_pr"), inc="A")
                    pl.free_at("ps4", "A", a_pr)
                    for j4 in range(4):
                        j = sub * 4 + j4
                        half = (j % 2) * 128
                        ptr = op(pl, "te", lambda e, j4=j4: e.transpose(
                            ps[5][:, (j4 % 2) * 8:(j4 % 2) * 8 + 8],
                            scf[0:8, j4 * 128:(j4 + 1) * 128], idF[0:8, 0:8]),
                            wait=[("A", a_pr)] +
                                 (pl.take("ps5") if j < 2 else [("V", pl.n["V"])]),
                            inc="P")
                        pv = smf[:, 8:16]
                        v_w = vchain(pl, [
                            lambda e, j4=j4: e.tensor_copy(
                                pv, ps[5][:, (j4 % 2) * 8:(j4 % 2) * 8 + 8]),
                            lambda e: e.reduce_max(smf[:, 0:1], pv, AX.X),
                            lambda e: e.tensor_tensor(
                                smf[:, 16:24], pv, bc(smf[:, 0:1], pv), ALU.is_ge),
                            lambda e: e.scalar_tensor_tensor(
                                smf[:, 24:32], smf[:, 16:24], -1e30, pv, ALU.mult, ALU.add),
                            lambda e: e.reduce_max(smf[:, 1:2], smf[:, 24:32], AX.X),
                            lambda e: e.tensor_add(smf[:, 2:3], smf[:, 0:1], smf[:, 1:2]),
                            lambda e: e.reciprocal(smf[:, 3:4], smf[:, 2:3]),
                            lambda e: e.tensor_tensor(
                                smf[:, 16:24], pv, bc(smf[:, 1:2], pv), ALU.is_ge),
                            lambda e: e.tensor_mul(smf[:, 24:32], pv, smf[:, 16:24]),
                            lambda e: e.tensor_tensor(
                                smf[:, 24:32], smf[:, 24:32], bc(smf[:, 3:4], pv), ALU.mult),
                            lambda e: e.tensor_scalar(
                                smb[:, 0:8], smf[:, 24:32], 0.5, None, ALU.mult),
                        ], wait=[("P", ptr)] + ([("P", ptw_prev)] if ptw_prev else []))
                        ptw = op(pl, "te", lambda e, half=half: e.transpose(
                            pt0[0:8, half:half + 128], smb[:, 0:8], idB),
                            wait=[("V", v_w)] + pl.take("pt0h%d" % (j % 2)), inc="P")
                        a_w = op(pl, "sc", lambda e, j=j, half=half: e.copy(
                            wT_sb[0:8, j * 128:(j + 1) * 128], pt0[0:8, half:half + 128]),
                            wait=[("P", ptw)], inc="A")
                        pl.free_at("pt0h%d" % (j % 2), "A", a_w)
                        ptw_prev = ptw
                    pl.free_at("scf_pr", "P", ptw_prev)
                pl.free_at("ps5", "V", pl.n["V"])
                a_wlast = pl.n["A"]
                order = list(range(9)) if st_i % 2 == 0 else list(range(8, -1, -1))
                if DBG_EXPERTS is not None:
                    order = [e_ for e_ in order if e_ in DBG_EXPERTS]
                for e_i in order:
                    if e_i == 8:
                        l_g = load_slot(0, "sg", c_sg, D, FF)
                        l_u = load_slot(1, "su", c_su, D, FF)
                        l_d = load_slot(2, "sd", c_sd, FF, D)
                        a_wbc = 0
                    else:
                        l_wr = dma(pl, "g", "L", w_row1[0:1, :], wT_sb[e_i:e_i + 1, :],
                                   wait=[("A", a_wlast)] + pl.take("w_row1"))
                        l_g = load_slot(0, ("g", e_i), c_gw[e_i], D, FF)
                        l_u = load_slot(1, ("u", e_i), c_uw[e_i], D, FF)
                        l_d = load_slot(2, ("d", e_i), c_dw[e_i], FF, D)
                        mmg(pl, ps[4][:, :], [(oner_b128, w_row1[0:1, 0:512])],
                            wait=[("L", l_wr)] + pl.take("ps4"), inc=False)
                        pw = mmg(pl, ps[5][:, :], [(oner_b128, w_row1[0:1, 512:1024])],
                                 wait=pl.take("ps5"))
                        pl.free_at("w_row1", "P", pw)
                        op(pl, "sc", lambda e: e.copy(scr[:, 0:512], ps[4][:, :]),
                           wait=[("P", pw)] + pl.take("wbc"))
                        a_wbc = op(pl, "sc", lambda e: e.copy(scr[:, 512:1024], ps[5][:, :]),
                                   inc="A")
                        pl.free_at("ps4", "A", a_wbc)
                        pl.free_at("ps5", "A", a_wbc)
                    for sub in range(2):
                        o = sub * 512
                        for hc in range(16):
                            bgi = "ps%d" % (hc % 2)
                            bui = "ps%d" % (2 + hc % 2)
                            sgr = hb[:, 10240 + (hc % 2) * 512:10240 + (hc % 2) * 512 + 512]
                            usr = hb[:, 8192 + (hc % 2) * 512:8192 + (hc % 2) * 512 + 512]
                            p_g = mmg(pl, ps[hc % 2],
                                      [(slotv(0)[:, c * 2048 + hc * 128:c * 2048 + hc * 128 + 128],
                                        xh[:, c * 1024 + o:c * 1024 + o + 512]) for c in range(8)],
                                      wait=([("L", l_g), ("V", v_ln)] if hc == 0 and sub == 0 else [])
                                      + pl.take(bgi))
                            p_u = mmg(pl, ps[2 + hc % 2],
                                      [(slotv(1)[:, c * 2048 + hc * 128:c * 2048 + hc * 128 + 128],
                                        xh[:, c * 1024 + o:c * 1024 + o + 512]) for c in range(8)],
                                      wait=([("L", l_u)] if hc == 0 and sub == 0 else [])
                                      + pl.take(bui))
                            ag = op(pl, "sc", lambda e, b_=ps[hc % 2], sgr=sgr: e.activation(
                                sgr, b_, AF.Silu),
                                wait=[("P", p_g)] + pl.take("sgr%d" % (hc % 2)), inc="A")
                            pl.free_at(bgi, "A", ag)
                            if e_i != 8:
                                au = op(pl, "v", lambda e, b_=ps[2 + hc % 2], usr=usr, o=o:
                                        e.scalar_tensor_tensor(
                                            usr, b_, 1.0, scr[:, o:o + 512],
                                            ALU.mult, ALU.mult),
                                        wait=[("P", p_u)] +
                                             ([("A", a_wbc)] if hc == 0 and sub == 0 else []),
                                        inc="V")
                            else:
                                au = op(pl, "v", lambda e, b_=ps[2 + hc % 2], usr=usr:
                                        e.tensor_scalar(usr, b_, 0.5, None, ALU.mult),
                                        wait=[("P", p_u)], inc="V")
                            pl.free_at(bui, "V", au)
                            vh_ = op(pl, "v", lambda e, hc=hc, sgr=sgr, usr=usr: e.tensor_mul(
                                hb[:, hc * 512:(hc + 1) * 512], sgr, usr),
                                wait=[("A", ag)], inc="V")
                            pl.free_at("sgr%d" % (hc % 2), "V", vh_)
                        v_h = pl.n["V"]
                        for oc in range(8):
                            bdi = "ps%d" % (4 + oc % 2)
                            p_d = mmg(pl, ps[4 + oc % 2],
                                      [(slotv(2)[:, ic * 1024 + oc * 128:ic * 1024 + oc * 128 + 128],
                                        hb[:, ic * 512:(ic + 1) * 512]) for ic in range(16)],
                                      wait=([("L", l_d), ("V", v_h)] if oc == 0 else [])
                                      + pl.take(bdi))
                            vd = op(pl, "v", lambda e, oc=oc, o=o, b_=ps[4 + oc % 2]:
                                    e.tensor_add(
                                        xs[:, oc * 1024 + o:oc * 1024 + o + 512],
                                        xs[:, oc * 1024 + o:oc * 1024 + o + 512], b_),
                                    wait=[("P", p_d)], inc="V")
                            pl.free_at(bdi, "V", vd)
                        if sub == 1:
                            slot_freeP[0] = slot_freeP[1] = pl.n["P"] - 8
                    slot_freeP[2] = pl.n["P"]
                    pl.free_at("wbc", "V", pl.n["V"])
                st = store_resid(dst, t0, TS, pl.n["V"])
                pl.free_at("hb", "P", pl.n["P"])
                pl.free_at("scr", "V", pl.n["V"])
                pl.free_at("xh", "P", pl.n["P"])
                pl.barrier()

        def bcx(small, like):
            return bass.broadcast_tensor_aps(like, small)[1]

        def attn_l0(src, dst):
            TS = 1024
            for i in range(3):
                slot_state[i] = None
            l_w = 0
            for (wi, (w_, col)) in enumerate([(wqT, 0), (wkT, 8192),
                                              (wvT, 16384), (woT, 24576)]):
                l_w = dma(pl, "g", "L",
                          wsl[:, col:col + 8192].rearrange("p (c t) -> p c t", t=1024),
                          r3(w_, D), wait=[("P", slot_freeP[0])] if wi == 0 else [])
            for (bi, b_) in enumerate([bq, bk, bv, bo]):
                for oc in range(8):
                    dma(pl, "g", "L", bcol[:, bi * 8 + oc:bi * 8 + oc + 1],
                        b_[oc * 128:(oc + 1) * 128].rearrange("(p o) -> p o", o=1))
            l_bias = pl.n["L"]
            for st_i in range(4):
                t0 = st_i * TS
                v_ln = ln_stripe(src, t0, TS)
                steps = []
                for c in range(8):
                    if st_i == 0:
                        steps.append(lambda e, c=c: e.reduce_sum(
                            smf[:, 32 + c:33 + c], xh[:, c * 1024:c * 1024 + TS], AX.X))
                    else:
                        steps.append(lambda e, c=c: e.reduce_sum(
                            smf[:, 40:41], xh[:, c * 1024:c * 1024 + TS], AX.X))
                        steps.append(lambda e, c=c: e.tensor_add(
                            smf[:, 32 + c:33 + c], smf[:, 32 + c:33 + c], smf[:, 40:41]))
                v_xm = vchain(pl, steps, wait=[("V", v_ln)])
                for (pi, (col, dstT)) in enumerate([(0, qTd), (8192, kTd), (16384, vTd)]):
                    y_prev = pl.take("hb_proj")
                    a_last = 0
                    for oc in range(8):
                        for sub in range(2):
                            o = sub * 512
                            bk_ = "ps%d" % ((oc * 2 + sub) % 4)
                            p_p = mmg(pl, ps[(oc * 2 + sub) % 4],
                                      [(wsl[:, col + c * 1024 + oc * 128:col + c * 1024 + oc * 128 + 128],
                                        xh[:, c * 1024 + o:c * 1024 + o + 512]) for c in range(8)],
                                      wait=([("L", l_w), ("V", v_ln)] if oc == 0 and sub == 0 else [])
                                      + pl.take(bk_))
                            a_last = op(pl, "sc", lambda e, oc=oc, o=o, pi=pi,
                                        b_=ps[(oc * 2 + sub) % 4]: e.activation(
                                            hb[:, oc * 1024 + o:oc * 1024 + o + 512], b_,
                                            AF.Identity, bias=bcol[:, pi * 8 + oc:pi * 8 + oc + 1]),
                                        wait=[("P", p_p), ("L", l_bias)] + (y_prev if oc == 0 and sub == 0 else []),
                                        inc="A")
                            pl.free_at(bk_, "A", a_last)
                    st = dma(pl, "sy", "Y",
                             r3(dstT, S)[:, :, t0:t0 + TS],
                             hb.rearrange("p (c t) -> p c t", t=1024)[:, 0:8, 0:TS],
                             wait=[("A", a_last)])
                    pl.free_at("hb_proj", "Y", st)
                pl.free_at("hb", "Y", pl.n["Y"])
                pl.barrier()
            def pol_w(p_i):
                return hb[:, p_i * 4096:(p_i + 1) * 4096] if p_i < 3 else xh[:, 0:4096]

            lw = 0
            for p_i in range(3):
                lw = dma(pl, "g", "L",
                         hb[:, p_i * 4096:(p_i + 1) * 4096].rearrange("p (c t) -> p c t", t=512),
                         r3(pw1T[p_i], 512))
            lw = dma(pl, "g", "L",
                     xh[:, 0:4096].rearrange("p (c t) -> p c t", t=512), r3(sw1T, 512))
            for p_i in range(3):
                lw = dma(pl, "g", "L",
                         pw2_sb[:, p_i * 64:p_i * 64 + 64].rearrange("p (c t) -> p c t", t=16),
                         r3(pw2T[p_i], 16))
            lw = dma(pl, "g", "L",
                     pw2_sb[:, 192:204].rearrange("p (c t) -> p c t", t=3),
                     r3(sw2T, 3))
            a_pb = 0
            pt_b = 0
            for (p_i, brow) in enumerate([pb1[0], pb1[1], pb1[2], sb1]):
                lb = dma(pl, "g", "L", bst[0:4, :],
                         brow.rearrange("(c t) -> c t", t=128),
                         wait=[("P", pt_b)] if pt_b else [])
                pt_b = op(pl, "te", lambda e: e.transpose(
                    pt0[0:128, 0:4], bst[0:4, :], idB[0:4, 0:4]),
                    wait=[("L", lb), ("A", a_pb)], inc="P")
                a_pb = op(pl, "sc", lambda e, p_i=p_i: e.copy(
                    pbcol[:, p_i * 4:(p_i + 1) * 4], pt0[0:128, 0:4]),
                    wait=[("P", pt_b)], inc="A")
            for p_i in range(3):
                dma(pl, "g", "L", pb2c[0:16, p_i:p_i + 1],
                    pb2[p_i].rearrange("(h o) -> h o", o=1))
            dma(pl, "g", "L", pb2c[0:3, 3:4], sb2.rearrange("(h o) -> h o", o=1))
            l_pol = pl.n["L"]
            v_xm2 = vchain(pl, [
                lambda e: e.tensor_scalar(smf[:, 32:40], smf[:, 32:40], 1.0 / S, None, ALU.mult),
                lambda e: e.tensor_copy(smb[:, 8:16], smf[:, 32:40]),
            ], wait=[("V", pl.n["V"])])
            a_hid = 0
            for p_i in range(4):
                pg = 0
                wv = pol_w(p_i)
                for cc in range(4):
                    pg = mmg(pl, ps[4][:, cc:cc + 1],
                             [(wv[:, c * 512 + cc * 128:c * 512 + cc * 128 + 128],
                               smb[:, 8 + c:9 + c]) for c in range(8)],
                             wait=([("L", l_pol), ("V", v_xm2)] + pl.take("ps4"))
                             if cc == 0 else [])
                for cc in range(4):
                    a_hid = op(pl, "sc", lambda e, p_i=p_i, cc=cc: e.activation(
                        smb[:, 16 + p_i * 4 + cc:17 + p_i * 4 + cc], ps[4][:, cc:cc + 1],
                        AF.Gelu, bias=pbcol[:, p_i * 4 + cc:p_i * 4 + cc + 1]),
                        wait=[("P", pg)], inc="A")
                pl.free_at("ps4", "A", a_hid)
            pg2 = 0
            for p_i in range(3):
                pg2 = mmg(pl, ps[5][0:16, p_i:p_i + 1],
                          [(pw2_sb[:, p_i * 64 + cc * 16:p_i * 64 + cc * 16 + 16],
                            smb[:, 16 + p_i * 4 + cc:17 + p_i * 4 + cc]) for cc in range(4)],
                          wait=([("A", a_hid)] + pl.take("ps5")) if p_i == 0 else [])
            pg2 = mmg(pl, ps[5][0:3, 3:4],
                      [(pw2_sb[:, 192 + cc * 3:192 + cc * 3 + 3],
                        smb[:, 28 + cc:29 + cc]) for cc in range(4)])
            v_pol = op(pl, "v", lambda e: e.tensor_add(polv[:, 0:4], ps[5][0:16, 0:4], pb2c),
                       wait=[("P", pg2)], inc="V")
            pl.free_at("ps5", "V", v_pol)
            l_r1 = dma(pl, "g", "L",
                       r_pol[0:1, 0:48].rearrange("o (r p) -> o r p", p=3),
                       polv[0:16, 0:3], wait=[("V", v_pol)])
            l_r2 = dma(pl, "g", "L", r_pol[0:1, 48:51], polv[0:3, 3:4],
                       wait=[("V", v_pol)])
            rows = [r_pol[0:1, 0:48].rearrange("o (r p) -> o p r", p=3)
                    [:, p_i:p_i + 1, :].rearrange("o p r -> o (p r)")
                    for p_i in range(3)] + [r_pol[0:1, 48:51]]
            outsm = [polsm[0:1, p_i * 16:p_i * 16 + 16] for p_i in range(3)] + \
                    [polsm[0:1, 48:51]]
            for i_r, (rv, ov) in enumerate(zip(rows, outsm)):
                n_ = rv.shape[-1]
                v_m = vchain(pl, [
                    lambda e, rv=rv: e.reduce_max(smf[0:1, 64:65], rv, AX.X),
                    lambda e: e.tensor_scalar(smf[0:1, 65:66], smf[0:1, 64:65],
                                              -1.0, None, ALU.mult),
                ], wait=[("L", l_r1), ("L", l_r2)] if i_r == 0 else [("A", pl.n["A"])])
                a_e = op(pl, "sc", lambda e, rv=rv, n_=n_: e.activation(
                    smf[0:1, 66:66 + n_], rv, AF.Exp, bias=smf[0:1, 65:66],
                    accum_out=smf[0:1, 82:83]),
                    wait=[("V", v_m)], inc="A")
                vchain(pl, [
                    lambda e: e.reciprocal(smf[0:1, 83:84], smf[0:1, 82:83]),
                    lambda e, ov=ov, n_=n_: e.tensor_tensor(
                        ov, smf[0:1, 66:66 + n_], bcx(smf[0:1, 83:84], smf[0:1, 66:66 + n_]),
                        ALU.mult),
                ], wait=[("A", a_e)])
            v_hg = vchain(pl, [
                lambda e: e.tensor_tensor(r_hg[:, :], polsm[0:1, 0:16],
                                          bcx(polsm[0:1, 48:49], polsm[0:1, 0:16]), ALU.mult),
                lambda e: e.tensor_tensor(smf[0:1, 64:80], polsm[0:1, 16:32],
                                          bcx(polsm[0:1, 49:50], polsm[0:1, 16:32]), ALU.mult),
                lambda e: e.tensor_add(r_hg[:, :], r_hg[:, :], smf[0:1, 64:80]),
                lambda e: e.tensor_tensor(smf[0:1, 64:80], polsm[0:1, 32:48],
                                          bcx(polsm[0:1, 50:51], polsm[0:1, 32:48]), ALU.mult),
                lambda e: e.tensor_add(r_hg[:, :], r_hg[:, :], smf[0:1, 64:80]),
                lambda e: e.tensor_scalar(r_sc[:, :], r_hg[:, :], 0.125, None, ALU.mult),
                lambda e: e.tensor_copy(r_px[0:1, 0:16], r_sc[:, :]),
            ], wait=[("V", pl.n["V"])])
            p_g8 = mmg(pl, ps[4][:, 0:16], [(oner_f, r_px[0:1, 0:16])],
                       wait=[("V", v_hg)] + pl.take("ps4"))
            v_g8 = op(pl, "v", lambda e: e.tensor_copy(smf[:, 48:64], ps[4][:, 0:16]),
                      wait=[("P", p_g8)], inc="V")
            pl.free_at("ps4", "V", v_g8)
            pl.barrier()
            QH = wsl[0:64, 32768:36864]
            KH = wsl[0:64, 36864:40960]
            QSQ = wsl[64:128, 32768:36864]
            KSQ = wsl[64:128, 36864:40960]
            VH = wsl[0:64, 40960:45056]
            AH = wsl[64:128, 40960:45056]
            VT = wsl[:, 45056:47104]
            for h in range(NH):
                lq = dma(pl, "sy", "Y", QH, qTd[64 * h:64 * h + 64, :],
                         wait=pl.take("slot2"))
                lk = dma(pl, "sy", "Y", KH, kTd[64 * h:64 * h + 64, :])
                lv = dma(pl, "sy", "Y", VH, vTd[64 * h:64 * h + 64, :])
                v_sq2 = vchain(pl, [
                    lambda e: e.tensor_mul(QSQ, QH, QH),
                    lambda e: e.tensor_mul(KSQ, KH, KH),
                ], wait=[("Y", lv)])
                v_vt = 0
                for jb in range(4):
                    pt_v = 0
                    for j2 in range(8):
                        j = jb * 8 + j2
                        pt_v = op(pl, "te", lambda e, j=j, j2=j2: e.transpose(
                            pt1[:, j2 * 64:j2 * 64 + 64],
                            VH[:, j * 128:(j + 1) * 128], idB[0:64, 0:64]),
                            wait=([("Y", lv)] if j == 0 else []) +
                                 (pl.take("pt1") if j2 == 0 else []),
                            inc="P" if j2 == 7 else None)
                    v_vt = op(pl, "v", lambda e, jb=jb: e.tensor_copy(
                        VT[:, jb * 512:(jb + 1) * 512], pt1[:, 0:512]),
                        wait=[("P", pt_v)], inc="V")
                    pl.free_at("pt1", "V", v_vt)
                p_kn = 0
                for j in range(32):
                    p_kn = mmg(pl, ps[4][:, j:j + 1],
                               [(KSQ[:, j * 128:(j + 1) * 128], onec_b[64:128, :])],
                               wait=([("V", v_sq2)] + pl.take("ps4")) if j == 0 else [],
                               inc=(j == 31))
                v_kn = vchain(pl, [
                    lambda e: e.tensor_copy(smf[:, 64:96], ps[4][:, 0:32]),
                    lambda e: e.reduce_max(smf[:, 96:97], smf[:, 64:96], AX.X),
                ], wait=[("P", p_kn)])
                pl.free_at("ps4", "V", v_kn)
                pt_k = op(pl, "te", lambda e: e.transpose(
                    ps[5][0:1, 0:128], smf[:, 96:97], idF),
                    wait=[("V", v_kn)] + pl.take("ps5"), inc="P")
                v_km = vchain(pl, [
                    lambda e: e.tensor_copy(r_px[0:1, 16:144] if False else r_px[0:1, 0:128],
                                            ps[5][0:1, 0:128]),
                    lambda e: e.reduce_max(polsm[0:1, 52:53], r_px[0:1, 0:128], AX.X),
                ], wait=[("P", pt_k)])
                pl.free_at("ps5", "V", v_km)
                a_sk = op(pl, "sc", lambda e: e.activation(
                    polsm[0:1, 53:54], polsm[0:1, 52:53], AF.Sqrt),
                    wait=[("V", v_km)], inc="A")
                v_sk = vchain(pl, [
                    lambda e, h=h: e.tensor_tensor(
                        polsm[0:1, 54:55], polsm[0:1, 53:54],
                        r_px[0:1, h:h + 1] if False else r_sc[0:1, h:h + 1], ALU.mult),
                ], wait=[("A", a_sk)])
                p_skb = mmg(pl, ps[5][:, 0:1], [(oner_f, polsm[0:1, 54:55])],
                            wait=[("V", v_sk)] + pl.take("ps5"))
                v_skb = op(pl, "v", lambda e: e.tensor_copy(smf[:, 97:98], ps[5][:, 0:1]),
                           wait=[("P", p_skb)], inc="V")
                pl.free_at("ps5", "V", v_skb)
                nq_done = 0
                for qi in range(32):
                    nkt = qi // 4 + 1
                    p_qn = mmg(pl, ps[3][:, 0:1],
                               [(QSQ[:, qi * 128:(qi + 1) * 128], onec_b[64:128, :])],
                               wait=pl.take("ps3") + ([("V", v_sq2)] if qi == 0 else []))
                    a_qn = op(pl, "sc", lambda e: e.activation(
                        smf[:, 98:99], ps[3][:, 0:1], AF.Sqrt),
                        wait=[("P", p_qn)] + pl.take("sqn"), inc="A")
                    pl.free_at("ps3", "A", a_qn)
                    v_nb = op(pl, "v", lambda e: e.tensor_scalar(
                        smf[:, 99:100], smf[:, 98:99], smf[:, 97:98], -1.0,
                        ALU.mult, ALU.mult),
                        wait=[("A", a_qn), ("V", v_skb)], inc="V")
                    pl.free_at("sqn", "V", v_nb)
                    for kt in range(nkt):
                        bsi = "ps%d" % (kt % 2)
                        p_s = mmg(pl, ps[kt % 2],
                                  [(QH[:, qi * 128:(qi + 1) * 128],
                                    KH[:, kt * 512:(kt + 1) * 512])],
                                  wait=([("Y", lk)] if qi == 0 and kt == 0 else [])
                                  + pl.take(bsi))
                        pb_ = "prb%d" % (kt % 2)
                        a_ex = op(pl, "sc", lambda e, kt=kt, b_=ps[kt % 2]: e.activation(
                            scr[:, (kt % 2) * 512:(kt % 2) * 512 + 512], b_, AF.Exp,
                            bias=smf[:, 99:100], scale=smf[:, 48 + h:49 + h],
                            accum_out=smf[:, 104 + kt:105 + kt]),
                            wait=[("P", p_s), ("V", v_nb)] + pl.take(pb_), inc="A")
                        pl.free_at(bsi, "A", a_ex)
                        last_probs = a_ex
                        if kt == nkt - 1:
                            v_msk = op(pl, "v", lambda e, kt=kt, qi=qi: e.scalar_tensor_tensor(
                                scr[:, (kt % 2) * 512:(kt % 2) * 512 + 512],
                                scr[:, (kt % 2) * 512:(kt % 2) * 512 + 512], 1.0,
                                cmask[:, (qi % 4) * 512:(qi % 4) * 512 + 512],
                                ALU.mult, ALU.mult,
                                accum_out=smf[:, 104 + kt:105 + kt]),
                                wait=[("A", a_ex)], inc="V")
                            last_probs = ("V", v_msk)
                        else:
                            last_probs = ("A", a_ex)
                        pt_p = 0
                        for j in range(4):
                            pt_p = op(pl, "te", lambda e, kt=kt, j=j: e.transpose(
                                pt0[:, (kt % 2) * 512 + j * 128:(kt % 2) * 512 + j * 128 + 128],
                                scr[:, (kt % 2) * 512 + j * 128:(kt % 2) * 512 + j * 128 + 128],
                                idB),
                                wait=([last_probs] + pl.take("pt0%d" % (kt % 2)))
                                if j == 0 else [],
                                inc="P" if j == 3 else None)
                        v_pt = op(pl, "v", lambda e, kt=kt: e.tensor_copy(
                            scr[:, 1024 + (kt % 2) * 512:1024 + (kt % 2) * 512 + 512],
                            pt0[:, (kt % 2) * 512:(kt % 2) * 512 + 512]),
                            wait=[("P", pt_p)] + pl.take("prT%d" % (kt % 2)), inc="V")
                        pl.free_at("pt0%d" % (kt % 2), "V", v_pt)
                        pl.free_at(pb_, "V", v_pt)
                        p_pv = mmg(pl, ps[2][:, 0:64],
                                   [(scr[:, 1024 + (kt % 2) * 512 + j * 128:
                                         1024 + (kt % 2) * 512 + j * 128 + 128],
                                     VT[:, (4 * kt + j) * 64:(4 * kt + j) * 64 + 64])
                                    for j in range(4)],
                                   wait=[("V", v_pt)] +
                                        (pl.take("ps2") if kt == 0 else []),
                                   start=(kt == 0), stop=(kt == nkt - 1))
                        pl.free_at("prT%d" % (kt % 2), "P", p_pv)
                    v_rs = vchain(pl, [
                        lambda e, nkt=nkt: e.reduce_sum(smf[:, 112:113],
                                                        smf[:, 104:104 + nkt], AX.X),
                        lambda e: e.reciprocal(smf[:, 113:114], smf[:, 112:113]),
                        lambda e: e.tensor_tensor(
                            smb[:, 0:64], ps[2][:, 0:64],
                            bcx(smf[:, 113:114], ps[2][:, 0:64]), ALU.mult),
                    ], wait=[("P", pl.n["P"])])
                    pl.free_at("ps2", "V", v_rs)
                    pt_o = op(pl, "te", lambda e, qi=qi: e.transpose(
                        pt1[0:64, (qi % 8) * 128:(qi % 8) * 128 + 128],
                        smb[:, 0:64], idB),
                        wait=[("V", v_rs)] + (pl.take("pt1") if qi % 8 == 0 else []),
                        inc="P")
                    if qi % 8 == 7:
                        v_oe = op(pl, "v", lambda e, qi=qi: e.tensor_copy(
                            AH[:, (qi - 7) * 128:(qi + 1) * 128], pt1[0:64, 0:1024]),
                            wait=[("P", pt_o)], inc="V")
                        pl.free_at("pt1", "V", v_oe)
                st_a = dma(pl, "sy", "Y", aTd[64 * h:64 * h + 64, :], AH,
                           wait=[("V", pl.n["V"])])
                pl.free_at("slot2", "Y", st_a)
            pl.barrier()
            for st_i in range(4):
                t0 = st_i * TS
                lx = dma(pl, "sy", "Y",
                         xs.rearrange("p (c t) -> p c t", t=1024)[:, :, 0:TS],
                         r3(src, S)[:, :, t0:t0 + TS], wait=pl.take("xs"))
                la = dma(pl, "sy", "Y",
                         hb.rearrange("p (c t) -> p c t", t=1024)[:, 0:8, 0:TS],
                         r3(aTd, S)[:, :, t0:t0 + TS], wait=pl.take("hb"))
                v_last = 0
                for oc in range(8):
                    for sub in range(2):
                        o = sub * 512
                        bk_ = "ps%d" % ((oc * 2 + sub) % 4)
                        p_p = mmg(pl, ps[(oc * 2 + sub) % 4],
                                  [(wsl[:, 16384 + 8192 + c * 1024 + oc * 128:
                                        16384 + 8192 + c * 1024 + oc * 128 + 128],
                                    hb[:, c * 1024 + o:c * 1024 + o + 512]) for c in range(8)],
                                  wait=([("Y", la), ("Y", lx)] if oc == 0 and sub == 0 else [])
                                  + pl.take(bk_))
                        v_last = op(pl, "v", lambda e, oc=oc, o=o,
                                    b_=ps[(oc * 2 + sub) % 4]: e.scalar_tensor_tensor(
                                        xs[:, oc * 1024 + o:oc * 1024 + o + 512], b_,
                                        bcol[:, 24 + oc:25 + oc],
                                        xs[:, oc * 1024 + o:oc * 1024 + o + 512],
                                        ALU.add, ALU.add),
                                    wait=[("P", p_p)], inc="V")
                        pl.free_at(bk_, "V", v_last)
                st = store_resid(dst, t0, TS, v_last)
                pl.free_at("hb", "Y", st)
                pl.barrier()
            for i in range(3):
                slot_state[i] = None
                slot_freeP[i] = pl.n["P"]

        def ffn_l1(src, dst, load_gw):
            TS = 512
            if load_gw:
                dma(pl, "g", "L", rw_sb.rearrange("p (c e) -> p c e", e=8), r3(Wg, 8))
                dma(pl, "g", "L", bg_sb, bg.rearrange("(p o) -> p o", o=1))
                dma(pl, "g", "L", b2_sb, b2)
                lb = dma(pl, "g", "L", bst[0:8, :], bf.rearrange("(c t) -> c t", t=128),
                         wait=pl.take("bst"))
                pt_b = op(pl, "te", lambda e: e.transpose(
                    pt0[0:128, 0:8], bst[0:8, :], idB[0:8, 0:8]),
                    wait=[("L", lb)] + pl.take("pt0f"), inc="P")
                v_bf = op(pl, "v", lambda e: e.tensor_scalar(
                    bfcol[:, 0:8], pt0[0:128, 0:8], 0.5, None, ALU.mult),
                    wait=[("P", pt_b)], inc="V")
                pl.free_at("bst", "P", pt_b)
                pl.free_at("pt0f", "V", v_bf)
            for st_i in range(8):
                t0 = st_i * TS
                v_ln = ln_stripe(src, t0, TS)
                v_mx = op(pl, "v", lambda e: e.memset(
                    xs.rearrange("p (c t) -> p c t", t=1024)[:, :, 512:1024], 0.0),
                    wait=[("V", v_ln)], inc="V")
                pr = mmg(pl, ps[4][0:8, :],
                         [(rw_sb[:, c * 8:(c + 1) * 8],
                           xh[:, c * 1024:c * 1024 + 512]) for c in range(8)],
                         wait=[("V", v_ln), ("L", pl.n["L"])] + pl.take("ps4"))
                v_lg = op(pl, "v", lambda e: e.tensor_scalar(
                    scf[0:8, 0:512], ps[4][0:8, :], bg_sb[:, 0:1], None, ALU.add),
                    wait=[("P", pr)] + pl.take("scf_pr"), inc="V")
                pl.free_at("ps4", "V", v_lg)
                ptw_prev = 0
                for j in range(4):
                    half = (j % 2) * 128
                    ptr = op(pl, "te", lambda e, j=j: e.transpose(
                        ps[5][:, (j % 2) * 8:(j % 2) * 8 + 8],
                        scf[0:8, j * 128:(j + 1) * 128], idF[0:8, 0:8]),
                        wait=[("V", v_lg)] +
                             (pl.take("ps5") if j < 2 else [("V", pl.n["V"])]),
                        inc="P")
                    pv = smf[:, 8:16]
                    v_m = vchain(pl, [
                        lambda e, j=j: e.tensor_copy(
                            pv, ps[5][:, (j % 2) * 8:(j % 2) * 8 + 8]),
                        lambda e: e.reduce_max(smf[:, 0:1], pv, AX.X),
                        lambda e: e.tensor_scalar(smf[:, 1:2], smf[:, 0:1],
                                                  -1.0, None, ALU.mult),
                    ], wait=[("P", ptr)] + ([("P", ptw_prev)] if ptw_prev else []))
                    a_e = op(pl, "sc", lambda e: e.activation(
                        smf[:, 16:24], pv, AF.Exp, bias=smf[:, 1:2],
                        accum_out=smf[:, 2:3]),
                        wait=[("V", v_m)], inc="A")
                    v_w = vchain(pl, [
                        lambda e: e.reciprocal(smf[:, 3:4], smf[:, 2:3]),
                        lambda e: e.tensor_tensor(
                            pv, smf[:, 16:24], bcx(smf[:, 3:4], smf[:, 16:24]), ALU.mult),
                        lambda e: e.reduce_max(smf[:, 0:1], pv, AX.X),
                        lambda e: e.tensor_tensor(
                            smf[:, 16:24], pv, bcx(smf[:, 0:1], pv), ALU.is_ge),
                        lambda e: e.scalar_tensor_tensor(
                            smf[:, 24:32], smf[:, 16:24], -1e30, pv, ALU.mult, ALU.add),
                        lambda e: e.reduce_max(smf[:, 1:2], smf[:, 24:32], AX.X),
                        lambda e: e.tensor_tensor(
                            smf[:, 16:24], pv, bcx(smf[:, 1:2], pv), ALU.is_ge),
                        lambda e: e.tensor_mul(smf[:, 24:32], pv, smf[:, 16:24]),
                        lambda e: e.tensor_scalar(
                            smb[:, 0:8], smf[:, 24:32], 0.5, None, ALU.mult),
                    ], wait=[("A", a_e)])
                    ptw = op(pl, "te", lambda e, half=half: e.transpose(
                        pt0[0:8, half:half + 128], smb[:, 0:8], idB),
                        wait=[("V", v_w)] + pl.take("pt0h%d" % (j % 2)), inc="P")
                    a_w = op(pl, "sc", lambda e, j=j, half=half: e.copy(
                        wT_sb[0:8, j * 128:(j + 1) * 128], pt0[0:8, half:half + 128]),
                        wait=[("P", ptw)], inc="A")
                    pl.free_at("pt0h%d" % (j % 2), "A", a_w)
                    ptw_prev = ptw
                pl.free_at("scf_pr", "P", ptw_prev)
                pl.free_at("ps5", "V", pl.n["V"])
                a_wlast = pl.n["A"]
                order = list(range(8)) if st_i % 2 == 0 else list(range(7, -1, -1))
                if DBG_EXPERTS is not None:
                    order = [e_ for e_ in order if e_ in DBG_EXPERTS]
                for e_i in order:
                    l_wr = dma(pl, "g", "L", w_row1[0:1, 0:512], wT_sb[e_i:e_i + 1, 0:512],
                               wait=[("A", a_wlast)] + pl.take("w_row1"))
                    l_g = load_slot(0, ("W1", e_i), c_W1[e_i], D, FF)
                    l_u = load_slot(1, ("W3", e_i), c_W3[e_i], D, FF)
                    l_d = load_slot(2, ("W2", e_i), c_W2[e_i], FF, D)
                    a_bc = 0
                    for (bi, (bsrc, bdst)) in enumerate([(b1[e_i], b1col), (b3[e_i], b3col)]):
                        lb = dma(pl, "g", "L", bst[:, :],
                                 bsrc.rearrange("(c t) -> c t", t=128),
                                 wait=pl.take("bst"))
                        pt_b = op(pl, "te", lambda e: e.transpose(
                            pt0[0:128, 0:16], bst[:, :], idB[0:16, 0:16]),
                            wait=[("L", lb)] + pl.take("pt0f") + pl.take("pt0h0")
                                 + pl.take("pt0h1"), inc="P")
                        pl.free_at("bst", "P", pt_b)
                        a_bc = op(pl, "sc", lambda e, bdst=bdst: e.copy(
                            bdst[:, 0:16], pt0[0:128, 0:16]),
                            wait=[("P", pt_b)], inc="A")
                        pl.free_at("pt0f", "A", a_bc)
                    mmg(pl, ps[4][:, :], [(oner_b128, w_row1[0:1, 0:512])],
                        wait=[("L", l_wr)] + pl.take("ps4"), inc=False)
                    pw = mmg(pl, ps[5][:, :], [(oner_b128, w_row1[0:1, 0:512])],
                             wait=pl.take("ps5"))
                    pl.free_at("w_row1", "P", pw)
                    a_wbc = op(pl, "sc", lambda e: e.copy(scr[:, 0:512], ps[4][:, :]),
                               wait=[("P", pw)] + pl.take("wbc"), inc="A")
                    pl.free_at("ps4", "A", a_wbc)
                    pl.free_at("ps5", "A", a_wbc)
                    for hc in range(16):
                        bgi = "ps%d" % (hc % 2)
                        bui = "ps%d" % (2 + hc % 2)
                        sgr = hb[:, 10240 + (hc % 2) * 512:10240 + (hc % 2) * 512 + 512]
                        usr = hb[:, 8192 + (hc % 2) * 512:8192 + (hc % 2) * 512 + 512]
                        p_g = mmg(pl, ps[hc % 2],
                                  [(slotv(0)[:, c * 2048 + hc * 128:c * 2048 + hc * 128 + 128],
                                    xh[:, c * 1024:c * 1024 + 512]) for c in range(8)],
                                  wait=([("L", l_g), ("V", v_ln)] if hc == 0 else [])
                                  + pl.take(bgi))
                        p_u = mmg(pl, ps[2 + hc % 2],
                                  [(slotv(1)[:, c * 2048 + hc * 128:c * 2048 + hc * 128 + 128],
                                    xh[:, c * 1024:c * 1024 + 512]) for c in range(8)],
                                  wait=([("L", l_u)] if hc == 0 else [])
                                  + pl.take(bui))
                        ag = op(pl, "sc", lambda e, hc=hc, b_=ps[hc % 2], sgr=sgr:
                                e.activation(sgr, b_, AF.Identity,
                                             bias=b1col[:, hc:hc + 1]),
                                wait=[("P", p_g), ("A", a_bc)] + pl.take("sgr%d" % (hc % 2)),
                                inc="A")
                        pl.free_at(bgi, "A", ag)
                        au = op(pl, "v", lambda e, hc=hc, b_=ps[2 + hc % 2], sgr=sgr, usr=usr:
                                e.scalar_tensor_tensor(
                                    usr, b_, b3col[:, hc:hc + 1], sgr, ALU.add, ALU.mult),
                                wait=[("P", p_u), ("A", ag)], inc="V")
                        pl.free_at(bui, "V", au)
                        ag2 = op(pl, "sc", lambda e, sgr=sgr, usr=usr: e.activation(
                            sgr, usr, AF.Silu),
                            wait=[("V", au)], inc="A")
                        vh_ = op(pl, "v", lambda e, hc=hc, sgr=sgr: e.tensor_mul(
                            hb[:, hc * 512:(hc + 1) * 512], sgr, scr[:, 0:512]),
                            wait=[("A", ag2)] + ([("A", a_wbc)] if hc == 0 else []),
                            inc="V")
                        pl.free_at("sgr%d" % (hc % 2), "V", vh_)
                    v_h = pl.n["V"]
                    for oc in range(8):
                        bdi = "ps%d" % (4 + oc % 2)
                        p_d = mmg(pl, ps[4 + oc % 2],
                                  [(slotv(2)[:, ic * 1024 + oc * 128:ic * 1024 + oc * 128 + 128],
                                    hb[:, ic * 512:(ic + 1) * 512]) for ic in range(16)],
                                  wait=([("L", l_d), ("V", v_h), ("V", v_mx)] if oc == 0 else [])
                                  + pl.take(bdi))
                        vd = op(pl, "v", lambda e, oc=oc, b_=ps[4 + oc % 2]: e.tensor_add(
                            xs[:, oc * 1024 + 512:oc * 1024 + 1024],
                            xs[:, oc * 1024 + 512:oc * 1024 + 1024], b_),
                            wait=[("P", p_d)], inc="V")
                        pl.free_at(bdi, "V", vd)
                    slot_freeP[0] = slot_freeP[1] = pl.n["P"] - 8
                    slot_freeP[2] = pl.n["P"]
                    pl.free_at("wbc", "V", pl.n["V"])
                l_f = load_slot(2, "Wf", c_Wf, D, D)
                v_mix = pl.n["V"]
                a_mb = 0
                for oc in range(8):
                    p_b2 = mmg(pl, ps[4 + oc % 2],
                               [(b2_sb[:, oc * 128:(oc + 1) * 128], wT_sb[:, 0:512])],
                               wait=([("A", pl.n["A"]), ("V", v_mix)] if oc == 0 else [])
                               + pl.take("ps%d" % (4 + oc % 2)))
                    a_mb = op(pl, "v", lambda e, oc=oc, b_=ps[4 + oc % 2]: e.tensor_add(
                        hb[:, oc * 512:(oc + 1) * 512],
                        xs[:, oc * 1024 + 512:oc * 1024 + 1024], b_),
                        wait=[("P", p_b2)], inc="V")
                    pl.free_at("ps%d" % (4 + oc % 2), "V", a_mb)
                v_mb = pl.n["V"]
                for oc in range(8):
                    bk_ = "ps%d" % (oc % 2)
                    p_f = mmg(pl, ps[oc % 2],
                              [(slotv(2)[:, ic * 1024 + oc * 128:ic * 1024 + oc * 128 + 128],
                                hb[:, ic * 512:(ic + 1) * 512]) for ic in range(8)],
                              wait=([("L", l_f), ("V", v_mb)] if oc == 0 else [])
                              + pl.take(bk_))
                    v_f = op(pl, "v", lambda e, oc=oc, b_=ps[oc % 2]: e.scalar_tensor_tensor(
                        xs[:, oc * 1024:oc * 1024 + 512], b_, bfcol[:, oc:oc + 1],
                        xs[:, oc * 1024:oc * 1024 + 512], ALU.add, ALU.add),
                        wait=[("P", p_f)], inc="V")
                    pl.free_at(bk_, "V", v_f)
                slot_freeP[2] = pl.n["P"]
                st = store_resid(dst, t0, TS, pl.n["V"])
                pl.free_at("hb", "P", pl.n["P"])
                pl.free_at("scr", "V", pl.n["V"])
                pl.free_at("xh", "P", pl.n["P"])
                pl.barrier()

        def attn_l1(src, dst):
            TS = 1024
            for i in range(3):
                slot_state[i] = None
            l_w = 0
            for (wi, (w_, col)) in enumerate([(w2qT, 0), (w2kT, 8192),
                                              (w2vT, 16384), (w2oT, 24576)]):
                l_w = dma(pl, "g", "L",
                          wsl[:, col:col + 8192].rearrange("p (c t) -> p c t", t=1024),
                          r3(w_, D), wait=[("P", slot_freeP[0])] if wi == 0 else [])
            v_z = op(pl, "v", lambda e: e.memset(bcol[:, 32:40], 0.0), inc="V")
            for st_i in range(4):
                t0 = st_i * TS
                v_ln = ln_stripe(src, t0, TS)
                for (pi, (col, dstT)) in enumerate([(0, qTd), (8192, kTd), (16384, vTd)]):
                    y_prev = pl.take("hb_proj")
                    a_last = 0
                    v_lastp = 0
                    for oc in range(8):
                        for sub in range(2):
                            o = sub * 512
                            bk_ = "ps%d" % ((oc * 2 + sub) % 4)
                            p_p = mmg(pl, ps[(oc * 2 + sub) % 4],
                                      [(wsl[:, col + c * 1024 + oc * 128:col + c * 1024 + oc * 128 + 128],
                                        xh[:, c * 1024 + o:c * 1024 + o + 512]) for c in range(8)],
                                      wait=([("L", l_w), ("V", v_ln)] if oc == 0 and sub == 0 else [])
                                      + pl.take(bk_))
                            if pi < 2:
                                v_lastp = op(pl, "v", lambda e, oc=oc, o=o,
                                             b_=ps[(oc * 2 + sub) % 4]: e.tensor_scalar(
                                                 hb[:, oc * 1024 + o:oc * 1024 + o + 512], b_,
                                                 0.0, 1e-6, ALU.max, ALU.add),
                                             wait=[("P", p_p)] + (y_prev if oc == 0 and sub == 0 else []),
                                             inc="V")
                                pl.free_at(bk_, "V", v_lastp)
                                a_last = None
                            else:
                                a_last = op(pl, "sc", lambda e, oc=oc, o=o,
                                            b_=ps[(oc * 2 + sub) % 4]: e.copy(
                                                hb[:, oc * 1024 + o:oc * 1024 + o + 512], b_),
                                            wait=[("P", p_p)] + (y_prev if oc == 0 and sub == 0 else []),
                                            inc="A")
                                pl.free_at(bk_, "A", a_last)
                    wsem = [("A", a_last)] if a_last else [("V", v_lastp)]
                    st = dma(pl, "sy", "Y",
                             r3(dstT, S)[:, :, t0:t0 + TS],
                             hb.rearrange("p (c t) -> p c t", t=1024)[:, 0:8, 0:TS],
                             wait=wsem)
                    pl.free_at("hb_proj", "Y", st)
                pl.free_at("hb", "Y", pl.n["Y"])
                pl.barrier()
            QH = wsl[0:64, 32768:36864]
            KH = wsl[0:64, 36864:40960]
            VH = wsl[0:64, 40960:45056]
            AH = wsl[64:128, 40960:45056]
            for h in range(NH):
                lq = dma(pl, "sy", "Y", QH, qTd[64 * h:64 * h + 64, :],
                         wait=pl.take("slot2"))
                lk = dma(pl, "sy", "Y", KH, kTd[64 * h:64 * h + 64, :])
                lv = dma(pl, "sy", "Y", VH, vTd[64 * h:64 * h + 64, :])
                v_st0 = vchain(pl, [
                    lambda e: e.memset(kv_sb, 0.0),
                    lambda e: e.memset(ks_sb, 0.0),
                    lambda e: e.memset(kt_sb, 0.0),
                    lambda e: e.memset(vt_sb, 0.0),
                    lambda e: e.memset(smb[:, 0:64], 0.0),
                ], wait=[])
                kvb = scr[:, 512:576]
                ksb = scr[:, 576:577]
                v_kvb = vchain(pl, [
                    lambda e: e.memset(scr[0:64, 512:577], 0.0),
                ], wait=[("V", v_st0)])
                for ci in range(32):
                    T0 = ci * 128
                    p_a = mmg(pl, ps[0][:, 0:128],
                              [(QH[:, T0:T0 + 128], KH[:, T0:T0 + 128])],
                              wait=([("Y", lv), ("V", v_kvb)] if ci == 0 else [])
                              + pl.take("ps0"))
                    v_a = op(pl, "v", lambda e: e.scalar_tensor_tensor(
                        at_sb, ps[0][:, 0:128], 1.0, cmask[:, 2048:2176],
                        ALU.mult, ALU.mult, accum_out=smf[:, 104:105]),
                        wait=[("P", p_a)] + pl.take("at"), inc="V")
                    pl.free_at("ps0", "V", v_a)
                    pt_kv = 0
                    for (ti, SRC) in enumerate([KH, VH]):
                        pt_kv = op(pl, "te", lambda e, ti=ti, SRC=SRC, T0=T0: e.transpose(
                            pt1[:, ti * 64:ti * 64 + 64], SRC[:, T0:T0 + 128],
                            idB[0:64, 0:64]),
                            wait=(pl.take("pt1") if ti == 0 else []), inc="P" if ti else None)
                    v_kt = vchain(pl, [
                        lambda e: e.tensor_copy(kt_sb, pt1[:, 0:64]),
                        lambda e: e.tensor_copy(vt_sb, pt1[:, 64:128]),
                    ], wait=[("P", pt_kv)] + pl.take("ktvt"))
                    pl.free_at("pt1", "V", v_kt)
                    pt_at = op(pl, "te", lambda e: e.transpose(
                        pt0[:, 0:128], at_sb, idB),
                        wait=[("V", v_a)] + pl.take("pt0at"), inc="P")
                    v_at = op(pl, "v", lambda e: e.tensor_copy(scr[:, 0:128], pt0[:, 0:128]),
                              wait=[("P", pt_at)] + pl.take("at2"), inc="V")
                    pl.free_at("pt0at", "V", v_at)
                    pl.free_at("at", "P", pt_at)
                    mmg(pl, ps[2][:, 0:64], [(scr[:, 0:128], vt_sb)],
                        wait=[("V", v_at), ("V", v_kt)] + pl.take("ps2"),
                        start=True, stop=False, inc=False)
                    p_o = mmg(pl, ps[2][:, 0:64],
                              [(QH[:, T0:T0 + 128], kvb[0:64, :])],
                              start=False, stop=True)
                    p_z = mmg(pl, ps[3][:, 0:1],
                              [(QH[:, T0:T0 + 128], ksb[0:64, :])],
                              wait=pl.take("ps3"))
                    v_o = vchain(pl, [
                        lambda e: e.tensor_add(smf[:, 105:106], ps[3][:, 0:1],
                                               smf[:, 104:105]),
                        lambda e: e.tensor_scalar(smf[:, 105:106], smf[:, 105:106],
                                                  1e-6, None, ALU.add),
                        lambda e: e.reciprocal(smf[:, 106:107], smf[:, 105:106]),
                        lambda e: e.tensor_tensor(
                            smb[:, 0:64], ps[2][:, 0:64],
                            bcx(smf[:, 106:107], ps[2][:, 0:64]), ALU.mult),
                    ], wait=[("P", p_o), ("P", p_z)])
                    pl.free_at("ps3", "V", v_o)
                    pl.free_at("ps2", "V", v_o)
                    p_kv = mmg(pl, ps[4][0:64, 0:64], [(kt_sb, vt_sb)],
                               wait=pl.take("ps4s"))
                    p_ks = mmg(pl, ps[5][0:64, 0:1], [(kt_sb, onec_b)],
                               wait=pl.take("ps5s"))
                    v_kv = vchain(pl, [
                        lambda e: e.tensor_add(kv_sb, kv_sb, ps[4][0:64, 0:64]),
                        lambda e: e.tensor_add(ks_sb, ks_sb, ps[5][0:64, 0:1]),
                        lambda e: e.tensor_copy(kvb[0:64, :], kv_sb),
                        lambda e: e.tensor_copy(ksb[0:64, :], ks_sb),
                    ], wait=[("P", p_kv), ("P", p_ks)])
                    pl.free_at("ps4s", "V", v_kv)
                    pl.free_at("ps5s", "V", v_kv)
                    pl.free_at("ktvt", "P", max(p_kv, p_ks))
                    pl.free_at("at2", "P", p_o)
                    pt_o = op(pl, "te", lambda e, ci=ci: e.transpose(
                        pt0[0:64, 128 + (ci % 4) * 128:128 + (ci % 4) * 128 + 128],
                        smb[:, 0:64], idB),
                        wait=[("V", v_o)] + (pl.take("pt0o") if ci % 4 == 0 else []),
                        inc="P")
                    if ci % 4 == 3:
                        v_oe = op(pl, "v", lambda e, ci=ci: e.tensor_copy(
                            AH[:, (ci - 3) * 128:(ci + 1) * 128], pt0[0:64, 128:640]),
                            wait=[("P", pt_o)], inc="V")
                        pl.free_at("pt0o", "V", v_oe)
                st_a = dma(pl, "sy", "Y", aTd[64 * h:64 * h + 64, :], AH,
                           wait=[("V", pl.n["V"])])
                pl.free_at("slot2", "Y", st_a)
            pl.barrier()
            for st_i in range(4):
                t0 = st_i * TS
                lx = dma(pl, "sy", "Y",
                         xs.rearrange("p (c t) -> p c t", t=1024)[:, :, 0:TS],
                         r3(src, S)[:, :, t0:t0 + TS], wait=pl.take("xs"))
                la = dma(pl, "sy", "Y",
                         hb.rearrange("p (c t) -> p c t", t=1024)[:, 0:8, 0:TS],
                         r3(aTd, S)[:, :, t0:t0 + TS], wait=pl.take("hb"))
                v_last = 0
                for oc in range(8):
                    for sub in range(2):
                        o = sub * 512
                        bk_ = "ps%d" % ((oc * 2 + sub) % 4)
                        p_p = mmg(pl, ps[(oc * 2 + sub) % 4],
                                  [(wsl[:, 24576 + c * 1024 + oc * 128:
                                        24576 + c * 1024 + oc * 128 + 128],
                                    hb[:, c * 1024 + o:c * 1024 + o + 512]) for c in range(8)],
                                  wait=([("Y", la), ("Y", lx), ("V", v_z)]
                                        if oc == 0 and sub == 0 else [])
                                  + pl.take(bk_))
                        v_last = op(pl, "v", lambda e, oc=oc, o=o,
                                    b_=ps[(oc * 2 + sub) % 4]: e.tensor_add(
                                        xs[:, oc * 1024 + o:oc * 1024 + o + 512],
                                        xs[:, oc * 1024 + o:oc * 1024 + o + 512], b_),
                                    wait=[("P", p_p)], inc="V")
                        pl.free_at(bk_, "V", v_last)
                st = store_resid(dst, t0, TS, v_last)
                pl.free_at("hb", "Y", st)
                pl.barrier()
            for i in range(3):
                slot_state[i] = None
                slot_freeP[i] = pl.n["P"]

        if "A" in phases:
            ffn_l0(xT, resid, load_router=True)
        if "B" in phases:
            attn_l0(resid if "A" in phases else xT, resid)
        if "C" in phases:
            ffn_l0(resid if ("A" in phases or "B" in phases) else xT, resid,
                   load_router="A" not in phases)
        ran = any(p in phases for p in "ABC")
        if "D" in phases:
            ffn_l1(resid if ran else xT, resid, load_gw=True)
            ran = True
        if "E" in phases:
            attn_l1(resid if ran else xT, resid)
            ran = True
        if "F" in phases:
            ffn_l1(resid if ran else xT, resid, load_gw="D" not in phases)

        ran_any = any(p in phases for p in "ABCDEF")
        src_final = resid if ran_any else xT
        done = dma(pl, "sy", "Y", y[:, :], src_final[:, :])

        @pl.on("sy")
        def _(eng, SM, done=done):
            eng.wait_ge(SM["Y"], done)

        with nc.Block() as block:
            @block.tensor
            def _(eng):
                for fn in pl.q["te"]:
                    fn(eng, SM)

            @block.scalar
            def _(eng):
                for fn in pl.q["sc"]:
                    fn(eng, SM)

            @block.vector
            def _(eng):
                for fn in pl.q["v"]:
                    fn(eng, SM)

            @block.gpsimd
            def _(eng):
                for fn in pl.q["g"]:
                    fn(eng, SM)

            @block.sync
            def _(eng):
                for fn in pl.q["sy"]:
                    fn(eng, SM)

    return nc


def _host_inputs(inputs):
    g = {k: np.asarray(v) for k, v in inputs.items()}
    T = lambda a: np.ascontiguousarray(np.asarray(a, np.float32).T)
    C = lambda a: np.ascontiguousarray(np.asarray(a, np.float32))
    common = {
        "wqT": T(g["L0_attn_wq"]), "wkT": T(g["L0_attn_wk"]),
        "wvT": T(g["L0_attn_wv"]), "woT": T(g["L0_attn_wo"]),
        "bq": C(g["L0_attn_bq"]), "bk": C(g["L0_attn_bk"]),
        "bv": C(g["L0_attn_bv"]), "bo": C(g["L0_attn_bo"]),
        "pw1T": np.ascontiguousarray(np.asarray(g["L0_attn_pw1"], np.float32).transpose(0, 2, 1)),
        "pb1": C(g["L0_attn_pb1"]),
        "pw2T": np.ascontiguousarray(np.asarray(g["L0_attn_pw2"], np.float32).transpose(0, 2, 1)),
        "pb2": C(g["L0_attn_pb2"]),
        "sw1T": T(g["L0_attn_sw1"]), "sb1": C(g["L0_attn_sb1"]),
        "sw2T": T(g["L0_attn_sw2"]), "sb2": C(g["L0_attn_sb2"]),
        "sgT": T(g["L0_ffn_shared_gate"]), "suT": T(g["L0_ffn_shared_up"]),
        "sdT": T(g["L0_ffn_shared_down"]),
        "gw": C(g["L0_ffn_gate_w"]), "uw": C(g["L0_ffn_up_w"]),
        "dw": C(g["L0_ffn_down_w"]),
        "rwT": T(g["L0_ffn_router_w"]), "rb": C(g["L0_ffn_routing_bias"]),
        "w2qT": T(g["L1_attn_Wq"]), "w2kT": T(g["L1_attn_Wk"]),
        "w2vT": T(g["L1_attn_Wv"]), "w2oT": T(g["L1_attn_Wo"]),
        "W1": C(g["L1_ffn_W1"]), "W3": C(g["L1_ffn_W3"]), "W2": C(g["L1_ffn_W2"]),
        "b1": C(g["L1_ffn_b1"]), "b3": C(g["L1_ffn_b3"]), "b2": C(g["L1_ffn_b2"]),
        "Wg": C(g["L1_ffn_Wg"]), "bg": C(g["L1_ffn_bg"]),
        "Wf": C(g["L1_ffn_Wf"]), "bf": C(g["L1_ffn_bf"]),
        "identF": np.eye(128, dtype=np.float32),
        "identB": np.eye(128, dtype=ml_dtypes.bfloat16),
        "mask128": np.tril(np.ones((128, 128))).astype(ml_dtypes.bfloat16),
    }
    jj = np.arange(512)[None, :]
    pp = np.arange(128)[:, None]
    mL0 = np.stack([(jj <= 128 * v_ + pp) for v_ in range(4)]).astype(ml_dtypes.bfloat16)
    common["maskL0"] = mL0
    x = np.asarray(g["L0_attn_x"], np.float32)
    maps = []
    for c in range(8):
        m = dict(common)
        m["xT"] = np.ascontiguousarray(x[c].T)
        maps.append(m)
    return maps


def kernel(**inputs):
    nc = build_program()
    maps = _host_inputs(inputs)
    decl = set(DECLARED)
    maps = [{k: v for k, v in m.items() if k in decl} for m in maps]
    res = run_bass_kernel_spmd(nc, maps, core_ids=list(range(8)))
    out = np.stack([np.asarray(res.results[c]["y"], np.float32).T for c in range(8)])
    return np.ascontiguousarray(out)


# version 02
# speedup vs baseline: 1.1034x; 1.1034x over previous
import numpy as np
import ml_dtypes
import concourse.bass as bass
import concourse.mybir as mybir
from concourse.bass_utils import run_bass_kernel_spmd
from contextlib import ExitStack

F32 = mybir.dt.float32
BF16 = mybir.dt.bfloat16
AF = mybir.ActivationFunctionType
ALU = mybir.AluOpType
AX = mybir.AxisListType

D = 1024
S = 4096
NH = 16
HD = 64
FF = 2048
NE = 8
EPS_LN = 1e-5

SEMS = ("L", "Y", "P", "A", "V")
PHASES = ("pc", "A", "B", "C", "D", "E", "F")
DBG_EXPERTS = None


class Plan:
    def __init__(self):
        self.q = {k: [] for k in ("te", "sc", "v", "g", "sy")}
        self.n = dict.fromkeys(SEMS, 0)
        self.bank = {}

    def on(self, eng):
        def deco(fn):
            self.q[eng].append(fn)
        return deco

    def inc(self, s, by=1):
        self.n[s] += by
        return self.n[s]

    def free_at(self, res, sem, val):
        self.bank[res] = (sem, val)

    def take(self, res):
        w = self.bank.pop(res, None)
        return [w] if w else []

    def barrier(self):
        tgt = dict(self.n)
        for e in self.q:
            @self.on(e)
            def _(eng, SM, tgt=tgt):
                for s, v in tgt.items():
                    if v:
                        eng.wait_ge(SM[s], v)


def op(pl, ek, fn, wait=(), inc=None):
    tgt = pl.inc(inc, 1) if inc else None
    wait = tuple(wait)

    @pl.on(ek)
    def _(eng, SM, fn=fn, wait=wait, tgt=tgt, inc=inc):
        for s, v in wait:
            if v:
                eng.wait_ge(SM[s], v)
        ins = fn(eng)
        if tgt is not None:
            ins.then_inc(SM[inc], 1)
    return tgt


def dma(pl, ek, sem, out_ap, in_ap, wait=()):
    tgt = pl.inc(sem, 16)
    wait = tuple(wait)

    @pl.on(ek)
    def _(eng, SM, out_ap=out_ap, in_ap=in_ap, wait=wait, sem=sem):
        for s, v in wait:
            if v:
                eng.wait_ge(SM[s], v)
        eng.dma_start(out=out_ap, in_=in_ap).then_inc(SM[sem], 16)
    return tgt


def vchain(pl, steps, wait=()):
    prev = None
    first = True
    for fn in steps:
        w = list(wait) if first else [("V", prev)]
        prev = op(pl, "v", fn, wait=w, inc="V")
        first = False
    return prev


def mmg(pl, out_ap, ops, wait=(), inc=True, start=True, stop=True):
    tgt = pl.inc("P", 1) if inc else None
    wait = tuple(wait)
    ops = tuple(ops)

    @pl.on("te")
    def _(eng, SM, out_ap=out_ap, ops=ops, wait=wait, tgt=tgt, start=start, stop=stop):
        for s, v in wait:
            if v:
                eng.wait_ge(SM[s], v)
        n = len(ops)
        ins = None
        for i, (l, r) in enumerate(ops):
            ins = eng.matmul(out_ap, l, r, start=(start and i == 0),
                             stop=(stop and i == n - 1))
        if tgt is not None:
            ins.then_inc(SM["P"], 1)
    return tgt


DECLARED = []


def build_program(phases=None):
    global DECLARED
    if phases is None:
        phases = PHASES
    nc = bass.Bass("TRN2", target_bir_lowering=False, debug=False)
    DECLARED = []

    def inp(name, shape, dt=F32):
        DECLARED.append(name)
        return nc.dram_tensor(name, list(shape), dt, kind="ExternalInput").ap()

    need_l0ffn = ("A" in phases) or ("C" in phases)
    need_l0att = "B" in phases
    need_l1ffn = ("D" in phases) or ("F" in phases)
    need_l1att = "E" in phases

    xT = inp("xT", (D, S))
    if need_l0att:
        wqT = inp("wqT", (D, D)); wkT = inp("wkT", (D, D))
        wvT = inp("wvT", (D, D)); woT = inp("woT", (D, D))
        bq = inp("bq", (D,)); bk = inp("bk", (D,)); bv = inp("bv", (D,)); bo = inp("bo", (D,))
        pw1T = inp("pw1T", (3, D, 512)); pb1 = inp("pb1", (3, 512))
        pw2T = inp("pw2T", (3, 512, NH)); pb2 = inp("pb2", (3, NH))
        sw1T = inp("sw1T", (D, 512)); sb1 = inp("sb1", (512,))
        sw2T = inp("sw2T", (512, 3)); sb2 = inp("sb2", (3,))
    if need_l0ffn:
        sgT = inp("sgT", (D, FF)); suT = inp("suT", (D, FF)); sdT = inp("sdT", (FF, D))
        gw = inp("gw", (NE, D, FF)); uw = inp("uw", (NE, D, FF)); dw = inp("dw", (NE, FF, D))
        rwT = inp("rwT", (D, NE)); rb = inp("rb", (NE,))
    if need_l1att:
        w2qT = inp("w2qT", (D, D)); w2kT = inp("w2kT", (D, D))
        w2vT = inp("w2vT", (D, D)); w2oT = inp("w2oT", (D, D))
    if need_l1ffn:
        W1 = inp("W1", (NE, D, FF)); W3 = inp("W3", (NE, D, FF)); W2 = inp("W2", (NE, FF, D))
        b1 = inp("b1", (NE, FF)); b3 = inp("b3", (NE, FF)); b2 = inp("b2", (NE, D))
        Wg = inp("Wg", (D, NE)); bg = inp("bg", (NE,))
        Wf = inp("Wf", (D, D)); bf = inp("bf", (D,))
    identF = inp("identF", (128, 128))
    identB = inp("identB", (128, 128), BF16)
    maskTu = inp("maskTu", (128, 128), BF16)
    mask128 = inp("mask128", (128, 128), BF16)

    y = nc.dram_tensor("y", [D, S], F32, kind="ExternalOutput").ap()

    def scratch(name, shape, dt):
        return nc.dram_tensor(name, list(shape), dt, kind="Internal").ap()

    resid = scratch("resid", (D, S), F32)
    qTd = scratch("qTd", (D, S), BF16)
    kTd = scratch("kTd", (D, S), BF16)
    vTd = scratch("vTd", (D, S), BF16)
    aTd = scratch("aTd", (D, S), BF16)
    c_sg = scratch("c_sg", (D, FF), BF16); c_su = scratch("c_su", (D, FF), BF16)
    c_sd = scratch("c_sd", (FF, D), BF16)
    c_gw = scratch("c_gw", (NE, D, FF), BF16); c_uw = scratch("c_uw", (NE, D, FF), BF16)
    c_dw = scratch("c_dw", (NE, FF, D), BF16)
    c_W1 = scratch("c_W1", (NE, D, FF), BF16); c_W3 = scratch("c_W3", (NE, D, FF), BF16)
    c_W2 = scratch("c_W2", (NE, FF, D), BF16)
    c_Wf = scratch("c_Wf", (D, D), BF16)

    ctx = ExitStack()
    with ctx:
        def sb(name, shape, dt):
            return ctx.enter_context(nc.sbuf_tensor(name, list(shape), dt)).ap()

        xs = sb("xs", (128, 8192), F32)
        xh = sb("xh", (128, 8192), BF16)
        wsl = sb("wsl", (128, 49152), BF16)
        hb = sb("hb", (128, 12288), BF16)
        scr = sb("scr", (128, 2048), BF16)
        scf = sb("scf", (128, 512), F32)
        cmask = sb("cmask", (128, 2176), BF16)
        idF = sb("idF", (128, 128), F32)
        idB = sb("idB", (128, 128), BF16)
        onec_b = sb("onec_b", (128, 1), BF16)
        oner_b = sb("oner_b", (1, 512), BF16)
        oner_b128 = sb("oner_b128", (1, 128), BF16)
        one64_b = sb("one64_b", (64, 1), BF16)
        one_b = sb("one_b", (1, 1), BF16)
        smf = sb("smf", (128, 128), F32)
        smb = sb("smb", (128, 64), BF16)
        bcol = sb("bcol", (128, 64), F32)
        oner_f = sb("oner_f", (1, 128), F32)
        polsm = sb("polsm", (1, 64), F32)
        r_px = sb("r_px", (1, 128), F32)
        rw_sb = sb("rw_sb", (128, 64), BF16)
        rb_sb = sb("rb_sb", (8, 1), F32)
        bg_sb = sb("bg_sb", (8, 1), F32)
        b2_sb = sb("b2_sb", (8, 1024), BF16)
        wT_sb = sb("wT_sb", (8, 1024), BF16)
        polv = sb("polv", (16, 8), F32)
        pw2_sb = sb("pw2_sb", (128, 256), BF16)
        r_ms = sb("r_ms", (1, 1024), BF16)
        r_t0 = sb("r_t0", (1, 1024), BF16)
        rb_nm = sb("rb_nm", (1, 1024), BF16)
        rb_r = sb("rb_r", (1, 1024), BF16)
        w_row1 = sb("w_row1", (1, 1024), BF16)
        bst = sb("bst", (16, 128), BF16)
        b1col = sb("b1col", (128, 16), F32)
        b3col = sb("b3col", (128, 16), F32)
        bfcol = sb("bfcol", (128, 8), F32)
        pbcol = sb("pbcol", (128, 16), F32)
        pb2c = sb("pb2c", (16, 4), F32)
        r_pol = sb("r_pol", (1, 64), F32)
        r_hg = sb("r_hg", (1, 16), F32)
        r_sc = sb("r_sc", (1, 16), F32)
        kv_sb = sb("kv_sb", (64, 64), F32)
        ks_sb = sb("ks_sb", (64, 1), F32)
        at_sb = sb("at_sb", (128, 128), BF16)
        kt_sb = sb("kt_sb", (128, 64), BF16)
        vt_sb = sb("vt_sb", (128, 64), BF16)
        ps = [ctx.enter_context(nc.psum_tensor("ps%d" % i, [128, 512], F32)).ap()
              for i in range(6)]
        pt0 = ctx.enter_context(nc.psum_tensor("pt0", [128, 1024], BF16)).ap()
        pt1 = ctx.enter_context(nc.psum_tensor("pt1", [128, 1024], BF16)).ap()

        SM = {s_: ctx.enter_context(nc.semaphore(s_)) for s_ in SEMS}
        pl = Plan()

        def slotv(i, cols=16384):
            return wsl[:, i * 16384:i * 16384 + cols]

        def r3(ap_, cols):
            return ap_.rearrange("(c p) t -> p c t", p=128)

        cpool = sb("cpool", (128, 2), F32)
        nc.const_aps.register(F32, EPS_LN, cpool[:, 0:1])
        nc.const_aps.register(F32, 1e-6, cpool[:, 1:2])
        op(pl, "v", lambda e: e.memset(cpool[:, 0:1], EPS_LN))
        op(pl, "v", lambda e: e.memset(cpool[:, 1:2], 1e-6))
        op(pl, "v", lambda e: e.memset(onec_b, 1.0))
        op(pl, "v", lambda e: e.memset(oner_b, 1.0))
        op(pl, "v", lambda e: e.memset(oner_b128, 1.0))
        op(pl, "v", lambda e: e.memset(one64_b, 1.0))
        op(pl, "v", lambda e: e.memset(oner_f, 1.0))
        op(pl, "v", lambda e: e.memset(one_b, 1.0), inc="V")
        dma(pl, "sy", "Y", idF[:, :], identF[:, :])
        dma(pl, "sy", "Y", idB[:, :], identB[:, :])
        dma(pl, "sy", "Y", cmask[:, 0:128], maskTu[:, :])
        dma(pl, "sy", "Y", cmask[:, 2048:2176], mask128[:, :])
        pl.barrier()

        if "pc" in phases:
            pairs = []
            if need_l0ffn:
                pairs += [(sgT, c_sg), (suT, c_su), (sdT, c_sd)]
                for e_ in range(NE):
                    pairs += [(gw[e_], c_gw[e_]), (uw[e_], c_uw[e_]), (dw[e_], c_dw[e_])]
            if need_l1ffn:
                for e_ in range(NE):
                    pairs += [(W1[e_], c_W1[e_]), (W3[e_], c_W3[e_]), (W2[e_], c_W2[e_])]
                pairs += [(Wf, c_Wf)]
            nbuf = 0
            lastst = {}
            for (src, dst) in pairs:
                rows, cols = src.shape
                for r0 in range(0, rows, 128):
                    b = nbuf % 6
                    nbuf += 1
                    view = hb[:, b * 2048:b * 2048 + cols]
                    lt = dma(pl, "g", "L", view, src[r0:r0 + 128, :],
                             wait=([("Y", lastst[b])] if b in lastst else []))
                    lastst[b] = dma(pl, "sy", "Y", dst[r0:r0 + 128, :], view,
                                    wait=[("L", lt)])
            pl.barrier()

        def ln_stripe(src, t0, TS):
            nsub = TS // 512
            lt = dma(pl, "sy", "Y",
                     xs.rearrange("p (c t) -> p c t", t=1024)[:, :, 0:TS],
                     r3(src, S)[:, :, t0:t0 + TS],
                     wait=pl.take("xs") + pl.take("hb") + pl.take("scr") + pl.take("xh"))
            xs3 = xs.rearrange("p (c t) -> p c t", t=1024)[:, :, 0:TS]
            xb3 = hb.rearrange("p (c t) -> p c t", t=1024)[:, 0:8, 0:TS]
            sq3 = xh.rearrange("p (c t) -> p c t", t=1024)[:, 0:8, 0:TS]
            a_xb = op(pl, "sc", lambda e: e.copy(xb3, xs3), wait=[("Y", lt)], inc="A")
            v_sq = op(pl, "v", lambda e: e.tensor_mul(sq3, xs3, xs3),
                      wait=[("Y", lt)], inc="V")
            for sub in range(nsub):
                o = sub * 512
                p_m = mmg(pl, ps[4][0:1, :],
                          [(onec_b, hb[:, c * 1024 + o:c * 1024 + o + 512]) for c in range(8)],
                          wait=[("A", a_xb)] + pl.take("ps4"))
                p_s = mmg(pl, ps[5][0:1, :],
                          [(onec_b, xh[:, c * 1024 + o:c * 1024 + o + 512])
                           for c in range(8)],
                          wait=[("V", v_sq)] + pl.take("ps5"))
                v_r = vchain(pl, [
                    lambda e, o=o: e.tensor_scalar(
                        rb_nm[:, o:o + 512], ps[4][0:1, :], -1.0 / D, None, ALU.mult),
                    lambda e, o=o: e.tensor_scalar(
                        r_ms[:, o:o + 512], ps[5][0:1, :], 1.0 / D, None, ALU.mult),
                ], wait=[("P", p_m), ("P", p_s)])
                pl.free_at("ps4", "V", v_r)
                pl.free_at("ps5", "V", v_r)
            op(pl, "v", lambda e: e.tensor_mul(r_t0[:, 0:TS], rb_nm[:, 0:TS], rb_nm[:, 0:TS]),
               wait=[("V", pl.n["V"])])
            v_var = op(pl, "v", lambda e: e.tensor_sub(
                r_t0[:, 0:TS], r_ms[:, 0:TS], r_t0[:, 0:TS]),
                wait=[("V", pl.n["V"])], inc="V")
            a_std = op(pl, "sc", lambda e: e.activation(
                r_ms[:, 0:TS], r_t0[:, 0:TS], AF.Sqrt, bias=EPS_LN),
                wait=[("V", v_var)], inc="A")
            def _recip(e):
                with nc.allow_low_precision(reason="LN rstd rows in bf16; tolerance allows"):
                    return e.reciprocal(rb_r[:, 0:TS], r_ms[:, 0:TS])
            v_rows = vchain(pl, [_recip], wait=[("A", a_std)])
            for sub in range(nsub):
                o = sub * 512
                p_b1 = mmg(pl, ps[4][:, :], [(oner_b128, rb_nm[0:1, o:o + 512])],
                           wait=[("V", v_rows)] + pl.take("ps4"))
                p_b2 = mmg(pl, ps[5][:, :], [(oner_b128, rb_r[0:1, o:o + 512])],
                           wait=pl.take("ps5"))
                steps = [lambda e: e.tensor_copy(scf[:, 0:512], ps[5][:, :])]
                for c in range(8):
                    steps.append(lambda e, c=c, o=o: e.tensor_add(
                        ps[5][:, :],
                        xs[:, c * 1024 + o:c * 1024 + o + 512], ps[4][:, :]))
                    steps.append(lambda e, c=c, o=o: e.tensor_mul(
                        xh[:, c * 1024 + o:c * 1024 + o + 512],
                        ps[5][:, :], scf[:, 0:512]))
                vv = vchain(pl, steps, wait=[("P", p_b2)])
                pl.free_at("ps4", "V", vv)
                pl.free_at("ps5", "V", vv)
            return pl.n["V"]

        def store_resid(dst, t0, TS, wait_v):
            st = dma(pl, "sy", "Y", r3(dst, S)[:, :, t0:t0 + TS],
                     xs.rearrange("p (c t) -> p c t", t=1024)[:, :, 0:TS],
                     wait=[("V", wait_v)])
            pl.free_at("xs", "Y", st)
            return st

        slot_state = {0: None, 1: None, 2: None}
        slot_freeP = {0: 0, 1: 0, 2: 0}

        def load_slot(i, key, src_ap, rows, cols):
            if slot_state[i] == key:
                return 0
            slot_state[i] = key
            view = wsl[:, i * 16384:i * 16384 + (rows // 128) * cols]
            return dma(pl, "g", "L", view.rearrange("p (c t) -> p c t", t=cols),
                       r3(src_ap, cols), wait=[("P", slot_freeP[i])])

        def invalidate_slots():
            for i in range(3):
                slot_state[i] = None

        def ffn_l0(src, dst, load_router):
            TS = 1024
            if load_router:
                dma(pl, "g", "L", rw_sb.rearrange("p (c e) -> p c e", e=8), r3(rwT, 8))
                dma(pl, "g", "L", rb_sb, rb.rearrange("(p o) -> p o", o=1))
            for st_i in range(S // TS):
                t0 = st_i * TS
                v_ln = ln_stripe(src, t0, TS)

                def bc(small, like):
                    return bass.broadcast_tensor_aps(like, small)[1]

                ptw_prev = 0
                for sub in range(2):
                    o = sub * 512
                    pr = mmg(pl, ps[4][0:8, :],
                             [(rw_sb[:, c * 8:(c + 1) * 8],
                               xh[:, c * 1024 + o:c * 1024 + o + 512]) for c in range(8)],
                             wait=[("V", v_ln), ("L", pl.n["L"])] + pl.take("ps4"))
                    a_pr = op(pl, "sc", lambda e: e.activation(
                        scf[0:8, 0:512], ps[4][0:8, :], AF.Sigmoid,
                        bias=rb_sb[:, 0:1]),
                        wait=[("P", pr)] + pl.take("scf_pr"), inc="A")
                    pl.free_at("ps4", "A", a_pr)
                    for j4 in range(4):
                        j = sub * 4 + j4
                        half = (j % 2) * 128
                        ptr = op(pl, "te", lambda e, j4=j4: e.transpose(
                            ps[5][:, (j4 % 2) * 8:(j4 % 2) * 8 + 8],
                            scf[0:8, j4 * 128:(j4 + 1) * 128], idF[0:8, 0:8]),
                            wait=[("A", a_pr)] +
                                 (pl.take("ps5") if j < 2 else [("V", pl.n["V"])]),
                            inc="P")
                        pv = smf[:, 8:16]
                        v_w = vchain(pl, [
                            lambda e, j4=j4: e.tensor_copy(
                                pv, ps[5][:, (j4 % 2) * 8:(j4 % 2) * 8 + 8]),
                            lambda e: e.reduce_max(smf[:, 0:1], pv, AX.X),
                            lambda e: e.tensor_tensor(
                                smf[:, 16:24], pv, bc(smf[:, 0:1], pv), ALU.is_ge),
                            lambda e: e.scalar_tensor_tensor(
                                smf[:, 24:32], smf[:, 16:24], -1e30, pv, ALU.mult, ALU.add),
                            lambda e: e.reduce_max(smf[:, 1:2], smf[:, 24:32], AX.X),
                            lambda e: e.tensor_add(smf[:, 2:3], smf[:, 0:1], smf[:, 1:2]),
                            lambda e: e.reciprocal(smf[:, 3:4], smf[:, 2:3]),
                            lambda e: e.tensor_tensor(
                                smf[:, 16:24], pv, bc(smf[:, 1:2], pv), ALU.is_ge),
                            lambda e: e.tensor_mul(smf[:, 24:32], pv, smf[:, 16:24]),
                            lambda e: e.tensor_tensor(
                                smf[:, 24:32], smf[:, 24:32], bc(smf[:, 3:4], pv), ALU.mult),
                            lambda e: e.tensor_scalar(
                                smb[:, 0:8], smf[:, 24:32], 0.5, None, ALU.mult),
                        ], wait=[("P", ptr)] + ([("P", ptw_prev)] if ptw_prev else []))
                        ptw = op(pl, "te", lambda e, half=half: e.transpose(
                            pt0[0:8, half:half + 128], smb[:, 0:8], idB),
                            wait=[("V", v_w)] + pl.take("pt0h%d" % (j % 2)), inc="P")
                        a_w = op(pl, "sc", lambda e, j=j, half=half: e.copy(
                            wT_sb[0:8, j * 128:(j + 1) * 128], pt0[0:8, half:half + 128]),
                            wait=[("P", ptw)], inc="A")
                        pl.free_at("pt0h%d" % (j % 2), "A", a_w)
                        ptw_prev = ptw
                    pl.free_at("scf_pr", "P", ptw_prev)
                pl.free_at("ps5", "V", pl.n["V"])
                a_wlast = pl.n["A"]
                order = list(range(9)) if st_i % 2 == 0 else list(range(8, -1, -1))
                if DBG_EXPERTS is not None:
                    order = [e_ for e_ in order if e_ in DBG_EXPERTS]
                for e_i in order:
                    if e_i == 8:
                        l_g = load_slot(0, "sg", c_sg, D, FF)
                        l_u = load_slot(1, "su", c_su, D, FF)
                        l_d = load_slot(2, "sd", c_sd, FF, D)
                        a_wbc = 0
                    else:
                        l_wr = dma(pl, "g", "L", w_row1[0:1, :], wT_sb[e_i:e_i + 1, :],
                                   wait=[("A", a_wlast)] + pl.take("w_row1"))
                        l_g = load_slot(0, ("g", e_i), c_gw[e_i], D, FF)
                        l_u = load_slot(1, ("u", e_i), c_uw[e_i], D, FF)
                        l_d = load_slot(2, ("d", e_i), c_dw[e_i], FF, D)
                        mmg(pl, ps[4][:, :], [(oner_b128, w_row1[0:1, 0:512])],
                            wait=[("L", l_wr)] + pl.take("ps4"), inc=False)
                        pw = mmg(pl, ps[5][:, :], [(oner_b128, w_row1[0:1, 512:1024])],
                                 wait=pl.take("ps5"))
                        pl.free_at("w_row1", "P", pw)
                        op(pl, "sc", lambda e: e.copy(scr[:, 0:512], ps[4][:, :]),
                           wait=[("P", pw)] + pl.take("wbc"))
                        a_wbc = op(pl, "sc", lambda e: e.copy(scr[:, 512:1024], ps[5][:, :]),
                                   inc="A")
                        pl.free_at("ps4", "A", a_wbc)
                        pl.free_at("ps5", "A", a_wbc)
                    for sub in range(2):
                        o = sub * 512
                        for hc in range(16):
                            bgi = "ps%d" % (hc % 2)
                            bui = "ps%d" % (2 + hc % 2)
                            sgr = hb[:, 10240 + (hc % 2) * 512:10240 + (hc % 2) * 512 + 512]
                            usr = hb[:, 8192 + (hc % 2) * 512:8192 + (hc % 2) * 512 + 512]
                            p_g = mmg(pl, ps[hc % 2],
                                      [(slotv(0)[:, c * 2048 + hc * 128:c * 2048 + hc * 128 + 128],
                                        xh[:, c * 1024 + o:c * 1024 + o + 512]) for c in range(8)],
                                      wait=([("L", l_g), ("V", v_ln)] if hc == 0 and sub == 0 else [])
                                      + pl.take(bgi))
                            p_u = mmg(pl, ps[2 + hc % 2],
                                      [(slotv(1)[:, c * 2048 + hc * 128:c * 2048 + hc * 128 + 128],
                                        xh[:, c * 1024 + o:c * 1024 + o + 512]) for c in range(8)],
                                      wait=([("L", l_u)] if hc == 0 and sub == 0 else [])
                                      + pl.take(bui))
                            ag = op(pl, "sc", lambda e, b_=ps[hc % 2], sgr=sgr: e.activation(
                                sgr, b_, AF.Silu),
                                wait=[("P", p_g)] + pl.take("sgr%d" % (hc % 2)), inc="A")
                            pl.free_at(bgi, "A", ag)
                            if e_i != 8:
                                au = op(pl, "v", lambda e, b_=ps[2 + hc % 2], usr=usr, o=o:
                                        e.scalar_tensor_tensor(
                                            usr, b_, 1.0, scr[:, o:o + 512],
                                            ALU.mult, ALU.mult),
                                        wait=[("P", p_u)] +
                                             ([("A", a_wbc)] if hc == 0 and sub == 0 else []),
                                        inc="V")
                            else:
                                au = op(pl, "v", lambda e, b_=ps[2 + hc % 2], usr=usr:
                                        e.tensor_scalar(usr, b_, 0.5, None, ALU.mult),
                                        wait=[("P", p_u)], inc="V")
                            pl.free_at(bui, "V", au)
                            vh_ = op(pl, "v", lambda e, hc=hc, sgr=sgr, usr=usr: e.tensor_mul(
                                hb[:, hc * 512:(hc + 1) * 512], sgr, usr),
                                wait=[("A", ag)], inc="V")
                            pl.free_at("sgr%d" % (hc % 2), "V", vh_)
                        v_h = pl.n["V"]
                        for oc in range(8):
                            bdi = "ps%d" % (4 + oc % 2)
                            p_d = mmg(pl, ps[4 + oc % 2],
                                      [(slotv(2)[:, ic * 1024 + oc * 128:ic * 1024 + oc * 128 + 128],
                                        hb[:, ic * 512:(ic + 1) * 512]) for ic in range(16)],
                                      wait=([("L", l_d), ("V", v_h)] if oc == 0 else [])
                                      + pl.take(bdi))
                            vd = op(pl, "v", lambda e, oc=oc, o=o, b_=ps[4 + oc % 2]:
                                    e.tensor_add(
                                        xs[:, oc * 1024 + o:oc * 1024 + o + 512],
                                        xs[:, oc * 1024 + o:oc * 1024 + o + 512], b_),
                                    wait=[("P", p_d)], inc="V")
                            pl.free_at(bdi, "V", vd)
                        if sub == 1:
                            slot_freeP[0] = slot_freeP[1] = pl.n["P"] - 8
                    slot_freeP[2] = pl.n["P"]
                    pl.free_at("wbc", "V", pl.n["V"])
                st = store_resid(dst, t0, TS, pl.n["V"])
                pl.free_at("hb", "P", pl.n["P"])
                pl.free_at("scr", "V", pl.n["V"])
                pl.free_at("xh", "P", pl.n["P"])
                pl.barrier()

        def bcx(small, like):
            return bass.broadcast_tensor_aps(like, small)[1]

        def attn_l0(src, dst):
            TS = 1024
            for i in range(3):
                slot_state[i] = None
            l_w = 0
            for (wi, (w_, col)) in enumerate([(wqT, 0), (wkT, 8192),
                                              (wvT, 16384), (woT, 24576)]):
                l_w = dma(pl, "g", "L",
                          wsl[:, col:col + 8192].rearrange("p (c t) -> p c t", t=1024),
                          r3(w_, D), wait=[("P", slot_freeP[0])] if wi == 0 else [])
            for (bi, b_) in enumerate([bq, bk, bv, bo]):
                for oc in range(8):
                    dma(pl, "g", "L", bcol[:, bi * 8 + oc:bi * 8 + oc + 1],
                        b_[oc * 128:(oc + 1) * 128].rearrange("(p o) -> p o", o=1))
            l_bias = pl.n["L"]
            for st_i in range(4):
                t0 = st_i * TS
                v_ln = ln_stripe(src, t0, TS)
                steps = []
                for c in range(8):
                    if st_i == 0:
                        steps.append(lambda e, c=c: e.reduce_sum(
                            smf[:, 32 + c:33 + c], xh[:, c * 1024:c * 1024 + TS], AX.X))
                    else:
                        steps.append(lambda e, c=c: e.reduce_sum(
                            smf[:, 40:41], xh[:, c * 1024:c * 1024 + TS], AX.X))
                        steps.append(lambda e, c=c: e.tensor_add(
                            smf[:, 32 + c:33 + c], smf[:, 32 + c:33 + c], smf[:, 40:41]))
                v_xm = vchain(pl, steps, wait=[("V", v_ln)])
                for (pi, (col, dstT)) in enumerate([(0, qTd), (8192, kTd), (16384, vTd)]):
                    y_prev = pl.take("hb_proj")
                    a_last = 0
                    for oc in range(8):
                        for sub in range(2):
                            o = sub * 512
                            bk_ = "ps%d" % ((oc * 2 + sub) % 4)
                            p_p = mmg(pl, ps[(oc * 2 + sub) % 4],
                                      [(wsl[:, col + c * 1024 + oc * 128:col + c * 1024 + oc * 128 + 128],
                                        xh[:, c * 1024 + o:c * 1024 + o + 512]) for c in range(8)],
                                      wait=([("L", l_w), ("V", v_ln)] if oc == 0 and sub == 0 else [])
                                      + pl.take(bk_))
                            a_last = op(pl, "sc", lambda e, oc=oc, o=o, pi=pi,
                                        b_=ps[(oc * 2 + sub) % 4]: e.activation(
                                            hb[:, oc * 1024 + o:oc * 1024 + o + 512], b_,
                                            AF.Identity, bias=bcol[:, pi * 8 + oc:pi * 8 + oc + 1]),
                                        wait=[("P", p_p), ("L", l_bias)] + (y_prev if oc == 0 and sub == 0 else []),
                                        inc="A")
                            pl.free_at(bk_, "A", a_last)
                    st = dma(pl, "sy", "Y",
                             r3(dstT, S)[:, :, t0:t0 + TS],
                             hb.rearrange("p (c t) -> p c t", t=1024)[:, 0:8, 0:TS],
                             wait=[("A", a_last)])
                    pl.free_at("hb_proj", "Y", st)
                pl.free_at("hb", "Y", pl.n["Y"])
                pl.barrier()
            def pol_w(p_i):
                return hb[:, p_i * 4096:(p_i + 1) * 4096] if p_i < 3 else xh[:, 0:4096]

            lw = 0
            for p_i in range(3):
                lw = dma(pl, "g", "L",
                         hb[:, p_i * 4096:(p_i + 1) * 4096].rearrange("p (c t) -> p c t", t=512),
                         r3(pw1T[p_i], 512))
            lw = dma(pl, "g", "L",
                     xh[:, 0:4096].rearrange("p (c t) -> p c t", t=512), r3(sw1T, 512))
            for p_i in range(3):
                lw = dma(pl, "g", "L",
                         pw2_sb[:, p_i * 64:p_i * 64 + 64].rearrange("p (c t) -> p c t", t=16),
                         r3(pw2T[p_i], 16))
            lw = dma(pl, "g", "L",
                     pw2_sb[:, 192:204].rearrange("p (c t) -> p c t", t=3),
                     r3(sw2T, 3))
            a_pb = 0
            pt_b = 0
            for (p_i, brow) in enumerate([pb1[0], pb1[1], pb1[2], sb1]):
                lb = dma(pl, "g", "L", bst[0:4, :],
                         brow.rearrange("(c t) -> c t", t=128),
                         wait=[("P", pt_b)] if pt_b else [])
                pt_b = op(pl, "te", lambda e: e.transpose(
                    pt0[0:128, 0:4], bst[0:4, :], idB[0:4, 0:4]),
                    wait=[("L", lb), ("A", a_pb)], inc="P")
                a_pb = op(pl, "sc", lambda e, p_i=p_i: e.copy(
                    pbcol[:, p_i * 4:(p_i + 1) * 4], pt0[0:128, 0:4]),
                    wait=[("P", pt_b)], inc="A")
            for p_i in range(3):
                dma(pl, "g", "L", pb2c[0:16, p_i:p_i + 1],
                    pb2[p_i].rearrange("(h o) -> h o", o=1))
            dma(pl, "g", "L", pb2c[0:3, 3:4], sb2.rearrange("(h o) -> h o", o=1))
            l_pol = pl.n["L"]
            v_xm2 = vchain(pl, [
                lambda e: e.tensor_scalar(smf[:, 32:40], smf[:, 32:40], 1.0 / S, None, ALU.mult),
                lambda e: e.tensor_copy(smb[:, 8:16], smf[:, 32:40]),
            ], wait=[("V", pl.n["V"])])
            a_hid = 0
            for p_i in range(4):
                pg = 0
                wv = pol_w(p_i)
                for cc in range(4):
                    pg = mmg(pl, ps[4][:, cc:cc + 1],
                             [(wv[:, c * 512 + cc * 128:c * 512 + cc * 128 + 128],
                               smb[:, 8 + c:9 + c]) for c in range(8)],
                             wait=([("L", l_pol), ("V", v_xm2)] + pl.take("ps4"))
                             if cc == 0 else [])
                for cc in range(4):
                    a_hid = op(pl, "sc", lambda e, p_i=p_i, cc=cc: e.activation(
                        smb[:, 16 + p_i * 4 + cc:17 + p_i * 4 + cc], ps[4][:, cc:cc + 1],
                        AF.Gelu, bias=pbcol[:, p_i * 4 + cc:p_i * 4 + cc + 1]),
                        wait=[("P", pg)], inc="A")
                pl.free_at("ps4", "A", a_hid)
            pg2 = 0
            for p_i in range(3):
                pg2 = mmg(pl, ps[5][0:16, p_i:p_i + 1],
                          [(pw2_sb[:, p_i * 64 + cc * 16:p_i * 64 + cc * 16 + 16],
                            smb[:, 16 + p_i * 4 + cc:17 + p_i * 4 + cc]) for cc in range(4)],
                          wait=([("A", a_hid)] + pl.take("ps5")) if p_i == 0 else [])
            pg2 = mmg(pl, ps[5][0:3, 3:4],
                      [(pw2_sb[:, 192 + cc * 3:192 + cc * 3 + 3],
                        smb[:, 28 + cc:29 + cc]) for cc in range(4)])
            v_pol = op(pl, "v", lambda e: e.tensor_add(polv[:, 0:4], ps[5][0:16, 0:4], pb2c),
                       wait=[("P", pg2)], inc="V")
            pl.free_at("ps5", "V", v_pol)
            l_r1 = dma(pl, "g", "L",
                       r_pol[0:1, 0:48].rearrange("o (r p) -> o r p", p=3),
                       polv[0:16, 0:3], wait=[("V", v_pol)])
            l_r2 = dma(pl, "g", "L", r_pol[0:1, 48:51], polv[0:3, 3:4],
                       wait=[("V", v_pol)])
            rows = [r_pol[0:1, 0:48].rearrange("o (r p) -> o p r", p=3)
                    [:, p_i:p_i + 1, :].rearrange("o p r -> o (p r)")
                    for p_i in range(3)] + [r_pol[0:1, 48:51]]
            outsm = [polsm[0:1, p_i * 16:p_i * 16 + 16] for p_i in range(3)] + \
                    [polsm[0:1, 48:51]]
            for i_r, (rv, ov) in enumerate(zip(rows, outsm)):
                n_ = rv.shape[-1]
                v_m = vchain(pl, [
                    lambda e, rv=rv: e.reduce_max(smf[0:1, 64:65], rv, AX.X),
                    lambda e: e.tensor_scalar(smf[0:1, 65:66], smf[0:1, 64:65],
                                              -1.0, None, ALU.mult),
                ], wait=[("L", l_r1), ("L", l_r2)] if i_r == 0 else [("A", pl.n["A"])])
                a_e = op(pl, "sc", lambda e, rv=rv, n_=n_: e.activation(
                    smf[0:1, 66:66 + n_], rv, AF.Exp, bias=smf[0:1, 65:66],
                    accum_out=smf[0:1, 82:83]),
                    wait=[("V", v_m)], inc="A")
                vchain(pl, [
                    lambda e: e.reciprocal(smf[0:1, 83:84], smf[0:1, 82:83]),
                    lambda e, ov=ov, n_=n_: e.tensor_tensor(
                        ov, smf[0:1, 66:66 + n_], bcx(smf[0:1, 83:84], smf[0:1, 66:66 + n_]),
                        ALU.mult),
                ], wait=[("A", a_e)])
            v_hg = vchain(pl, [
                lambda e: e.tensor_tensor(r_hg[:, :], polsm[0:1, 0:16],
                                          bcx(polsm[0:1, 48:49], polsm[0:1, 0:16]), ALU.mult),
                lambda e: e.tensor_tensor(smf[0:1, 64:80], polsm[0:1, 16:32],
                                          bcx(polsm[0:1, 49:50], polsm[0:1, 16:32]), ALU.mult),
                lambda e: e.tensor_add(r_hg[:, :], r_hg[:, :], smf[0:1, 64:80]),
                lambda e: e.tensor_tensor(smf[0:1, 64:80], polsm[0:1, 32:48],
                                          bcx(polsm[0:1, 50:51], polsm[0:1, 32:48]), ALU.mult),
                lambda e: e.tensor_add(r_hg[:, :], r_hg[:, :], smf[0:1, 64:80]),
                lambda e: e.tensor_scalar(r_sc[:, :], r_hg[:, :], 0.125, None, ALU.mult),
                lambda e: e.tensor_copy(r_px[0:1, 0:16], r_sc[:, :]),
            ], wait=[("V", pl.n["V"])])
            p_g8 = mmg(pl, ps[4][:, 0:16], [(oner_f, r_px[0:1, 0:16])],
                       wait=[("V", v_hg)] + pl.take("ps4"))
            v_g8 = op(pl, "v", lambda e: e.tensor_copy(smf[:, 48:64], ps[4][:, 0:16]),
                      wait=[("P", p_g8)], inc="V")
            pl.free_at("ps4", "V", v_g8)
            pl.barrier()
            QH = wsl[0:64, 32768:36864]
            KH = wsl[0:64, 36864:40960]
            QSQ = wsl[64:128, 32768:36864]
            KSQ = wsl[64:128, 36864:40960]
            VH = wsl[0:64, 40960:45056]
            AH = wsl[64:128, 40960:45056]
            VT = wsl[:, 45056:47136]
            VT3 = VT.rearrange("p (c t) -> p c t", t=65)
            v_one = op(pl, "v", lambda e: e.memset(VT3[:, :, 64:65], 1.0), inc="V")
            NPB = 6
            for h in range(NH):
                lq = dma(pl, "sy", "Y", QH, qTd[64 * h:64 * h + 64, :],
                         wait=pl.take("slot2"))
                lk = dma(pl, "sy", "Y", KH, kTd[64 * h:64 * h + 64, :])
                lv = dma(pl, "sy", "Y", VH, vTd[64 * h:64 * h + 64, :])
                v_sq2 = vchain(pl, [
                    lambda e: e.tensor_mul(QSQ, QH, QH),
                    lambda e: e.tensor_mul(KSQ, KH, KH),
                ], wait=[("Y", lv)])
                for jb in range(4):
                    pt_v = 0
                    for j2 in range(8):
                        j = jb * 8 + j2
                        pt_v = op(pl, "te", lambda e, j=j, j2=j2: e.transpose(
                            pt1[:, j2 * 64:j2 * 64 + 64],
                            VH[:, j * 128:(j + 1) * 128], idB[0:64, 0:64]),
                            wait=([("Y", lv)] if j == 0 else []) +
                                 (pl.take("pt1") if j2 == 0 else []),
                            inc="P" if j2 == 7 else None)
                    v_vt = op(pl, "v", lambda e, jb=jb: e.tensor_copy(
                        VT3[:, jb * 8:(jb + 1) * 8, 0:64],
                        pt1[:, 0:512].rearrange("p (c t) -> p c t", t=64)),
                        wait=[("P", pt_v)] + ([("V", v_one)] if h == 0 and jb == 0 else []),
                        inc="V")
                    pl.free_at("pt1", "V", v_vt)
                for (sqv, colo) in ((KSQ, 52), (QSQ, 53)):
                    p_kn = 0
                    for j in range(32):
                        p_kn = mmg(pl, ps[4][:, j:j + 1],
                                   [(sqv[:, j * 128:(j + 1) * 128], onec_b[64:128, :])],
                                   wait=([("V", v_sq2)] + pl.take("ps4")) if j == 0 else [],
                                   inc=(j == 31))
                    v_kn = vchain(pl, [
                        lambda e: e.tensor_copy(smf[:, 64:96], ps[4][:, 0:32]),
                        lambda e: e.reduce_max(smf[:, 96:97], smf[:, 64:96], AX.X),
                    ], wait=[("P", p_kn)])
                    pl.free_at("ps4", "V", v_kn)
                    pt_k = op(pl, "te", lambda e: e.transpose(
                        ps[5][0:1, 0:128], smf[:, 96:97], idF),
                        wait=[("V", v_kn)] + pl.take("ps5") + pl.take("psS3"), inc="P")
                    v_km = vchain(pl, [
                        lambda e: e.tensor_copy(r_px[0:1, 0:128], ps[5][0:1, 0:128]),
                        lambda e, colo=colo: e.reduce_max(
                            polsm[0:1, colo:colo + 1], r_px[0:1, 0:128], AX.X),
                    ], wait=[("P", pt_k)])
                    pl.free_at("ps5", "V", v_km)
                a_sk = op(pl, "sc", lambda e: e.activation(
                    polsm[0:1, 55:57], polsm[0:1, 52:54], AF.Sqrt),
                    wait=[("V", v_km)], inc="A")
                v_sk = vchain(pl, [
                    lambda e: e.tensor_mul(polsm[0:1, 57:58], polsm[0:1, 55:56],
                                           polsm[0:1, 56:57]),
                    lambda e, h=h: e.tensor_mul(polsm[0:1, 57:58], polsm[0:1, 57:58],
                                                r_sc[0:1, h:h + 1]),
                    lambda e: e.tensor_scalar(polsm[0:1, 57:58], polsm[0:1, 57:58],
                                              -1.0, None, ALU.mult),
                ], wait=[("A", a_sk)])
                p_nb = mmg(pl, ps[5][:, 0:1], [(oner_f, polsm[0:1, 57:58])],
                           wait=[("V", v_sk)] + pl.take("ps5"))
                v_nbh = op(pl, "v", lambda e: e.tensor_copy(smf[:, 99:100], ps[5][:, 0:1]),
                           wait=[("P", p_nb)], inc="V")
                pl.free_at("ps5", "V", v_nbh)
                SB = (0, 1, 3, 5)
                for qi in range(32):
                    nkc = qi + 1
                    a_ex = {}

                    def emit_QKT(kc):
                        bsi = "psS%d" % (kc % 4)
                        w_ = pl.take(bsi)
                        if qi == 0 and kc == 0:
                            w_ += [("Y", lk), ("V", v_nbh)]
                        if kc % 4 == 3 and qi <= 4:
                            w_ += [("V", v_nbh)]
                        return mmg(pl, ps[SB[kc % 4]][:, 0:128],
                                   [(KH[:, kc * 128:(kc + 1) * 128],
                                     QH[:, qi * 128:(qi + 1) * 128])],
                                   wait=w_)

                    def emit_EXP(kc, p_s):
                        pb = "prb%d" % (kc % NPB)
                        a_ex[kc] = op(pl, "sc", lambda e, kc=kc, b_=ps[SB[kc % 4]]:
                                      e.activation(
                                          scr[:, (kc % NPB) * 128:(kc % NPB) * 128 + 128],
                                          b_[:, 0:128], AF.Exp,
                                          bias=smf[:, 99:100], scale=smf[:, 48 + h:49 + h]),
                                      wait=[("P", p_s)] + pl.take(pb), inc="A")
                        pl.free_at("psS%d" % (kc % 4), "A", a_ex[kc])
                        if kc == qi:
                            v_msk = op(pl, "v", lambda e, kc=kc: e.tensor_mul(
                                scr[:, (kc % NPB) * 128:(kc % NPB) * 128 + 128],
                                scr[:, (kc % NPB) * 128:(kc % NPB) * 128 + 128],
                                cmask[:, 0:128]),
                                wait=[("A", a_ex[kc])], inc="V")
                            a_ex[kc] = ("V", v_msk)
                        else:
                            a_ex[kc] = ("A", a_ex[kc])

                    def emit_PV(kc):
                        p_pv = mmg(pl, ps[2][:, 0:65],
                                   [(scr[:, (kc % NPB) * 128:(kc % NPB) * 128 + 128],
                                     VT[:, kc * 65:kc * 65 + 65])],
                                   wait=[a_ex[kc]] + (pl.take("ps2") if kc == 0 else []),
                                   start=(kc == 0), stop=(kc == nkc - 1))
                        pl.free_at("prb%d" % (kc % NPB), "P", p_pv)

                    for v_s in range(nkc + 2):
                        p_s = emit_QKT(v_s) if v_s < nkc else None
                        if 0 <= v_s - 2 < nkc:
                            emit_PV(v_s - 2)
                        if v_s < nkc:
                            emit_EXP(v_s, p_s)
                    v_rs = vchain(pl, [
                        lambda e: e.reciprocal(smf[:, 113:114], ps[2][:, 64:65]),
                        lambda e: e.tensor_tensor(
                            smb[:, 0:64], ps[2][:, 0:64],
                            bcx(smf[:, 113:114], ps[2][:, 0:64]), ALU.mult),
                    ], wait=[("P", pl.n["P"])])
                    pl.free_at("ps2", "V", v_rs)
                    pt_o = op(pl, "te", lambda e, qi=qi: e.transpose(
                        pt1[0:64, (qi % 8) * 128:(qi % 8) * 128 + 128],
                        smb[:, 0:64], idB),
                        wait=[("V", v_rs)] + (pl.take("pt1") if qi % 8 == 0 else []),
                        inc="P")
                    if qi % 8 == 7:
                        v_oe = op(pl, "v", lambda e, qi=qi: e.tensor_copy(
                            AH[:, (qi - 7) * 128:(qi + 1) * 128], pt1[0:64, 0:1024]),
                            wait=[("P", pt_o)], inc="V")
                        pl.free_at("pt1", "V", v_oe)
                st_a = dma(pl, "sy", "Y", aTd[64 * h:64 * h + 64, :], AH,
                           wait=[("V", pl.n["V"])])
                pl.free_at("slot2", "Y", st_a)
            pl.barrier()
            for st_i in range(4):
                t0 = st_i * TS
                lx = dma(pl, "sy", "Y",
                         xs.rearrange("p (c t) -> p c t", t=1024)[:, :, 0:TS],
                         r3(src, S)[:, :, t0:t0 + TS], wait=pl.take("xs"))
                la = dma(pl, "sy", "Y",
                         hb.rearrange("p (c t) -> p c t", t=1024)[:, 0:8, 0:TS],
                         r3(aTd, S)[:, :, t0:t0 + TS], wait=pl.take("hb"))
                v_last = 0
                for oc in range(8):
                    for sub in range(2):
                        o = sub * 512
                        bk_ = "ps%d" % ((oc * 2 + sub) % 4)
                        p_p = mmg(pl, ps[(oc * 2 + sub) % 4],
                                  [(wsl[:, 16384 + 8192 + c * 1024 + oc * 128:
                                        16384 + 8192 + c * 1024 + oc * 128 + 128],
                                    hb[:, c * 1024 + o:c * 1024 + o + 512]) for c in range(8)],
                                  wait=([("Y", la), ("Y", lx)] if oc == 0 and sub == 0 else [])
                                  + pl.take(bk_))
                        v_last = op(pl, "v", lambda e, oc=oc, o=o,
                                    b_=ps[(oc * 2 + sub) % 4]: e.scalar_tensor_tensor(
                                        xs[:, oc * 1024 + o:oc * 1024 + o + 512], b_,
                                        bcol[:, 24 + oc:25 + oc],
                                        xs[:, oc * 1024 + o:oc * 1024 + o + 512],
                                        ALU.add, ALU.add),
                                    wait=[("P", p_p)], inc="V")
                        pl.free_at(bk_, "V", v_last)
                st = store_resid(dst, t0, TS, v_last)
                pl.free_at("hb", "Y", st)
                pl.barrier()
            for i in range(3):
                slot_state[i] = None
                slot_freeP[i] = pl.n["P"]

        def ffn_l1(src, dst, load_gw):
            TS = 512
            if load_gw:
                dma(pl, "g", "L", rw_sb.rearrange("p (c e) -> p c e", e=8), r3(Wg, 8))
                dma(pl, "g", "L", bg_sb, bg.rearrange("(p o) -> p o", o=1))
                dma(pl, "g", "L", b2_sb, b2)
                lb = dma(pl, "g", "L", bst[0:8, :], bf.rearrange("(c t) -> c t", t=128),
                         wait=pl.take("bst"))
                pt_b = op(pl, "te", lambda e: e.transpose(
                    pt0[0:128, 0:8], bst[0:8, :], idB[0:8, 0:8]),
                    wait=[("L", lb)] + pl.take("pt0f"), inc="P")
                v_bf = op(pl, "v", lambda e: e.tensor_scalar(
                    bfcol[:, 0:8], pt0[0:128, 0:8], 0.5, None, ALU.mult),
                    wait=[("P", pt_b)], inc="V")
                pl.free_at("bst", "P", pt_b)
                pl.free_at("pt0f", "V", v_bf)
            for st_i in range(8):
                t0 = st_i * TS
                v_ln = ln_stripe(src, t0, TS)
                v_mx = op(pl, "v", lambda e: e.memset(
                    xs.rearrange("p (c t) -> p c t", t=1024)[:, :, 512:1024], 0.0),
                    wait=[("V", v_ln)], inc="V")
                pr = mmg(pl, ps[4][0:8, :],
                         [(rw_sb[:, c * 8:(c + 1) * 8],
                           xh[:, c * 1024:c * 1024 + 512]) for c in range(8)],
                         wait=[("V", v_ln), ("L", pl.n["L"])] + pl.take("ps4"))
                v_lg = op(pl, "v", lambda e: e.tensor_scalar(
                    scf[0:8, 0:512], ps[4][0:8, :], bg_sb[:, 0:1], None, ALU.add),
                    wait=[("P", pr)] + pl.take("scf_pr"), inc="V")
                pl.free_at("ps4", "V", v_lg)
                ptw_prev = 0
                for j in range(4):
                    half = (j % 2) * 128
                    ptr = op(pl, "te", lambda e, j=j: e.transpose(
                        ps[5][:, (j % 2) * 8:(j % 2) * 8 + 8],
                        scf[0:8, j * 128:(j + 1) * 128], idF[0:8, 0:8]),
                        wait=[("V", v_lg)] +
                             (pl.take("ps5") if j < 2 else [("V", pl.n["V"])]),
                        inc="P")
                    pv = smf[:, 8:16]
                    v_m = vchain(pl, [
                        lambda e, j=j: e.tensor_copy(
                            pv, ps[5][:, (j % 2) * 8:(j % 2) * 8 + 8]),
                        lambda e: e.reduce_max(smf[:, 0:1], pv, AX.X),
                        lambda e: e.tensor_scalar(smf[:, 1:2], smf[:, 0:1],
                                                  -1.0, None, ALU.mult),
                    ], wait=[("P", ptr)] + ([("P", ptw_prev)] if ptw_prev else []))
                    a_e = op(pl, "sc", lambda e: e.activation(
                        smf[:, 16:24], pv, AF.Exp, bias=smf[:, 1:2],
                        accum_out=smf[:, 2:3]),
                        wait=[("V", v_m)], inc="A")
                    v_w = vchain(pl, [
                        lambda e: e.reciprocal(smf[:, 3:4], smf[:, 2:3]),
                        lambda e: e.tensor_tensor(
                            pv, smf[:, 16:24], bcx(smf[:, 3:4], smf[:, 16:24]), ALU.mult),
                        lambda e: e.reduce_max(smf[:, 0:1], pv, AX.X),
                        lambda e: e.tensor_tensor(
                            smf[:, 16:24], pv, bcx(smf[:, 0:1], pv), ALU.is_ge),
                        lambda e: e.scalar_tensor_tensor(
                            smf[:, 24:32], smf[:, 16:24], -1e30, pv, ALU.mult, ALU.add),
                        lambda e: e.reduce_max(smf[:, 1:2], smf[:, 24:32], AX.X),
                        lambda e: e.tensor_tensor(
                            smf[:, 16:24], pv, bcx(smf[:, 1:2], pv), ALU.is_ge),
                        lambda e: e.tensor_mul(smf[:, 24:32], pv, smf[:, 16:24]),
                        lambda e: e.tensor_scalar(
                            smb[:, 0:8], smf[:, 24:32], 0.5, None, ALU.mult),
                    ], wait=[("A", a_e)])
                    ptw = op(pl, "te", lambda e, half=half: e.transpose(
                        pt0[0:8, half:half + 128], smb[:, 0:8], idB),
                        wait=[("V", v_w)] + pl.take("pt0h%d" % (j % 2)), inc="P")
                    a_w = op(pl, "sc", lambda e, j=j, half=half: e.copy(
                        wT_sb[0:8, j * 128:(j + 1) * 128], pt0[0:8, half:half + 128]),
                        wait=[("P", ptw)], inc="A")
                    pl.free_at("pt0h%d" % (j % 2), "A", a_w)
                    ptw_prev = ptw
                pl.free_at("scf_pr", "P", ptw_prev)
                pl.free_at("ps5", "V", pl.n["V"])
                a_wlast = pl.n["A"]
                order = list(range(8)) if st_i % 2 == 0 else list(range(7, -1, -1))
                if DBG_EXPERTS is not None:
                    order = [e_ for e_ in order if e_ in DBG_EXPERTS]
                for e_i in order:
                    l_wr = dma(pl, "g", "L", w_row1[0:1, 0:512], wT_sb[e_i:e_i + 1, 0:512],
                               wait=[("A", a_wlast)] + pl.take("w_row1"))
                    l_g = load_slot(0, ("W1", e_i), c_W1[e_i], D, FF)
                    l_u = load_slot(1, ("W3", e_i), c_W3[e_i], D, FF)
                    l_d = load_slot(2, ("W2", e_i), c_W2[e_i], FF, D)
                    a_bc = 0
                    for (bi, (bsrc, bdst)) in enumerate([(b1[e_i], b1col), (b3[e_i], b3col)]):
                        lb = dma(pl, "g", "L", bst[:, :],
                                 bsrc.rearrange("(c t) -> c t", t=128),
                                 wait=pl.take("bst"))
                        pt_b = op(pl, "te", lambda e: e.transpose(
                            pt0[0:128, 0:16], bst[:, :], idB[0:16, 0:16]),
                            wait=[("L", lb)] + pl.take("pt0f") + pl.take("pt0h0")
                                 + pl.take("pt0h1"), inc="P")
                        pl.free_at("bst", "P", pt_b)
                        a_bc = op(pl, "sc", lambda e, bdst=bdst: e.copy(
                            bdst[:, 0:16], pt0[0:128, 0:16]),
                            wait=[("P", pt_b)], inc="A")
                        pl.free_at("pt0f", "A", a_bc)
                    mmg(pl, ps[4][:, :], [(oner_b128, w_row1[0:1, 0:512])],
                        wait=[("L", l_wr)] + pl.take("ps4"), inc=False)
                    pw = mmg(pl, ps[5][:, :], [(oner_b128, w_row1[0:1, 0:512])],
                             wait=pl.take("ps5"))
                    pl.free_at("w_row1", "P", pw)
                    a_wbc = op(pl, "sc", lambda e: e.copy(scr[:, 0:512], ps[4][:, :]),
                               wait=[("P", pw)] + pl.take("wbc"), inc="A")
                    pl.free_at("ps4", "A", a_wbc)
                    pl.free_at("ps5", "A", a_wbc)
                    for hc in range(16):
                        bgi = "ps%d" % (hc % 2)
                        bui = "ps%d" % (2 + hc % 2)
                        sgr = hb[:, 10240 + (hc % 2) * 512:10240 + (hc % 2) * 512 + 512]
                        usr = hb[:, 8192 + (hc % 2) * 512:8192 + (hc % 2) * 512 + 512]
                        p_g = mmg(pl, ps[hc % 2],
                                  [(slotv(0)[:, c * 2048 + hc * 128:c * 2048 + hc * 128 + 128],
                                    xh[:, c * 1024:c * 1024 + 512]) for c in range(8)],
                                  wait=([("L", l_g), ("V", v_ln)] if hc == 0 else [])
                                  + pl.take(bgi))
                        p_u = mmg(pl, ps[2 + hc % 2],
                                  [(slotv(1)[:, c * 2048 + hc * 128:c * 2048 + hc * 128 + 128],
                                    xh[:, c * 1024:c * 1024 + 512]) for c in range(8)],
                                  wait=([("L", l_u)] if hc == 0 else [])
                                  + pl.take(bui))
                        ag = op(pl, "sc", lambda e, hc=hc, b_=ps[hc % 2], sgr=sgr:
                                e.activation(sgr, b_, AF.Identity,
                                             bias=b1col[:, hc:hc + 1]),
                                wait=[("P", p_g), ("A", a_bc)] + pl.take("sgr%d" % (hc % 2)),
                                inc="A")
                        pl.free_at(bgi, "A", ag)
                        au = op(pl, "v", lambda e, hc=hc, b_=ps[2 + hc % 2], sgr=sgr, usr=usr:
                                e.scalar_tensor_tensor(
                                    usr, b_, b3col[:, hc:hc + 1], sgr, ALU.add, ALU.mult),
                                wait=[("P", p_u), ("A", ag)], inc="V")
                        pl.free_at(bui, "V", au)
                        ag2 = op(pl, "sc", lambda e, sgr=sgr, usr=usr: e.activation(
                            sgr, usr, AF.Silu),
                            wait=[("V", au)], inc="A")
                        vh_ = op(pl, "v", lambda e, hc=hc, sgr=sgr: e.tensor_mul(
                            hb[:, hc * 512:(hc + 1) * 512], sgr, scr[:, 0:512]),
                            wait=[("A", ag2)] + ([("A", a_wbc)] if hc == 0 else []),
                            inc="V")
                        pl.free_at("sgr%d" % (hc % 2), "V", vh_)
                    v_h = pl.n["V"]
                    for oc in range(8):
                        bdi = "ps%d" % (4 + oc % 2)
                        p_d = mmg(pl, ps[4 + oc % 2],
                                  [(slotv(2)[:, ic * 1024 + oc * 128:ic * 1024 + oc * 128 + 128],
                                    hb[:, ic * 512:(ic + 1) * 512]) for ic in range(16)],
                                  wait=([("L", l_d), ("V", v_h), ("V", v_mx)] if oc == 0 else [])
                                  + pl.take(bdi))
                        vd = op(pl, "v", lambda e, oc=oc, b_=ps[4 + oc % 2]: e.tensor_add(
                            xs[:, oc * 1024 + 512:oc * 1024 + 1024],
                            xs[:, oc * 1024 + 512:oc * 1024 + 1024], b_),
                            wait=[("P", p_d)], inc="V")
                        pl.free_at(bdi, "V", vd)
                    slot_freeP[0] = slot_freeP[1] = pl.n["P"] - 8
                    slot_freeP[2] = pl.n["P"]
                    pl.free_at("wbc", "V", pl.n["V"])
                l_f = load_slot(2, "Wf", c_Wf, D, D)
                v_mix = pl.n["V"]
                a_mb = 0
                for oc in range(8):
                    p_b2 = mmg(pl, ps[4 + oc % 2],
                               [(b2_sb[:, oc * 128:(oc + 1) * 128], wT_sb[:, 0:512])],
                               wait=([("A", pl.n["A"]), ("V", v_mix)] if oc == 0 else [])
                               + pl.take("ps%d" % (4 + oc % 2)))
                    a_mb = op(pl, "v", lambda e, oc=oc, b_=ps[4 + oc % 2]: e.tensor_add(
                        hb[:, oc * 512:(oc + 1) * 512],
                        xs[:, oc * 1024 + 512:oc * 1024 + 1024], b_),
                        wait=[("P", p_b2)], inc="V")
                    pl.free_at("ps%d" % (4 + oc % 2), "V", a_mb)
                v_mb = pl.n["V"]
                for oc in range(8):
                    bk_ = "ps%d" % (oc % 2)
                    p_f = mmg(pl, ps[oc % 2],
                              [(slotv(2)[:, ic * 1024 + oc * 128:ic * 1024 + oc * 128 + 128],
                                hb[:, ic * 512:(ic + 1) * 512]) for ic in range(8)],
                              wait=([("L", l_f), ("V", v_mb)] if oc == 0 else [])
                              + pl.take(bk_))
                    v_f = op(pl, "v", lambda e, oc=oc, b_=ps[oc % 2]: e.scalar_tensor_tensor(
                        xs[:, oc * 1024:oc * 1024 + 512], b_, bfcol[:, oc:oc + 1],
                        xs[:, oc * 1024:oc * 1024 + 512], ALU.add, ALU.add),
                        wait=[("P", p_f)], inc="V")
                    pl.free_at(bk_, "V", v_f)
                slot_freeP[2] = pl.n["P"]
                st = store_resid(dst, t0, TS, pl.n["V"])
                pl.free_at("hb", "P", pl.n["P"])
                pl.free_at("scr", "V", pl.n["V"])
                pl.free_at("xh", "P", pl.n["P"])
                pl.barrier()

        def attn_l1(src, dst):
            TS = 1024
            for i in range(3):
                slot_state[i] = None
            l_w = 0
            for (wi, (w_, col)) in enumerate([(w2qT, 0), (w2kT, 8192),
                                              (w2vT, 16384), (w2oT, 24576)]):
                l_w = dma(pl, "g", "L",
                          wsl[:, col:col + 8192].rearrange("p (c t) -> p c t", t=1024),
                          r3(w_, D), wait=[("P", slot_freeP[0])] if wi == 0 else [])
            v_z = op(pl, "v", lambda e: e.memset(bcol[:, 32:40], 0.0), inc="V")
            for st_i in range(4):
                t0 = st_i * TS
                v_ln = ln_stripe(src, t0, TS)
                for (pi, (col, dstT)) in enumerate([(0, qTd), (8192, kTd), (16384, vTd)]):
                    y_prev = pl.take("hb_proj")
                    a_last = 0
                    v_lastp = 0
                    for oc in range(8):
                        for sub in range(2):
                            o = sub * 512
                            bk_ = "ps%d" % ((oc * 2 + sub) % 4)
                            p_p = mmg(pl, ps[(oc * 2 + sub) % 4],
                                      [(wsl[:, col + c * 1024 + oc * 128:col + c * 1024 + oc * 128 + 128],
                                        xh[:, c * 1024 + o:c * 1024 + o + 512]) for c in range(8)],
                                      wait=([("L", l_w), ("V", v_ln)] if oc == 0 and sub == 0 else [])
                                      + pl.take(bk_))
                            if pi < 2:
                                v_lastp = op(pl, "v", lambda e, oc=oc, o=o,
                                             b_=ps[(oc * 2 + sub) % 4]: e.tensor_scalar(
                                                 hb[:, oc * 1024 + o:oc * 1024 + o + 512], b_,
                                                 0.0, 1e-6, ALU.max, ALU.add),
                                             wait=[("P", p_p)] + (y_prev if oc == 0 and sub == 0 else []),
                                             inc="V")
                                pl.free_at(bk_, "V", v_lastp)
                                a_last = None
                            else:
                                a_last = op(pl, "sc", lambda e, oc=oc, o=o,
                                            b_=ps[(oc * 2 + sub) % 4]: e.copy(
                                                hb[:, oc * 1024 + o:oc * 1024 + o + 512], b_),
                                            wait=[("P", p_p)] + (y_prev if oc == 0 and sub == 0 else []),
                                            inc="A")
                                pl.free_at(bk_, "A", a_last)
                    wsem = [("A", a_last)] if a_last else [("V", v_lastp)]
                    st = dma(pl, "sy", "Y",
                             r3(dstT, S)[:, :, t0:t0 + TS],
                             hb.rearrange("p (c t) -> p c t", t=1024)[:, 0:8, 0:TS],
                             wait=wsem)
                    pl.free_at("hb_proj", "Y", st)
                pl.free_at("hb", "Y", pl.n["Y"])
                pl.barrier()
            QH = wsl[0:64, 32768:36864]
            KH = wsl[0:64, 36864:40960]
            VH = wsl[0:64, 40960:45056]
            AH = wsl[64:128, 40960:45056]
            for h in range(NH):
                lq = dma(pl, "sy", "Y", QH, qTd[64 * h:64 * h + 64, :],
                         wait=pl.take("slot2"))
                lk = dma(pl, "sy", "Y", KH, kTd[64 * h:64 * h + 64, :])
                lv = dma(pl, "sy", "Y", VH, vTd[64 * h:64 * h + 64, :])
                v_st0 = vchain(pl, [
                    lambda e: e.memset(kv_sb, 0.0),
                    lambda e: e.memset(ks_sb, 0.0),
                    lambda e: e.memset(kt_sb, 0.0),
                    lambda e: e.memset(vt_sb, 0.0),
                    lambda e: e.memset(smb[:, 0:64], 0.0),
                ], wait=[])
                kvb = scr[:, 512:576]
                ksb = scr[:, 576:577]
                v_kvb = vchain(pl, [
                    lambda e: e.memset(scr[0:64, 512:577], 0.0),
                ], wait=[("V", v_st0)])
                for ci in range(32):
                    T0 = ci * 128
                    p_a = mmg(pl, ps[0][:, 0:128],
                              [(QH[:, T0:T0 + 128], KH[:, T0:T0 + 128])],
                              wait=([("Y", lv), ("V", v_kvb)] if ci == 0 else [])
                              + pl.take("ps0"))
                    v_a = op(pl, "v", lambda e: e.scalar_tensor_tensor(
                        at_sb, ps[0][:, 0:128], 1.0, cmask[:, 2048:2176],
                        ALU.mult, ALU.mult, accum_out=smf[:, 104:105]),
                        wait=[("P", p_a)] + pl.take("at"), inc="V")
                    pl.free_at("ps0", "V", v_a)
                    pt_kv = 0
                    for (ti, SRC) in enumerate([KH, VH]):
                        pt_kv = op(pl, "te", lambda e, ti=ti, SRC=SRC, T0=T0: e.transpose(
                            pt1[:, ti * 64:ti * 64 + 64], SRC[:, T0:T0 + 128],
                            idB[0:64, 0:64]),
                            wait=(pl.take("pt1") if ti == 0 else []), inc="P" if ti else None)
                    v_kt = vchain(pl, [
                        lambda e: e.tensor_copy(kt_sb, pt1[:, 0:64]),
                        lambda e: e.tensor_copy(vt_sb, pt1[:, 64:128]),
                    ], wait=[("P", pt_kv)] + pl.take("ktvt"))
                    pl.free_at("pt1", "V", v_kt)
                    pt_at = op(pl, "te", lambda e: e.transpose(
                        pt0[:, 0:128], at_sb, idB),
                        wait=[("V", v_a)] + pl.take("pt0at"), inc="P")
                    v_at = op(pl, "v", lambda e: e.tensor_copy(scr[:, 0:128], pt0[:, 0:128]),
                              wait=[("P", pt_at)] + pl.take("at2"), inc="V")
                    pl.free_at("pt0at", "V", v_at)
                    pl.free_at("at", "P", pt_at)
                    mmg(pl, ps[2][:, 0:64], [(scr[:, 0:128], vt_sb)],
                        wait=[("V", v_at), ("V", v_kt)] + pl.take("ps2"),
                        start=True, stop=False, inc=False)
                    p_o = mmg(pl, ps[2][:, 0:64],
                              [(QH[:, T0:T0 + 128], kvb[0:64, :])],
                              start=False, stop=True)
                    p_z = mmg(pl, ps[3][:, 0:1],
                              [(QH[:, T0:T0 + 128], ksb[0:64, :])],
                              wait=pl.take("ps3"))
                    v_o = vchain(pl, [
                        lambda e: e.tensor_add(smf[:, 105:106], ps[3][:, 0:1],
                                               smf[:, 104:105]),
                        lambda e: e.tensor_scalar(smf[:, 105:106], smf[:, 105:106],
                                                  1e-6, None, ALU.add),
                        lambda e: e.reciprocal(smf[:, 106:107], smf[:, 105:106]),
                        lambda e: e.tensor_tensor(
                            smb[:, 0:64], ps[2][:, 0:64],
                            bcx(smf[:, 106:107], ps[2][:, 0:64]), ALU.mult),
                    ], wait=[("P", p_o), ("P", p_z)])
                    pl.free_at("ps3", "V", v_o)
                    pl.free_at("ps2", "V", v_o)
                    p_kv = mmg(pl, ps[4][0:64, 0:64], [(kt_sb, vt_sb)],
                               wait=pl.take("ps4s"))
                    p_ks = mmg(pl, ps[5][0:64, 0:1], [(kt_sb, onec_b)],
                               wait=pl.take("ps5s"))
                    v_kv = vchain(pl, [
                        lambda e: e.tensor_add(kv_sb, kv_sb, ps[4][0:64, 0:64]),
                        lambda e: e.tensor_add(ks_sb, ks_sb, ps[5][0:64, 0:1]),
                        lambda e: e.tensor_copy(kvb[0:64, :], kv_sb),
                        lambda e: e.tensor_copy(ksb[0:64, :], ks_sb),
                    ], wait=[("P", p_kv), ("P", p_ks)])
                    pl.free_at("ps4s", "V", v_kv)
                    pl.free_at("ps5s", "V", v_kv)
                    pl.free_at("ktvt", "P", max(p_kv, p_ks))
                    pl.free_at("at2", "P", p_o)
                    pt_o = op(pl, "te", lambda e, ci=ci: e.transpose(
                        pt0[0:64, 128 + (ci % 4) * 128:128 + (ci % 4) * 128 + 128],
                        smb[:, 0:64], idB),
                        wait=[("V", v_o)] + (pl.take("pt0o") if ci % 4 == 0 else []),
                        inc="P")
                    if ci % 4 == 3:
                        v_oe = op(pl, "v", lambda e, ci=ci: e.tensor_copy(
                            AH[:, (ci - 3) * 128:(ci + 1) * 128], pt0[0:64, 128:640]),
                            wait=[("P", pt_o)], inc="V")
                        pl.free_at("pt0o", "V", v_oe)
                st_a = dma(pl, "sy", "Y", aTd[64 * h:64 * h + 64, :], AH,
                           wait=[("V", pl.n["V"])])
                pl.free_at("slot2", "Y", st_a)
            pl.barrier()
            for st_i in range(4):
                t0 = st_i * TS
                lx = dma(pl, "sy", "Y",
                         xs.rearrange("p (c t) -> p c t", t=1024)[:, :, 0:TS],
                         r3(src, S)[:, :, t0:t0 + TS], wait=pl.take("xs"))
                la = dma(pl, "sy", "Y",
                         hb.rearrange("p (c t) -> p c t", t=1024)[:, 0:8, 0:TS],
                         r3(aTd, S)[:, :, t0:t0 + TS], wait=pl.take("hb"))
                v_last = 0
                for oc in range(8):
                    for sub in range(2):
                        o = sub * 512
                        bk_ = "ps%d" % ((oc * 2 + sub) % 4)
                        p_p = mmg(pl, ps[(oc * 2 + sub) % 4],
                                  [(wsl[:, 24576 + c * 1024 + oc * 128:
                                        24576 + c * 1024 + oc * 128 + 128],
                                    hb[:, c * 1024 + o:c * 1024 + o + 512]) for c in range(8)],
                                  wait=([("Y", la), ("Y", lx), ("V", v_z)]
                                        if oc == 0 and sub == 0 else [])
                                  + pl.take(bk_))
                        v_last = op(pl, "v", lambda e, oc=oc, o=o,
                                    b_=ps[(oc * 2 + sub) % 4]: e.tensor_add(
                                        xs[:, oc * 1024 + o:oc * 1024 + o + 512],
                                        xs[:, oc * 1024 + o:oc * 1024 + o + 512], b_),
                                    wait=[("P", p_p)], inc="V")
                        pl.free_at(bk_, "V", v_last)
                st = store_resid(dst, t0, TS, v_last)
                pl.free_at("hb", "Y", st)
                pl.barrier()
            for i in range(3):
                slot_state[i] = None
                slot_freeP[i] = pl.n["P"]

        if "A" in phases:
            ffn_l0(xT, resid, load_router=True)
        if "B" in phases:
            attn_l0(resid if "A" in phases else xT, resid)
        if "C" in phases:
            ffn_l0(resid if ("A" in phases or "B" in phases) else xT, resid,
                   load_router="A" not in phases)
        ran = any(p in phases for p in "ABC")
        if "D" in phases:
            ffn_l1(resid if ran else xT, resid, load_gw=True)
            ran = True
        if "E" in phases:
            attn_l1(resid if ran else xT, resid)
            ran = True
        if "F" in phases:
            ffn_l1(resid if ran else xT, resid, load_gw="D" not in phases)

        ran_any = any(p in phases for p in "ABCDEF")
        src_final = resid if ran_any else xT
        done = dma(pl, "sy", "Y", y[:, :], src_final[:, :])

        @pl.on("sy")
        def _(eng, SM, done=done):
            eng.wait_ge(SM["Y"], done)

        with nc.Block() as block:
            @block.tensor
            def _(eng):
                for fn in pl.q["te"]:
                    fn(eng, SM)

            @block.scalar
            def _(eng):
                for fn in pl.q["sc"]:
                    fn(eng, SM)

            @block.vector
            def _(eng):
                for fn in pl.q["v"]:
                    fn(eng, SM)

            @block.gpsimd
            def _(eng):
                for fn in pl.q["g"]:
                    fn(eng, SM)

            @block.sync
            def _(eng):
                for fn in pl.q["sy"]:
                    fn(eng, SM)

    return nc


def _host_inputs(inputs):
    g = {k: np.asarray(v) for k, v in inputs.items()}
    T = lambda a: np.ascontiguousarray(np.asarray(a, np.float32).T)
    C = lambda a: np.ascontiguousarray(np.asarray(a, np.float32))
    common = {
        "wqT": T(g["L0_attn_wq"]), "wkT": T(g["L0_attn_wk"]),
        "wvT": T(g["L0_attn_wv"]), "woT": T(g["L0_attn_wo"]),
        "bq": C(g["L0_attn_bq"]), "bk": C(g["L0_attn_bk"]),
        "bv": C(g["L0_attn_bv"]), "bo": C(g["L0_attn_bo"]),
        "pw1T": np.ascontiguousarray(np.asarray(g["L0_attn_pw1"], np.float32).transpose(0, 2, 1)),
        "pb1": C(g["L0_attn_pb1"]),
        "pw2T": np.ascontiguousarray(np.asarray(g["L0_attn_pw2"], np.float32).transpose(0, 2, 1)),
        "pb2": C(g["L0_attn_pb2"]),
        "sw1T": T(g["L0_attn_sw1"]), "sb1": C(g["L0_attn_sb1"]),
        "sw2T": T(g["L0_attn_sw2"]), "sb2": C(g["L0_attn_sb2"]),
        "sgT": T(g["L0_ffn_shared_gate"]), "suT": T(g["L0_ffn_shared_up"]),
        "sdT": T(g["L0_ffn_shared_down"]),
        "gw": C(g["L0_ffn_gate_w"]), "uw": C(g["L0_ffn_up_w"]),
        "dw": C(g["L0_ffn_down_w"]),
        "rwT": T(g["L0_ffn_router_w"]), "rb": C(g["L0_ffn_routing_bias"]),
        "w2qT": T(g["L1_attn_Wq"]), "w2kT": T(g["L1_attn_Wk"]),
        "w2vT": T(g["L1_attn_Wv"]), "w2oT": T(g["L1_attn_Wo"]),
        "W1": C(g["L1_ffn_W1"]), "W3": C(g["L1_ffn_W3"]), "W2": C(g["L1_ffn_W2"]),
        "b1": C(g["L1_ffn_b1"]), "b3": C(g["L1_ffn_b3"]), "b2": C(g["L1_ffn_b2"]),
        "Wg": C(g["L1_ffn_Wg"]), "bg": C(g["L1_ffn_bg"]),
        "Wf": C(g["L1_ffn_Wf"]), "bf": C(g["L1_ffn_bf"]),
        "identF": np.eye(128, dtype=np.float32),
        "identB": np.eye(128, dtype=ml_dtypes.bfloat16),
        "mask128": np.tril(np.ones((128, 128))).astype(ml_dtypes.bfloat16),
    }
    common["maskTu"] = np.triu(np.ones((128, 128))).astype(ml_dtypes.bfloat16)
    x = np.asarray(g["L0_attn_x"], np.float32)
    maps = []
    for c in range(8):
        m = dict(common)
        m["xT"] = np.ascontiguousarray(x[c].T)
        maps.append(m)
    return maps


def kernel(**inputs):
    nc = build_program()
    maps = _host_inputs(inputs)
    decl = set(DECLARED)
    maps = [{k: v for k, v in m.items() if k in decl} for m in maps]
    res = run_bass_kernel_spmd(nc, maps, core_ids=list(range(8)))
    out = np.stack([np.asarray(res.results[c]["y"], np.float32).T for c in range(8)])
    return np.ascontiguousarray(out)


# version 03
# speedup vs baseline: 1.1701x; 1.0605x over previous
import numpy as np
import ml_dtypes
import concourse.bass as bass
import concourse.mybir as mybir
from concourse.bass_utils import run_bass_kernel_spmd
from contextlib import ExitStack

F32 = mybir.dt.float32
BF16 = mybir.dt.bfloat16
AF = mybir.ActivationFunctionType
ALU = mybir.AluOpType
AX = mybir.AxisListType

D = 1024
S = 4096
NH = 16
HD = 64
FF = 2048
NE = 8
EPS_LN = 1e-5

SEMS = ("L", "Y", "P", "A", "V")
PHASES = ("pc", "A", "B", "C", "D", "E", "F")
DBG_EXPERTS = None


class Plan:
    def __init__(self):
        self.q = {k: [] for k in ("te", "sc", "v", "g", "sy")}
        self.n = dict.fromkeys(SEMS, 0)
        self.bank = {}

    def on(self, eng):
        def deco(fn):
            self.q[eng].append(fn)
        return deco

    def inc(self, s, by=1):
        self.n[s] += by
        return self.n[s]

    def free_at(self, res, sem, val):
        self.bank[res] = (sem, val)

    def take(self, res):
        w = self.bank.pop(res, None)
        return [w] if w else []

    def barrier(self):
        tgt = dict(self.n)
        for e in self.q:
            @self.on(e)
            def _(eng, SM, tgt=tgt):
                for s, v in tgt.items():
                    if v:
                        eng.wait_ge(SM[s], v)


def op(pl, ek, fn, wait=(), inc=None):
    tgt = pl.inc(inc, 1) if inc else None
    wait = tuple(wait)

    @pl.on(ek)
    def _(eng, SM, fn=fn, wait=wait, tgt=tgt, inc=inc):
        for s, v in wait:
            if v:
                eng.wait_ge(SM[s], v)
        ins = fn(eng)
        if tgt is not None:
            ins.then_inc(SM[inc], 1)
    return tgt


def dma(pl, ek, sem, out_ap, in_ap, wait=()):
    tgt = pl.inc(sem, 16)
    wait = tuple(wait)

    @pl.on(ek)
    def _(eng, SM, out_ap=out_ap, in_ap=in_ap, wait=wait, sem=sem):
        for s, v in wait:
            if v:
                eng.wait_ge(SM[s], v)
        eng.dma_start(out=out_ap, in_=in_ap).then_inc(SM[sem], 16)
    return tgt


def vchain(pl, steps, wait=()):
    prev = None
    first = True
    for fn in steps:
        w = list(wait) if first else [("V", prev)]
        prev = op(pl, "v", fn, wait=w, inc="V")
        first = False
    return prev


def mmg(pl, out_ap, ops, wait=(), inc=True, start=True, stop=True):
    tgt = pl.inc("P", 1) if inc else None
    wait = tuple(wait)
    ops = tuple(ops)

    @pl.on("te")
    def _(eng, SM, out_ap=out_ap, ops=ops, wait=wait, tgt=tgt, start=start, stop=stop):
        for s, v in wait:
            if v:
                eng.wait_ge(SM[s], v)
        n = len(ops)
        ins = None
        for i, (l, r) in enumerate(ops):
            ins = eng.matmul(out_ap, l, r, start=(start and i == 0),
                             stop=(stop and i == n - 1))
        if tgt is not None:
            ins.then_inc(SM["P"], 1)
    return tgt


DECLARED = []


def build_program(phases=None):
    global DECLARED
    if phases is None:
        phases = PHASES
    nc = bass.Bass("TRN2", target_bir_lowering=False, debug=False)
    DECLARED = []

    def inp(name, shape, dt=F32):
        DECLARED.append(name)
        return nc.dram_tensor(name, list(shape), dt, kind="ExternalInput").ap()

    need_l0ffn = ("A" in phases) or ("C" in phases)
    need_l0att = "B" in phases
    need_l1ffn = ("D" in phases) or ("F" in phases)
    need_l1att = "E" in phases

    xT = inp("xT", (D, S))
    if need_l0att:
        wqT = inp("wqT", (D, D)); wkT = inp("wkT", (D, D))
        wvT = inp("wvT", (D, D)); woT = inp("woT", (D, D))
        bq = inp("bq", (D,)); bk = inp("bk", (D,)); bv = inp("bv", (D,)); bo = inp("bo", (D,))
        pw1T = inp("pw1T", (3, D, 512)); pb1 = inp("pb1", (3, 512))
        pw2T = inp("pw2T", (3, 512, NH)); pb2 = inp("pb2", (3, NH))
        sw1T = inp("sw1T", (D, 512)); sb1 = inp("sb1", (512,))
        sw2T = inp("sw2T", (512, 3)); sb2 = inp("sb2", (3,))
    if need_l0ffn:
        sgT = inp("sgT", (D, FF)); suT = inp("suT", (D, FF)); sdT = inp("sdT", (FF, D))
        gw = inp("gw", (NE, D, FF)); uw = inp("uw", (NE, D, FF)); dw = inp("dw", (NE, FF, D))
        rwT = inp("rwT", (D, NE)); rb = inp("rb", (NE,))
    if need_l1att:
        w2qT = inp("w2qT", (D, D)); w2kT = inp("w2kT", (D, D))
        w2vT = inp("w2vT", (D, D)); w2oT = inp("w2oT", (D, D))
    if need_l1ffn:
        W1 = inp("W1", (NE, D, FF)); W3 = inp("W3", (NE, D, FF)); W2 = inp("W2", (NE, FF, D))
        b1 = inp("b1", (NE, FF)); b3 = inp("b3", (NE, FF)); b2 = inp("b2", (NE, D))
        Wg = inp("Wg", (D, NE)); bg = inp("bg", (NE,))
        Wf = inp("Wf", (D, D)); bf = inp("bf", (D,))
    identF = inp("identF", (128, 128))
    identB = inp("identB", (128, 128), BF16)
    maskTu = inp("maskTu", (128, 128), BF16)
    mask128 = inp("mask128", (128, 128), BF16)

    y = nc.dram_tensor("y", [D, S], F32, kind="ExternalOutput").ap()

    def scratch(name, shape, dt):
        return nc.dram_tensor(name, list(shape), dt, kind="Internal").ap()

    resid = scratch("resid", (D, S), F32)
    qTd = scratch("qTd", (D, S), BF16)
    kTd = scratch("kTd", (D, S), BF16)
    vTd = scratch("vTd", (D, S), BF16)
    aTd = scratch("aTd", (D, S), BF16)
    c_sg = scratch("c_sg", (D, FF), BF16); c_su = scratch("c_su", (D, FF), BF16)
    c_sd = scratch("c_sd", (FF, D), BF16)
    c_gw = scratch("c_gw", (NE, D, FF), BF16); c_uw = scratch("c_uw", (NE, D, FF), BF16)
    c_dw = scratch("c_dw", (NE, FF, D), BF16)
    c_W1 = scratch("c_W1", (NE, D, FF), BF16); c_W3 = scratch("c_W3", (NE, D, FF), BF16)
    c_W2 = scratch("c_W2", (NE, FF, D), BF16)
    c_Wf = scratch("c_Wf", (D, D), BF16)

    ctx = ExitStack()
    with ctx:
        def sb(name, shape, dt):
            return ctx.enter_context(nc.sbuf_tensor(name, list(shape), dt)).ap()

        xs = sb("xs", (128, 8192), F32)
        xh = sb("xh", (128, 8192), BF16)
        wsl = sb("wsl", (128, 49152), BF16)
        hb = sb("hb", (128, 12288), BF16)
        scr = sb("scr", (128, 2048), BF16)
        scf = sb("scf", (128, 512), F32)
        cmask = sb("cmask", (128, 2176), BF16)
        idF = sb("idF", (128, 128), F32)
        idB = sb("idB", (128, 128), BF16)
        onec_b = sb("onec_b", (128, 1), BF16)
        oner_b = sb("oner_b", (1, 512), BF16)
        oner_b128 = sb("oner_b128", (1, 128), BF16)
        one64_b = sb("one64_b", (64, 1), BF16)
        one_b = sb("one_b", (1, 1), BF16)
        smf = sb("smf", (128, 128), F32)
        smb = sb("smb", (128, 64), BF16)
        bcol = sb("bcol", (128, 64), F32)
        oner_f = sb("oner_f", (1, 128), F32)
        polsm = sb("polsm", (1, 64), F32)
        r_px = sb("r_px", (1, 128), F32)
        rw_sb = sb("rw_sb", (128, 64), BF16)
        rb_sb = sb("rb_sb", (8, 1), F32)
        bg_sb = sb("bg_sb", (8, 1), F32)
        b2_sb = sb("b2_sb", (8, 1024), BF16)
        wT_sb = sb("wT_sb", (8, 1024), BF16)
        polv = sb("polv", (16, 8), F32)
        pw2_sb = sb("pw2_sb", (128, 256), BF16)
        r_ms = sb("r_ms", (1, 1024), BF16)
        r_t0 = sb("r_t0", (1, 1024), BF16)
        rb_nm = sb("rb_nm", (1, 1024), BF16)
        rb_r = sb("rb_r", (1, 1024), BF16)
        w_row1 = sb("w_row1", (1, 1024), BF16)
        bst = sb("bst", (16, 128), BF16)
        b1col = sb("b1col", (128, 16), F32)
        b3col = sb("b3col", (128, 16), F32)
        bfcol = sb("bfcol", (128, 8), F32)
        pbcol = sb("pbcol", (128, 16), F32)
        pb2c = sb("pb2c", (16, 4), F32)
        r_pol = sb("r_pol", (1, 64), F32)
        r_hg = sb("r_hg", (1, 16), F32)
        r_sc = sb("r_sc", (1, 16), F32)
        kv_sb = sb("kv_sb", (64, 65), F32)
        ks_sb = sb("ks_sb", (64, 1), F32)
        at_sb = sb("at_sb", (128, 128), BF16)
        kt_sb = sb("kt_sb", (128, 64), BF16)
        vt_sb = sb("vt_sb", (128, 65), BF16)
        ps = [ctx.enter_context(nc.psum_tensor("ps%d" % i, [128, 512], F32)).ap()
              for i in range(6)]
        pt0 = ctx.enter_context(nc.psum_tensor("pt0", [128, 1024], BF16)).ap()
        pt1 = ctx.enter_context(nc.psum_tensor("pt1", [128, 1024], BF16)).ap()

        SM = {s_: ctx.enter_context(nc.semaphore(s_)) for s_ in SEMS}
        pl = Plan()

        def slotv(i, cols=16384):
            return wsl[:, i * 16384:i * 16384 + cols]

        def r3(ap_, cols):
            return ap_.rearrange("(c p) t -> p c t", p=128)

        cpool = sb("cpool", (128, 2), F32)
        nc.const_aps.register(F32, EPS_LN, cpool[:, 0:1])
        nc.const_aps.register(F32, 1e-6, cpool[:, 1:2])
        op(pl, "v", lambda e: e.memset(cpool[:, 0:1], EPS_LN))
        op(pl, "v", lambda e: e.memset(cpool[:, 1:2], 1e-6))
        op(pl, "v", lambda e: e.memset(onec_b, 1.0))
        op(pl, "v", lambda e: e.memset(oner_b, 1.0))
        op(pl, "v", lambda e: e.memset(oner_b128, 1.0))
        op(pl, "v", lambda e: e.memset(one64_b, 1.0))
        op(pl, "v", lambda e: e.memset(oner_f, 1.0))
        op(pl, "v", lambda e: e.memset(one_b, 1.0), inc="V")
        dma(pl, "sy", "Y", idF[:, :], identF[:, :])
        dma(pl, "sy", "Y", idB[:, :], identB[:, :])
        dma(pl, "sy", "Y", cmask[:, 0:128], maskTu[:, :])
        dma(pl, "sy", "Y", cmask[:, 2048:2176], mask128[:, :])
        pl.barrier()

        if "pc" in phases:
            pairs = []
            if need_l0ffn:
                pairs += [(sgT, c_sg), (suT, c_su), (sdT, c_sd)]
                for e_ in range(NE):
                    pairs += [(gw[e_], c_gw[e_]), (uw[e_], c_uw[e_]), (dw[e_], c_dw[e_])]
            if need_l1ffn:
                for e_ in range(NE):
                    pairs += [(W1[e_], c_W1[e_]), (W3[e_], c_W3[e_]), (W2[e_], c_W2[e_])]
                pairs += [(Wf, c_Wf)]
            nbuf = 0
            lastst = {}
            for (src, dst) in pairs:
                rows, cols = src.shape
                for r0 in range(0, rows, 128):
                    b = nbuf % 6
                    nbuf += 1
                    view = hb[:, b * 2048:b * 2048 + cols]
                    lt = dma(pl, "g", "L", view, src[r0:r0 + 128, :],
                             wait=([("Y", lastst[b])] if b in lastst else []))
                    lastst[b] = dma(pl, "sy", "Y", dst[r0:r0 + 128, :], view,
                                    wait=[("L", lt)])
            pl.barrier()

        def ln_stripe(src, t0, TS):
            nsub = TS // 512
            lt = dma(pl, "sy", "Y",
                     xs.rearrange("p (c t) -> p c t", t=1024)[:, :, 0:TS],
                     r3(src, S)[:, :, t0:t0 + TS],
                     wait=pl.take("xs") + pl.take("hb") + pl.take("scr") + pl.take("xh"))
            xs3 = xs.rearrange("p (c t) -> p c t", t=1024)[:, :, 0:TS]
            xb3 = hb.rearrange("p (c t) -> p c t", t=1024)[:, 0:8, 0:TS]
            sq3 = xh.rearrange("p (c t) -> p c t", t=1024)[:, 0:8, 0:TS]
            a_xb = op(pl, "sc", lambda e: e.copy(xb3, xs3), wait=[("Y", lt)], inc="A")
            v_sq = op(pl, "v", lambda e: e.tensor_mul(sq3, xs3, xs3),
                      wait=[("Y", lt)], inc="V")
            for sub in range(nsub):
                o = sub * 512
                p_m = mmg(pl, ps[4][0:1, :],
                          [(onec_b, hb[:, c * 1024 + o:c * 1024 + o + 512]) for c in range(8)],
                          wait=[("A", a_xb)] + pl.take("ps4"))
                p_s = mmg(pl, ps[5][0:1, :],
                          [(onec_b, xh[:, c * 1024 + o:c * 1024 + o + 512])
                           for c in range(8)],
                          wait=[("V", v_sq)] + pl.take("ps5"))
                v_r = vchain(pl, [
                    lambda e, o=o: e.tensor_scalar(
                        rb_nm[:, o:o + 512], ps[4][0:1, :], -1.0 / D, None, ALU.mult),
                    lambda e, o=o: e.tensor_scalar(
                        r_ms[:, o:o + 512], ps[5][0:1, :], 1.0 / D, None, ALU.mult),
                ], wait=[("P", p_m), ("P", p_s)])
                pl.free_at("ps4", "V", v_r)
                pl.free_at("ps5", "V", v_r)
            op(pl, "v", lambda e: e.tensor_mul(r_t0[:, 0:TS], rb_nm[:, 0:TS], rb_nm[:, 0:TS]),
               wait=[("V", pl.n["V"])])
            v_var = op(pl, "v", lambda e: e.tensor_sub(
                r_t0[:, 0:TS], r_ms[:, 0:TS], r_t0[:, 0:TS]),
                wait=[("V", pl.n["V"])], inc="V")
            a_std = op(pl, "sc", lambda e: e.activation(
                r_ms[:, 0:TS], r_t0[:, 0:TS], AF.Sqrt, bias=EPS_LN),
                wait=[("V", v_var)], inc="A")
            def _recip(e):
                with nc.allow_low_precision(reason="LN rstd rows in bf16; tolerance allows"):
                    return e.reciprocal(rb_r[:, 0:TS], r_ms[:, 0:TS])
            v_rows = vchain(pl, [_recip], wait=[("A", a_std)])
            for sub in range(nsub):
                o = sub * 512
                p_b1 = mmg(pl, ps[4][:, :], [(oner_b128, rb_nm[0:1, o:o + 512])],
                           wait=[("V", v_rows)] + pl.take("ps4"))
                p_b2 = mmg(pl, ps[5][:, :], [(oner_b128, rb_r[0:1, o:o + 512])],
                           wait=pl.take("ps5"))
                steps = [lambda e: e.tensor_copy(scf[:, 0:512], ps[5][:, :])]
                for c in range(8):
                    steps.append(lambda e, c=c, o=o: e.tensor_add(
                        ps[5][:, :],
                        xs[:, c * 1024 + o:c * 1024 + o + 512], ps[4][:, :]))
                    steps.append(lambda e, c=c, o=o: e.tensor_mul(
                        xh[:, c * 1024 + o:c * 1024 + o + 512],
                        ps[5][:, :], scf[:, 0:512]))
                vv = vchain(pl, steps, wait=[("P", p_b2)])
                pl.free_at("ps4", "V", vv)
                pl.free_at("ps5", "V", vv)
            return pl.n["V"]

        def store_resid(dst, t0, TS, wait_v):
            st = dma(pl, "sy", "Y", r3(dst, S)[:, :, t0:t0 + TS],
                     xs.rearrange("p (c t) -> p c t", t=1024)[:, :, 0:TS],
                     wait=[("V", wait_v)])
            pl.free_at("xs", "Y", st)
            return st

        slot_state = {0: None, 1: None, 2: None}
        slot_freeP = {0: 0, 1: 0, 2: 0}

        def load_slot(i, key, src_ap, rows, cols):
            if slot_state[i] == key:
                return 0
            slot_state[i] = key
            view = wsl[:, i * 16384:i * 16384 + (rows // 128) * cols]
            return dma(pl, "g", "L", view.rearrange("p (c t) -> p c t", t=cols),
                       r3(src_ap, cols), wait=[("P", slot_freeP[i])])

        def invalidate_slots():
            for i in range(3):
                slot_state[i] = None

        def ffn_l0(src, dst, load_router):
            TS = 1024
            if load_router:
                dma(pl, "g", "L", rw_sb.rearrange("p (c e) -> p c e", e=8), r3(rwT, 8))
                dma(pl, "g", "L", rb_sb, rb.rearrange("(p o) -> p o", o=1))
            for st_i in range(S // TS):
                t0 = st_i * TS
                v_ln = ln_stripe(src, t0, TS)

                def bc(small, like):
                    return bass.broadcast_tensor_aps(like, small)[1]

                ptw_prev = 0
                for sub in range(2):
                    o = sub * 512
                    pr = mmg(pl, ps[4][0:8, :],
                             [(rw_sb[:, c * 8:(c + 1) * 8],
                               xh[:, c * 1024 + o:c * 1024 + o + 512]) for c in range(8)],
                             wait=[("V", v_ln), ("L", pl.n["L"])] + pl.take("ps4"))
                    a_pr = op(pl, "sc", lambda e: e.activation(
                        scf[0:8, 0:512], ps[4][0:8, :], AF.Sigmoid,
                        bias=rb_sb[:, 0:1]),
                        wait=[("P", pr)] + pl.take("scf_pr"), inc="A")
                    pl.free_at("ps4", "A", a_pr)
                    for j4 in range(4):
                        j = sub * 4 + j4
                        half = (j % 2) * 128
                        ptr = op(pl, "te", lambda e, j4=j4: e.transpose(
                            ps[5][:, (j4 % 2) * 8:(j4 % 2) * 8 + 8],
                            scf[0:8, j4 * 128:(j4 + 1) * 128], idF[0:8, 0:8]),
                            wait=[("A", a_pr)] +
                                 (pl.take("ps5") if j < 2 else [("V", pl.n["V"])]),
                            inc="P")
                        pv = smf[:, 8:16]
                        v_w = vchain(pl, [
                            lambda e, j4=j4: e.tensor_copy(
                                pv, ps[5][:, (j4 % 2) * 8:(j4 % 2) * 8 + 8]),
                            lambda e: e.reduce_max(smf[:, 0:1], pv, AX.X),
                            lambda e: e.tensor_tensor(
                                smf[:, 16:24], pv, bc(smf[:, 0:1], pv), ALU.is_ge),
                            lambda e: e.scalar_tensor_tensor(
                                smf[:, 24:32], smf[:, 16:24], -1e30, pv, ALU.mult, ALU.add),
                            lambda e: e.reduce_max(smf[:, 1:2], smf[:, 24:32], AX.X),
                            lambda e: e.tensor_add(smf[:, 2:3], smf[:, 0:1], smf[:, 1:2]),
                            lambda e: e.reciprocal(smf[:, 3:4], smf[:, 2:3]),
                            lambda e: e.tensor_tensor(
                                smf[:, 16:24], pv, bc(smf[:, 1:2], pv), ALU.is_ge),
                            lambda e: e.tensor_mul(smf[:, 24:32], pv, smf[:, 16:24]),
                            lambda e: e.tensor_tensor(
                                smf[:, 24:32], smf[:, 24:32], bc(smf[:, 3:4], pv), ALU.mult),
                            lambda e: e.tensor_scalar(
                                smb[:, 0:8], smf[:, 24:32], 0.5, None, ALU.mult),
                        ], wait=[("P", ptr)] + ([("P", ptw_prev)] if ptw_prev else []))
                        ptw = op(pl, "te", lambda e, half=half: e.transpose(
                            pt0[0:8, half:half + 128], smb[:, 0:8], idB),
                            wait=[("V", v_w)] + pl.take("pt0h%d" % (j % 2)), inc="P")
                        a_w = op(pl, "sc", lambda e, j=j, half=half: e.copy(
                            wT_sb[0:8, j * 128:(j + 1) * 128], pt0[0:8, half:half + 128]),
                            wait=[("P", ptw)], inc="A")
                        pl.free_at("pt0h%d" % (j % 2), "A", a_w)
                        ptw_prev = ptw
                    pl.free_at("scf_pr", "P", ptw_prev)
                pl.free_at("ps5", "V", pl.n["V"])
                a_wlast = pl.n["A"]
                order = list(range(9)) if st_i % 2 == 0 else list(range(8, -1, -1))
                if DBG_EXPERTS is not None:
                    order = [e_ for e_ in order if e_ in DBG_EXPERTS]
                for e_i in order:
                    if e_i == 8:
                        l_g = load_slot(0, "sg", c_sg, D, FF)
                        l_u = load_slot(1, "su", c_su, D, FF)
                        l_d = load_slot(2, "sd", c_sd, FF, D)
                        a_wbc = 0
                    else:
                        l_wr = dma(pl, "g", "L", w_row1[0:1, :], wT_sb[e_i:e_i + 1, :],
                                   wait=[("A", a_wlast)] + pl.take("w_row1"))
                        l_g = load_slot(0, ("g", e_i), c_gw[e_i], D, FF)
                        l_u = load_slot(1, ("u", e_i), c_uw[e_i], D, FF)
                        l_d = load_slot(2, ("d", e_i), c_dw[e_i], FF, D)
                        mmg(pl, ps[4][:, :], [(oner_b128, w_row1[0:1, 0:512])],
                            wait=[("L", l_wr)] + pl.take("ps4"), inc=False)
                        pw = mmg(pl, ps[5][:, :], [(oner_b128, w_row1[0:1, 512:1024])],
                                 wait=pl.take("ps5"))
                        pl.free_at("w_row1", "P", pw)
                        op(pl, "sc", lambda e: e.copy(scr[:, 0:512], ps[4][:, :]),
                           wait=[("P", pw)] + pl.take("wbc"))
                        a_wbc = op(pl, "sc", lambda e: e.copy(scr[:, 512:1024], ps[5][:, :]),
                                   inc="A")
                        pl.free_at("ps4", "A", a_wbc)
                        pl.free_at("ps5", "A", a_wbc)
                    for sub in range(2):
                        o = sub * 512
                        for hc in range(16):
                            bgi = "ps%d" % (hc % 2)
                            bui = "ps%d" % (2 + hc % 2)
                            sgr = hb[:, 10240 + (hc % 2) * 512:10240 + (hc % 2) * 512 + 512]
                            usr = hb[:, 8192 + (hc % 2) * 512:8192 + (hc % 2) * 512 + 512]
                            p_g = mmg(pl, ps[hc % 2],
                                      [(slotv(0)[:, c * 2048 + hc * 128:c * 2048 + hc * 128 + 128],
                                        xh[:, c * 1024 + o:c * 1024 + o + 512]) for c in range(8)],
                                      wait=([("L", l_g), ("V", v_ln)] if hc == 0 and sub == 0 else [])
                                      + pl.take(bgi))
                            p_u = mmg(pl, ps[2 + hc % 2],
                                      [(slotv(1)[:, c * 2048 + hc * 128:c * 2048 + hc * 128 + 128],
                                        xh[:, c * 1024 + o:c * 1024 + o + 512]) for c in range(8)],
                                      wait=([("L", l_u)] if hc == 0 and sub == 0 else [])
                                      + pl.take(bui))
                            ag = op(pl, "sc", lambda e, b_=ps[hc % 2], sgr=sgr: e.activation(
                                sgr, b_, AF.Silu),
                                wait=[("P", p_g)] + pl.take("sgr%d" % (hc % 2)), inc="A")
                            pl.free_at(bgi, "A", ag)
                            if e_i != 8:
                                au = op(pl, "v", lambda e, b_=ps[2 + hc % 2], usr=usr, o=o:
                                        e.scalar_tensor_tensor(
                                            usr, b_, 1.0, scr[:, o:o + 512],
                                            ALU.mult, ALU.mult),
                                        wait=[("P", p_u)] +
                                             ([("A", a_wbc)] if hc == 0 and sub == 0 else []),
                                        inc="V")
                            else:
                                au = op(pl, "v", lambda e, b_=ps[2 + hc % 2], usr=usr:
                                        e.tensor_scalar(usr, b_, 0.5, None, ALU.mult),
                                        wait=[("P", p_u)], inc="V")
                            pl.free_at(bui, "V", au)
                            vh_ = op(pl, "v", lambda e, hc=hc, sgr=sgr, usr=usr: e.tensor_mul(
                                hb[:, hc * 512:(hc + 1) * 512], sgr, usr),
                                wait=[("A", ag)], inc="V")
                            pl.free_at("sgr%d" % (hc % 2), "V", vh_)
                        v_h = pl.n["V"]
                        for oc in range(8):
                            bdi = "ps%d" % (4 + oc % 2)
                            p_d = mmg(pl, ps[4 + oc % 2],
                                      [(slotv(2)[:, ic * 1024 + oc * 128:ic * 1024 + oc * 128 + 128],
                                        hb[:, ic * 512:(ic + 1) * 512]) for ic in range(16)],
                                      wait=([("L", l_d), ("V", v_h)] if oc == 0 else [])
                                      + pl.take(bdi))
                            vd = op(pl, "v", lambda e, oc=oc, o=o, b_=ps[4 + oc % 2]:
                                    e.tensor_add(
                                        xs[:, oc * 1024 + o:oc * 1024 + o + 512],
                                        xs[:, oc * 1024 + o:oc * 1024 + o + 512], b_),
                                    wait=[("P", p_d)], inc="V")
                            pl.free_at(bdi, "V", vd)
                        if sub == 1:
                            slot_freeP[0] = slot_freeP[1] = pl.n["P"] - 8
                    slot_freeP[2] = pl.n["P"]
                    pl.free_at("wbc", "V", pl.n["V"])
                st = store_resid(dst, t0, TS, pl.n["V"])
                pl.free_at("hb", "P", pl.n["P"])
                pl.free_at("scr", "V", pl.n["V"])
                pl.free_at("xh", "P", pl.n["P"])
                pl.barrier()

        def bcx(small, like):
            return bass.broadcast_tensor_aps(like, small)[1]

        def attn_l0(src, dst):
            TS = 1024
            for i in range(3):
                slot_state[i] = None
            l_w = 0
            for (wi, (w_, col)) in enumerate([(wqT, 0), (wkT, 8192),
                                              (wvT, 16384), (woT, 24576)]):
                l_w = dma(pl, "g", "L",
                          wsl[:, col:col + 8192].rearrange("p (c t) -> p c t", t=1024),
                          r3(w_, D), wait=[("P", slot_freeP[0])] if wi == 0 else [])
            for (bi, b_) in enumerate([bq, bk, bv, bo]):
                for oc in range(8):
                    dma(pl, "g", "L", bcol[:, bi * 8 + oc:bi * 8 + oc + 1],
                        b_[oc * 128:(oc + 1) * 128].rearrange("(p o) -> p o", o=1))
            l_bias = pl.n["L"]
            for st_i in range(4):
                t0 = st_i * TS
                v_ln = ln_stripe(src, t0, TS)
                steps = []
                for c in range(8):
                    if st_i == 0:
                        steps.append(lambda e, c=c: e.reduce_sum(
                            smf[:, 32 + c:33 + c], xh[:, c * 1024:c * 1024 + TS], AX.X))
                    else:
                        steps.append(lambda e, c=c: e.reduce_sum(
                            smf[:, 40:41], xh[:, c * 1024:c * 1024 + TS], AX.X))
                        steps.append(lambda e, c=c: e.tensor_add(
                            smf[:, 32 + c:33 + c], smf[:, 32 + c:33 + c], smf[:, 40:41]))
                v_xm = vchain(pl, steps, wait=[("V", v_ln)])
                for (pi, (col, dstT)) in enumerate([(0, qTd), (8192, kTd), (16384, vTd)]):
                    y_prev = pl.take("hb_proj")
                    a_last = 0
                    for oc in range(8):
                        for sub in range(2):
                            o = sub * 512
                            bk_ = "ps%d" % ((oc * 2 + sub) % 4)
                            p_p = mmg(pl, ps[(oc * 2 + sub) % 4],
                                      [(wsl[:, col + c * 1024 + oc * 128:col + c * 1024 + oc * 128 + 128],
                                        xh[:, c * 1024 + o:c * 1024 + o + 512]) for c in range(8)],
                                      wait=([("L", l_w), ("V", v_ln)] if oc == 0 and sub == 0 else [])
                                      + pl.take(bk_))
                            a_last = op(pl, "sc", lambda e, oc=oc, o=o, pi=pi,
                                        b_=ps[(oc * 2 + sub) % 4]: e.activation(
                                            hb[:, oc * 1024 + o:oc * 1024 + o + 512], b_,
                                            AF.Identity, bias=bcol[:, pi * 8 + oc:pi * 8 + oc + 1]),
                                        wait=[("P", p_p), ("L", l_bias)] + (y_prev if oc == 0 and sub == 0 else []),
                                        inc="A")
                            pl.free_at(bk_, "A", a_last)
                    st = dma(pl, "sy", "Y",
                             r3(dstT, S)[:, :, t0:t0 + TS],
                             hb.rearrange("p (c t) -> p c t", t=1024)[:, 0:8, 0:TS],
                             wait=[("A", a_last)])
                    pl.free_at("hb_proj", "Y", st)
                pl.free_at("hb", "Y", pl.n["Y"])
                pl.barrier()
            def pol_w(p_i):
                return hb[:, p_i * 4096:(p_i + 1) * 4096] if p_i < 3 else xh[:, 0:4096]

            lw = 0
            for p_i in range(3):
                lw = dma(pl, "g", "L",
                         hb[:, p_i * 4096:(p_i + 1) * 4096].rearrange("p (c t) -> p c t", t=512),
                         r3(pw1T[p_i], 512))
            lw = dma(pl, "g", "L",
                     xh[:, 0:4096].rearrange("p (c t) -> p c t", t=512), r3(sw1T, 512))
            for p_i in range(3):
                lw = dma(pl, "g", "L",
                         pw2_sb[:, p_i * 64:p_i * 64 + 64].rearrange("p (c t) -> p c t", t=16),
                         r3(pw2T[p_i], 16))
            lw = dma(pl, "g", "L",
                     pw2_sb[:, 192:204].rearrange("p (c t) -> p c t", t=3),
                     r3(sw2T, 3))
            a_pb = 0
            pt_b = 0
            for (p_i, brow) in enumerate([pb1[0], pb1[1], pb1[2], sb1]):
                lb = dma(pl, "g", "L", bst[0:4, :],
                         brow.rearrange("(c t) -> c t", t=128),
                         wait=[("P", pt_b)] if pt_b else [])
                pt_b = op(pl, "te", lambda e: e.transpose(
                    pt0[0:128, 0:4], bst[0:4, :], idB[0:4, 0:4]),
                    wait=[("L", lb), ("A", a_pb)], inc="P")
                a_pb = op(pl, "sc", lambda e, p_i=p_i: e.copy(
                    pbcol[:, p_i * 4:(p_i + 1) * 4], pt0[0:128, 0:4]),
                    wait=[("P", pt_b)], inc="A")
            for p_i in range(3):
                dma(pl, "g", "L", pb2c[0:16, p_i:p_i + 1],
                    pb2[p_i].rearrange("(h o) -> h o", o=1))
            dma(pl, "g", "L", pb2c[0:3, 3:4], sb2.rearrange("(h o) -> h o", o=1))
            l_pol = pl.n["L"]
            v_xm2 = vchain(pl, [
                lambda e: e.tensor_scalar(smf[:, 32:40], smf[:, 32:40], 1.0 / S, None, ALU.mult),
                lambda e: e.tensor_copy(smb[:, 8:16], smf[:, 32:40]),
            ], wait=[("V", pl.n["V"])])
            a_hid = 0
            for p_i in range(4):
                pg = 0
                wv = pol_w(p_i)
                for cc in range(4):
                    pg = mmg(pl, ps[4][:, cc:cc + 1],
                             [(wv[:, c * 512 + cc * 128:c * 512 + cc * 128 + 128],
                               smb[:, 8 + c:9 + c]) for c in range(8)],
                             wait=([("L", l_pol), ("V", v_xm2)] + pl.take("ps4"))
                             if cc == 0 else [])
                for cc in range(4):
                    a_hid = op(pl, "sc", lambda e, p_i=p_i, cc=cc: e.activation(
                        smb[:, 16 + p_i * 4 + cc:17 + p_i * 4 + cc], ps[4][:, cc:cc + 1],
                        AF.Gelu, bias=pbcol[:, p_i * 4 + cc:p_i * 4 + cc + 1]),
                        wait=[("P", pg)], inc="A")
                pl.free_at("ps4", "A", a_hid)
            pg2 = 0
            for p_i in range(3):
                pg2 = mmg(pl, ps[5][0:16, p_i:p_i + 1],
                          [(pw2_sb[:, p_i * 64 + cc * 16:p_i * 64 + cc * 16 + 16],
                            smb[:, 16 + p_i * 4 + cc:17 + p_i * 4 + cc]) for cc in range(4)],
                          wait=([("A", a_hid)] + pl.take("ps5")) if p_i == 0 else [])
            pg2 = mmg(pl, ps[5][0:3, 3:4],
                      [(pw2_sb[:, 192 + cc * 3:192 + cc * 3 + 3],
                        smb[:, 28 + cc:29 + cc]) for cc in range(4)])
            v_pol = op(pl, "v", lambda e: e.tensor_add(polv[:, 0:4], ps[5][0:16, 0:4], pb2c),
                       wait=[("P", pg2)], inc="V")
            pl.free_at("ps5", "V", v_pol)
            l_r1 = dma(pl, "g", "L",
                       r_pol[0:1, 0:48].rearrange("o (r p) -> o r p", p=3),
                       polv[0:16, 0:3], wait=[("V", v_pol)])
            l_r2 = dma(pl, "g", "L", r_pol[0:1, 48:51], polv[0:3, 3:4],
                       wait=[("V", v_pol)])
            rows = [r_pol[0:1, 0:48].rearrange("o (r p) -> o p r", p=3)
                    [:, p_i:p_i + 1, :].rearrange("o p r -> o (p r)")
                    for p_i in range(3)] + [r_pol[0:1, 48:51]]
            outsm = [polsm[0:1, p_i * 16:p_i * 16 + 16] for p_i in range(3)] + \
                    [polsm[0:1, 48:51]]
            for i_r, (rv, ov) in enumerate(zip(rows, outsm)):
                n_ = rv.shape[-1]
                v_m = vchain(pl, [
                    lambda e, rv=rv: e.reduce_max(smf[0:1, 64:65], rv, AX.X),
                    lambda e: e.tensor_scalar(smf[0:1, 65:66], smf[0:1, 64:65],
                                              -1.0, None, ALU.mult),
                ], wait=[("L", l_r1), ("L", l_r2)] if i_r == 0 else [("A", pl.n["A"])])
                a_e = op(pl, "sc", lambda e, rv=rv, n_=n_: e.activation(
                    smf[0:1, 66:66 + n_], rv, AF.Exp, bias=smf[0:1, 65:66],
                    accum_out=smf[0:1, 82:83]),
                    wait=[("V", v_m)], inc="A")
                vchain(pl, [
                    lambda e: e.reciprocal(smf[0:1, 83:84], smf[0:1, 82:83]),
                    lambda e, ov=ov, n_=n_: e.tensor_tensor(
                        ov, smf[0:1, 66:66 + n_], bcx(smf[0:1, 83:84], smf[0:1, 66:66 + n_]),
                        ALU.mult),
                ], wait=[("A", a_e)])
            v_hg = vchain(pl, [
                lambda e: e.tensor_tensor(r_hg[:, :], polsm[0:1, 0:16],
                                          bcx(polsm[0:1, 48:49], polsm[0:1, 0:16]), ALU.mult),
                lambda e: e.tensor_tensor(smf[0:1, 64:80], polsm[0:1, 16:32],
                                          bcx(polsm[0:1, 49:50], polsm[0:1, 16:32]), ALU.mult),
                lambda e: e.tensor_add(r_hg[:, :], r_hg[:, :], smf[0:1, 64:80]),
                lambda e: e.tensor_tensor(smf[0:1, 64:80], polsm[0:1, 32:48],
                                          bcx(polsm[0:1, 50:51], polsm[0:1, 32:48]), ALU.mult),
                lambda e: e.tensor_add(r_hg[:, :], r_hg[:, :], smf[0:1, 64:80]),
                lambda e: e.tensor_scalar(r_sc[:, :], r_hg[:, :], 0.125, None, ALU.mult),
                lambda e: e.tensor_copy(r_px[0:1, 0:16], r_sc[:, :]),
            ], wait=[("V", pl.n["V"])])
            p_g8 = mmg(pl, ps[4][:, 0:16], [(oner_f, r_px[0:1, 0:16])],
                       wait=[("V", v_hg)] + pl.take("ps4"))
            v_g8 = op(pl, "v", lambda e: e.tensor_copy(smf[:, 48:64], ps[4][:, 0:16]),
                      wait=[("P", p_g8)], inc="V")
            pl.free_at("ps4", "V", v_g8)
            pl.barrier()
            QH = wsl[0:64, 32768:36864]
            KH = wsl[0:64, 36864:40960]
            QSQ = wsl[64:128, 32768:36864]
            KSQ = wsl[64:128, 36864:40960]
            VH = wsl[0:64, 40960:45056]
            AH = wsl[64:128, 40960:45056]
            VT = wsl[:, 45056:47136]
            VT3 = VT.rearrange("p (c t) -> p c t", t=65)
            v_one = op(pl, "v", lambda e: e.memset(VT3[:, :, 64:65], 1.0), inc="V")
            NPB = 6
            for h in range(NH):
                lq = dma(pl, "sy", "Y", QH, qTd[64 * h:64 * h + 64, :],
                         wait=pl.take("slot2"))
                lk = dma(pl, "sy", "Y", KH, kTd[64 * h:64 * h + 64, :])
                lv = dma(pl, "sy", "Y", VH, vTd[64 * h:64 * h + 64, :])
                v_sq2 = vchain(pl, [
                    lambda e: e.tensor_mul(QSQ, QH, QH),
                    lambda e: e.tensor_mul(KSQ, KH, KH),
                ], wait=[("Y", lv)])
                for jb in range(4):
                    pt_v = 0
                    for j2 in range(8):
                        j = jb * 8 + j2
                        pt_v = op(pl, "te", lambda e, j=j, j2=j2: e.transpose(
                            pt1[:, j2 * 64:j2 * 64 + 64],
                            VH[:, j * 128:(j + 1) * 128], idB[0:64, 0:64]),
                            wait=([("Y", lv)] if j == 0 else []) +
                                 (pl.take("pt1") if j2 == 0 else []),
                            inc="P" if j2 == 7 else None)
                    v_vt = op(pl, "v", lambda e, jb=jb: e.tensor_copy(
                        VT3[:, jb * 8:(jb + 1) * 8, 0:64],
                        pt1[:, 0:512].rearrange("p (c t) -> p c t", t=64)),
                        wait=[("P", pt_v)] + ([("V", v_one)] if h == 0 and jb == 0 else []),
                        inc="V")
                    pl.free_at("pt1", "V", v_vt)
                for (sqv, colo) in ((KSQ, 52), (QSQ, 53)):
                    p_kn = 0
                    for j in range(32):
                        p_kn = mmg(pl, ps[4][:, j:j + 1],
                                   [(sqv[:, j * 128:(j + 1) * 128], onec_b[64:128, :])],
                                   wait=([("V", v_sq2)] + pl.take("ps4")) if j == 0 else [],
                                   inc=(j == 31))
                    v_kn = vchain(pl, [
                        lambda e: e.tensor_copy(smf[:, 64:96], ps[4][:, 0:32]),
                        lambda e: e.reduce_max(smf[:, 96:97], smf[:, 64:96], AX.X),
                    ], wait=[("P", p_kn)])
                    pl.free_at("ps4", "V", v_kn)
                    pt_k = op(pl, "te", lambda e: e.transpose(
                        ps[5][0:1, 0:128], smf[:, 96:97], idF),
                        wait=[("V", v_kn)] + pl.take("ps5"), inc="P")
                    v_km = vchain(pl, [
                        lambda e: e.tensor_copy(r_px[0:1, 0:128], ps[5][0:1, 0:128]),
                        lambda e, colo=colo: e.reduce_max(
                            polsm[0:1, colo:colo + 1], r_px[0:1, 0:128], AX.X),
                    ], wait=[("P", pt_k)])
                    pl.free_at("ps5", "V", v_km)
                a_sk = op(pl, "sc", lambda e: e.activation(
                    polsm[0:1, 55:57], polsm[0:1, 52:54], AF.Sqrt),
                    wait=[("V", v_km)], inc="A")
                v_sk = vchain(pl, [
                    lambda e: e.tensor_mul(polsm[0:1, 57:58], polsm[0:1, 55:56],
                                           polsm[0:1, 56:57]),
                    lambda e, h=h: e.tensor_mul(polsm[0:1, 57:58], polsm[0:1, 57:58],
                                                r_sc[0:1, h:h + 1]),
                    lambda e: e.tensor_scalar(polsm[0:1, 57:58], polsm[0:1, 57:58],
                                              -1.0, None, ALU.mult),
                ], wait=[("A", a_sk)])
                p_nb = mmg(pl, ps[5][:, 0:1], [(oner_f, polsm[0:1, 57:58])],
                           wait=[("V", v_sk)] + pl.take("ps5"))
                v_nbh = op(pl, "v", lambda e: e.tensor_copy(smf[:, 99:100], ps[5][:, 0:1]),
                           wait=[("P", p_nb)], inc="V")
                pl.free_at("ps5", "V", v_nbh)
                for qi in range(32):
                    nkc = qi + 1
                    ng = (nkc + 3) // 4
                    a_ex = {}

                    def emit_QKTg(g):
                        n_in_g = min(4, nkc - 4 * g)
                        p_s = 0
                        for j in range(n_in_g):
                            kc = 4 * g + j
                            w_ = []
                            if j == 0:
                                w_ += pl.take("psB%d" % (g % 2))
                            if qi == 0 and kc == 0:
                                w_ += [("Y", lk), ("V", v_nbh)]
                            p_s = mmg(pl, ps[g % 2][:, j * 128:j * 128 + 128],
                                      [(KH[:, kc * 128:(kc + 1) * 128],
                                        QH[:, qi * 128:(qi + 1) * 128])],
                                      wait=w_, inc=(j == n_in_g - 1))
                        return p_s

                    def emit_EXPg(g, p_s):
                        n_in_g = min(4, nkc - 4 * g)
                        pb = "prb%d" % (g % 2)
                        a_ = op(pl, "sc", lambda e, g=g, n_in_g=n_in_g: e.activation(
                            scr[:, (g % 2) * 512:(g % 2) * 512 + n_in_g * 128],
                            ps[g % 2][:, 0:n_in_g * 128], AF.Exp,
                            bias=smf[:, 99:100], scale=smf[:, 48 + h:49 + h]),
                            wait=[("P", p_s)] + pl.take(pb), inc="A")
                        pl.free_at("psB%d" % (g % 2), "A", a_)
                        if 4 * g <= qi < 4 * g + 4:
                            jd = qi - 4 * g
                            v_msk = op(pl, "v", lambda e, g=g, jd=jd: e.tensor_mul(
                                scr[:, (g % 2) * 512 + jd * 128:(g % 2) * 512 + jd * 128 + 128],
                                scr[:, (g % 2) * 512 + jd * 128:(g % 2) * 512 + jd * 128 + 128],
                                cmask[:, 0:128]),
                                wait=[("A", a_)], inc="V")
                            a_ex[g] = ("V", v_msk)
                        else:
                            a_ex[g] = ("A", a_)

                    def emit_PVg(g):
                        n_in_g = min(4, nkc - 4 * g)
                        p_pv = mmg(pl, ps[2][:, 0:65],
                                   [(scr[:, (g % 2) * 512 + j * 128:(g % 2) * 512 + j * 128 + 128],
                                     VT[:, (4 * g + j) * 65:(4 * g + j) * 65 + 65])
                                    for j in range(n_in_g)],
                                   wait=[a_ex[g]] + (pl.take("ps2") if g == 0 else []),
                                   start=(g == 0), stop=(g == ng - 1))
                        pl.free_at("prb%d" % (g % 2), "P", p_pv)

                    for v_s in range(ng + 1):
                        p_s = emit_QKTg(v_s) if v_s < ng else None
                        if v_s >= 1:
                            emit_PVg(v_s - 1)
                        if v_s < ng:
                            emit_EXPg(v_s, p_s)
                    v_rs = vchain(pl, [
                        lambda e: e.reciprocal(smf[:, 113:114], ps[2][:, 64:65]),
                        lambda e: e.tensor_tensor(
                            smb[:, 0:64], ps[2][:, 0:64],
                            bcx(smf[:, 113:114], ps[2][:, 0:64]), ALU.mult),
                    ], wait=[("P", pl.n["P"])])
                    pl.free_at("ps2", "V", v_rs)
                    pt_o = op(pl, "te", lambda e, qi=qi: e.transpose(
                        pt1[0:64, (qi % 8) * 128:(qi % 8) * 128 + 128],
                        smb[:, 0:64], idB),
                        wait=[("V", v_rs)] + (pl.take("pt1") if qi % 8 == 0 else []),
                        inc="P")
                    if qi % 8 == 7:
                        v_oe = op(pl, "v", lambda e, qi=qi: e.tensor_copy(
                            AH[:, (qi - 7) * 128:(qi + 1) * 128], pt1[0:64, 0:1024]),
                            wait=[("P", pt_o)], inc="V")
                        pl.free_at("pt1", "V", v_oe)
                st_a = dma(pl, "sy", "Y", aTd[64 * h:64 * h + 64, :], AH,
                           wait=[("V", pl.n["V"])])
                pl.free_at("slot2", "Y", st_a)
            pl.barrier()
            for st_i in range(4):
                t0 = st_i * TS
                lx = dma(pl, "sy", "Y",
                         xs.rearrange("p (c t) -> p c t", t=1024)[:, :, 0:TS],
                         r3(src, S)[:, :, t0:t0 + TS], wait=pl.take("xs"))
                la = dma(pl, "sy", "Y",
                         hb.rearrange("p (c t) -> p c t", t=1024)[:, 0:8, 0:TS],
                         r3(aTd, S)[:, :, t0:t0 + TS], wait=pl.take("hb"))
                v_last = 0
                for oc in range(8):
                    for sub in range(2):
                        o = sub * 512
                        bk_ = "ps%d" % ((oc * 2 + sub) % 4)
                        p_p = mmg(pl, ps[(oc * 2 + sub) % 4],
                                  [(wsl[:, 16384 + 8192 + c * 1024 + oc * 128:
                                        16384 + 8192 + c * 1024 + oc * 128 + 128],
                                    hb[:, c * 1024 + o:c * 1024 + o + 512]) for c in range(8)],
                                  wait=([("Y", la), ("Y", lx)] if oc == 0 and sub == 0 else [])
                                  + pl.take(bk_))
                        v_last = op(pl, "v", lambda e, oc=oc, o=o,
                                    b_=ps[(oc * 2 + sub) % 4]: e.scalar_tensor_tensor(
                                        xs[:, oc * 1024 + o:oc * 1024 + o + 512], b_,
                                        bcol[:, 24 + oc:25 + oc],
                                        xs[:, oc * 1024 + o:oc * 1024 + o + 512],
                                        ALU.add, ALU.add),
                                    wait=[("P", p_p)], inc="V")
                        pl.free_at(bk_, "V", v_last)
                st = store_resid(dst, t0, TS, v_last)
                pl.free_at("hb", "Y", st)
                pl.barrier()
            for i in range(3):
                slot_state[i] = None
                slot_freeP[i] = pl.n["P"]

        def ffn_l1(src, dst, load_gw):
            TS = 512
            if load_gw:
                dma(pl, "g", "L", rw_sb.rearrange("p (c e) -> p c e", e=8), r3(Wg, 8))
                dma(pl, "g", "L", bg_sb, bg.rearrange("(p o) -> p o", o=1))
                dma(pl, "g", "L", b2_sb, b2)
                lb = dma(pl, "g", "L", bst[0:8, :], bf.rearrange("(c t) -> c t", t=128),
                         wait=pl.take("bst"))
                pt_b = op(pl, "te", lambda e: e.transpose(
                    pt0[0:128, 0:8], bst[0:8, :], idB[0:8, 0:8]),
                    wait=[("L", lb)] + pl.take("pt0f"), inc="P")
                v_bf = op(pl, "v", lambda e: e.tensor_scalar(
                    bfcol[:, 0:8], pt0[0:128, 0:8], 0.5, None, ALU.mult),
                    wait=[("P", pt_b)], inc="V")
                pl.free_at("bst", "P", pt_b)
                pl.free_at("pt0f", "V", v_bf)
            for st_i in range(8):
                t0 = st_i * TS
                v_ln = ln_stripe(src, t0, TS)
                v_mx = op(pl, "v", lambda e: e.memset(
                    xs.rearrange("p (c t) -> p c t", t=1024)[:, :, 512:1024], 0.0),
                    wait=[("V", v_ln)], inc="V")
                pr = mmg(pl, ps[4][0:8, :],
                         [(rw_sb[:, c * 8:(c + 1) * 8],
                           xh[:, c * 1024:c * 1024 + 512]) for c in range(8)],
                         wait=[("V", v_ln), ("L", pl.n["L"])] + pl.take("ps4"))
                v_lg = op(pl, "v", lambda e: e.tensor_scalar(
                    scf[0:8, 0:512], ps[4][0:8, :], bg_sb[:, 0:1], None, ALU.add),
                    wait=[("P", pr)] + pl.take("scf_pr"), inc="V")
                pl.free_at("ps4", "V", v_lg)
                ptw_prev = 0
                for j in range(4):
                    half = (j % 2) * 128
                    ptr = op(pl, "te", lambda e, j=j: e.transpose(
                        ps[5][:, (j % 2) * 8:(j % 2) * 8 + 8],
                        scf[0:8, j * 128:(j + 1) * 128], idF[0:8, 0:8]),
                        wait=[("V", v_lg)] +
                             (pl.take("ps5") if j < 2 else [("V", pl.n["V"])]),
                        inc="P")
                    pv = smf[:, 8:16]
                    v_m = vchain(pl, [
                        lambda e, j=j: e.tensor_copy(
                            pv, ps[5][:, (j % 2) * 8:(j % 2) * 8 + 8]),
                        lambda e: e.reduce_max(smf[:, 0:1], pv, AX.X),
                        lambda e: e.tensor_scalar(smf[:, 1:2], smf[:, 0:1],
                                                  -1.0, None, ALU.mult),
                    ], wait=[("P", ptr)] + ([("P", ptw_prev)] if ptw_prev else []))
                    a_e = op(pl, "sc", lambda e: e.activation(
                        smf[:, 16:24], pv, AF.Exp, bias=smf[:, 1:2],
                        accum_out=smf[:, 2:3]),
                        wait=[("V", v_m)], inc="A")
                    v_w = vchain(pl, [
                        lambda e: e.reciprocal(smf[:, 3:4], smf[:, 2:3]),
                        lambda e: e.tensor_tensor(
                            pv, smf[:, 16:24], bcx(smf[:, 3:4], smf[:, 16:24]), ALU.mult),
                        lambda e: e.reduce_max(smf[:, 0:1], pv, AX.X),
                        lambda e: e.tensor_tensor(
                            smf[:, 16:24], pv, bcx(smf[:, 0:1], pv), ALU.is_ge),
                        lambda e: e.scalar_tensor_tensor(
                            smf[:, 24:32], smf[:, 16:24], -1e30, pv, ALU.mult, ALU.add),
                        lambda e: e.reduce_max(smf[:, 1:2], smf[:, 24:32], AX.X),
                        lambda e: e.tensor_tensor(
                            smf[:, 16:24], pv, bcx(smf[:, 1:2], pv), ALU.is_ge),
                        lambda e: e.tensor_mul(smf[:, 24:32], pv, smf[:, 16:24]),
                        lambda e: e.tensor_scalar(
                            smb[:, 0:8], smf[:, 24:32], 0.5, None, ALU.mult),
                    ], wait=[("A", a_e)])
                    ptw = op(pl, "te", lambda e, half=half: e.transpose(
                        pt0[0:8, half:half + 128], smb[:, 0:8], idB),
                        wait=[("V", v_w)] + pl.take("pt0h%d" % (j % 2)), inc="P")
                    a_w = op(pl, "sc", lambda e, j=j, half=half: e.copy(
                        wT_sb[0:8, j * 128:(j + 1) * 128], pt0[0:8, half:half + 128]),
                        wait=[("P", ptw)], inc="A")
                    pl.free_at("pt0h%d" % (j % 2), "A", a_w)
                    ptw_prev = ptw
                pl.free_at("scf_pr", "P", ptw_prev)
                pl.free_at("ps5", "V", pl.n["V"])
                a_wlast = pl.n["A"]
                order = list(range(8)) if st_i % 2 == 0 else list(range(7, -1, -1))
                if DBG_EXPERTS is not None:
                    order = [e_ for e_ in order if e_ in DBG_EXPERTS]
                for e_i in order:
                    l_wr = dma(pl, "g", "L", w_row1[0:1, 0:512], wT_sb[e_i:e_i + 1, 0:512],
                               wait=[("A", a_wlast)] + pl.take("w_row1"))
                    l_g = load_slot(0, ("W1", e_i), c_W1[e_i], D, FF)
                    l_u = load_slot(1, ("W3", e_i), c_W3[e_i], D, FF)
                    l_d = load_slot(2, ("W2", e_i), c_W2[e_i], FF, D)
                    a_bc = 0
                    for (bi, (bsrc, bdst)) in enumerate([(b1[e_i], b1col), (b3[e_i], b3col)]):
                        lb = dma(pl, "g", "L", bst[:, :],
                                 bsrc.rearrange("(c t) -> c t", t=128),
                                 wait=pl.take("bst"))
                        pt_b = op(pl, "te", lambda e: e.transpose(
                            pt0[0:128, 0:16], bst[:, :], idB[0:16, 0:16]),
                            wait=[("L", lb)] + pl.take("pt0f") + pl.take("pt0h0")
                                 + pl.take("pt0h1"), inc="P")
                        pl.free_at("bst", "P", pt_b)
                        a_bc = op(pl, "sc", lambda e, bdst=bdst: e.copy(
                            bdst[:, 0:16], pt0[0:128, 0:16]),
                            wait=[("P", pt_b)], inc="A")
                        pl.free_at("pt0f", "A", a_bc)
                    mmg(pl, ps[4][:, :], [(oner_b128, w_row1[0:1, 0:512])],
                        wait=[("L", l_wr)] + pl.take("ps4"), inc=False)
                    pw = mmg(pl, ps[5][:, :], [(oner_b128, w_row1[0:1, 0:512])],
                             wait=pl.take("ps5"))
                    pl.free_at("w_row1", "P", pw)
                    a_wbc = op(pl, "sc", lambda e: e.copy(scr[:, 0:512], ps[4][:, :]),
                               wait=[("P", pw)] + pl.take("wbc"), inc="A")
                    pl.free_at("ps4", "A", a_wbc)
                    pl.free_at("ps5", "A", a_wbc)
                    for hc in range(16):
                        bgi = "ps%d" % (hc % 2)
                        bui = "ps%d" % (2 + hc % 2)
                        sgr = hb[:, 10240 + (hc % 2) * 512:10240 + (hc % 2) * 512 + 512]
                        usr = hb[:, 8192 + (hc % 2) * 512:8192 + (hc % 2) * 512 + 512]
                        p_g = mmg(pl, ps[hc % 2],
                                  [(slotv(0)[:, c * 2048 + hc * 128:c * 2048 + hc * 128 + 128],
                                    xh[:, c * 1024:c * 1024 + 512]) for c in range(8)],
                                  wait=([("L", l_g), ("V", v_ln)] if hc == 0 else [])
                                  + pl.take(bgi))
                        p_u = mmg(pl, ps[2 + hc % 2],
                                  [(slotv(1)[:, c * 2048 + hc * 128:c * 2048 + hc * 128 + 128],
                                    xh[:, c * 1024:c * 1024 + 512]) for c in range(8)],
                                  wait=([("L", l_u)] if hc == 0 else [])
                                  + pl.take(bui))
                        ag = op(pl, "sc", lambda e, hc=hc, b_=ps[hc % 2], sgr=sgr:
                                e.activation(sgr, b_, AF.Identity,
                                             bias=b1col[:, hc:hc + 1]),
                                wait=[("P", p_g), ("A", a_bc)] + pl.take("sgr%d" % (hc % 2)),
                                inc="A")
                        pl.free_at(bgi, "A", ag)
                        au = op(pl, "v", lambda e, hc=hc, b_=ps[2 + hc % 2], sgr=sgr, usr=usr:
                                e.scalar_tensor_tensor(
                                    usr, b_, b3col[:, hc:hc + 1], sgr, ALU.add, ALU.mult),
                                wait=[("P", p_u), ("A", ag)], inc="V")
                        pl.free_at(bui, "V", au)
                        ag2 = op(pl, "sc", lambda e, sgr=sgr, usr=usr: e.activation(
                            sgr, usr, AF.Silu),
                            wait=[("V", au)], inc="A")
                        vh_ = op(pl, "v", lambda e, hc=hc, sgr=sgr: e.tensor_mul(
                            hb[:, hc * 512:(hc + 1) * 512], sgr, scr[:, 0:512]),
                            wait=[("A", ag2)] + ([("A", a_wbc)] if hc == 0 else []),
                            inc="V")
                        pl.free_at("sgr%d" % (hc % 2), "V", vh_)
                    v_h = pl.n["V"]
                    for oc in range(8):
                        bdi = "ps%d" % (4 + oc % 2)
                        p_d = mmg(pl, ps[4 + oc % 2],
                                  [(slotv(2)[:, ic * 1024 + oc * 128:ic * 1024 + oc * 128 + 128],
                                    hb[:, ic * 512:(ic + 1) * 512]) for ic in range(16)],
                                  wait=([("L", l_d), ("V", v_h), ("V", v_mx)] if oc == 0 else [])
                                  + pl.take(bdi))
                        vd = op(pl, "v", lambda e, oc=oc, b_=ps[4 + oc % 2]: e.tensor_add(
                            xs[:, oc * 1024 + 512:oc * 1024 + 1024],
                            xs[:, oc * 1024 + 512:oc * 1024 + 1024], b_),
                            wait=[("P", p_d)], inc="V")
                        pl.free_at(bdi, "V", vd)
                    slot_freeP[0] = slot_freeP[1] = pl.n["P"] - 8
                    slot_freeP[2] = pl.n["P"]
                    pl.free_at("wbc", "V", pl.n["V"])
                l_f = load_slot(2, "Wf", c_Wf, D, D)
                v_mix = pl.n["V"]
                a_mb = 0
                for oc in range(8):
                    p_b2 = mmg(pl, ps[4 + oc % 2],
                               [(b2_sb[:, oc * 128:(oc + 1) * 128], wT_sb[:, 0:512])],
                               wait=([("A", pl.n["A"]), ("V", v_mix)] if oc == 0 else [])
                               + pl.take("ps%d" % (4 + oc % 2)))
                    a_mb = op(pl, "v", lambda e, oc=oc, b_=ps[4 + oc % 2]: e.tensor_add(
                        hb[:, oc * 512:(oc + 1) * 512],
                        xs[:, oc * 1024 + 512:oc * 1024 + 1024], b_),
                        wait=[("P", p_b2)], inc="V")
                    pl.free_at("ps%d" % (4 + oc % 2), "V", a_mb)
                v_mb = pl.n["V"]
                for oc in range(8):
                    bk_ = "ps%d" % (oc % 2)
                    p_f = mmg(pl, ps[oc % 2],
                              [(slotv(2)[:, ic * 1024 + oc * 128:ic * 1024 + oc * 128 + 128],
                                hb[:, ic * 512:(ic + 1) * 512]) for ic in range(8)],
                              wait=([("L", l_f), ("V", v_mb)] if oc == 0 else [])
                              + pl.take(bk_))
                    v_f = op(pl, "v", lambda e, oc=oc, b_=ps[oc % 2]: e.scalar_tensor_tensor(
                        xs[:, oc * 1024:oc * 1024 + 512], b_, bfcol[:, oc:oc + 1],
                        xs[:, oc * 1024:oc * 1024 + 512], ALU.add, ALU.add),
                        wait=[("P", p_f)], inc="V")
                    pl.free_at(bk_, "V", v_f)
                slot_freeP[2] = pl.n["P"]
                st = store_resid(dst, t0, TS, pl.n["V"])
                pl.free_at("hb", "P", pl.n["P"])
                pl.free_at("scr", "V", pl.n["V"])
                pl.free_at("xh", "P", pl.n["P"])
                pl.barrier()

        def attn_l1(src, dst):
            TS = 1024
            for i in range(3):
                slot_state[i] = None
            l_w = 0
            for (wi, (w_, col)) in enumerate([(w2qT, 0), (w2kT, 8192),
                                              (w2vT, 16384), (w2oT, 24576)]):
                l_w = dma(pl, "g", "L",
                          wsl[:, col:col + 8192].rearrange("p (c t) -> p c t", t=1024),
                          r3(w_, D), wait=[("P", slot_freeP[0])] if wi == 0 else [])
            v_z = op(pl, "v", lambda e: e.memset(bcol[:, 32:40], 0.0), inc="V")
            for st_i in range(4):
                t0 = st_i * TS
                v_ln = ln_stripe(src, t0, TS)
                for (pi, (col, dstT)) in enumerate([(0, qTd), (8192, kTd), (16384, vTd)]):
                    y_prev = pl.take("hb_proj")
                    a_last = 0
                    v_lastp = 0
                    for oc in range(8):
                        for sub in range(2):
                            o = sub * 512
                            bk_ = "ps%d" % ((oc * 2 + sub) % 4)
                            p_p = mmg(pl, ps[(oc * 2 + sub) % 4],
                                      [(wsl[:, col + c * 1024 + oc * 128:col + c * 1024 + oc * 128 + 128],
                                        xh[:, c * 1024 + o:c * 1024 + o + 512]) for c in range(8)],
                                      wait=([("L", l_w), ("V", v_ln)] if oc == 0 and sub == 0 else [])
                                      + pl.take(bk_))
                            if pi < 2:
                                v_lastp = op(pl, "v", lambda e, oc=oc, o=o,
                                             b_=ps[(oc * 2 + sub) % 4]: e.tensor_scalar(
                                                 hb[:, oc * 1024 + o:oc * 1024 + o + 512], b_,
                                                 0.0, 1e-6, ALU.max, ALU.add),
                                             wait=[("P", p_p)] + (y_prev if oc == 0 and sub == 0 else []),
                                             inc="V")
                                pl.free_at(bk_, "V", v_lastp)
                                a_last = None
                            else:
                                a_last = op(pl, "sc", lambda e, oc=oc, o=o,
                                            b_=ps[(oc * 2 + sub) % 4]: e.copy(
                                                hb[:, oc * 1024 + o:oc * 1024 + o + 512], b_),
                                            wait=[("P", p_p)] + (y_prev if oc == 0 and sub == 0 else []),
                                            inc="A")
                                pl.free_at(bk_, "A", a_last)
                    wsem = [("A", a_last)] if a_last else [("V", v_lastp)]
                    st = dma(pl, "sy", "Y",
                             r3(dstT, S)[:, :, t0:t0 + TS],
                             hb.rearrange("p (c t) -> p c t", t=1024)[:, 0:8, 0:TS],
                             wait=wsem)
                    pl.free_at("hb_proj", "Y", st)
                pl.free_at("hb", "Y", pl.n["Y"])
                pl.barrier()
            QH = wsl[0:64, 32768:36864]
            KH = wsl[0:64, 36864:40960]
            VH = wsl[0:64, 40960:45056]
            AH = wsl[64:128, 40960:45056]
            kvb65 = scr[0:64, 512:577]
            v_one2 = op(pl, "v", lambda e: e.memset(vt_sb[:, 64:65], 1.0), inc="V")
            for h in range(NH):
                lq = dma(pl, "sy", "Y", QH, qTd[64 * h:64 * h + 64, :],
                         wait=pl.take("slot2"))
                lk = dma(pl, "sy", "Y", KH, kTd[64 * h:64 * h + 64, :])
                lv = dma(pl, "sy", "Y", VH, vTd[64 * h:64 * h + 64, :])
                v_kv = vchain(pl, [
                    lambda e: e.memset(kv_sb, 0.0),
                    lambda e: e.memset(kvb65, 0.0),
                ], wait=[("V", v_one2)] + pl.take("kvst"))
                for ci in range(32):
                    T0 = ci * 128
                    bA = "psE%d" % (ci % 2)
                    p_a = mmg(pl, ps[ci % 2][:, 0:128],
                              [(KH[:, T0:T0 + 128], QH[:, T0:T0 + 128])],
                              wait=([("Y", lv)] if ci == 0 else []) + pl.take(bA))
                    v_a = op(pl, "v", lambda e, b_=ps[ci % 2]: e.tensor_mul(
                        at_sb, b_[:, 0:128], cmask[:, 0:128]),
                        wait=[("P", p_a)] + pl.take("at"), inc="V")
                    pl.free_at(bA, "V", v_a)
                    pt_kv = 0
                    for (ti, SRC) in enumerate([KH, VH]):
                        pt_kv = op(pl, "te", lambda e, ti=ti, SRC=SRC, T0=T0: e.transpose(
                            pt1[:, ti * 64:ti * 64 + 64], SRC[:, T0:T0 + 128],
                            idB[0:64, 0:64]),
                            wait=(pl.take("pt1") if ti == 0 else []), inc="P" if ti else None)
                    a_kt = op(pl, "sc", lambda e: e.copy(kt_sb, pt1[:, 0:64]),
                              wait=[("P", pt_kv)] + pl.take("ktvt"))
                    a_vt = op(pl, "sc", lambda e: e.copy(vt_sb[:, 0:64], pt1[:, 64:128]),
                              inc="A")
                    pl.free_at("pt1", "A", a_vt)
                    mmg(pl, ps[2][:, 0:65], [(at_sb, vt_sb)],
                        wait=[("V", v_a), ("A", a_vt), ("V", v_kv)] + pl.take("ps2"),
                        start=True, stop=False, inc=False)
                    p_o = mmg(pl, ps[2][:, 0:65],
                              [(QH[:, T0:T0 + 128], kvb65)],
                              start=False, stop=True)
                    p_kv = mmg(pl, ps[3][0:64, 0:65], [(kt_sb, vt_sb)],
                               wait=pl.take("ps3s"))
                    v_o = vchain(pl, [
                        lambda e: e.tensor_scalar(smf[:, 105:106], ps[2][:, 64:65],
                                                  1e-6, None, ALU.add),
                        lambda e: e.reciprocal(smf[:, 106:107], smf[:, 105:106]),
                        lambda e: e.tensor_tensor(
                            smb[:, 0:64], ps[2][:, 0:64],
                            bcx(smf[:, 106:107], ps[2][:, 0:64]), ALU.mult),
                    ], wait=[("P", p_o)])
                    pl.free_at("ps2", "V", v_o)
                    v_kv = vchain(pl, [
                        lambda e: e.tensor_add(kv_sb, kv_sb, ps[3][0:64, 0:65]),
                        lambda e: e.tensor_copy(kvb65, kv_sb),
                    ], wait=[("P", p_kv)])
                    pl.free_at("ps3s", "V", v_kv)
                    pl.free_at("ktvt", "P", p_kv)
                    pl.free_at("at", "P", p_o)
                    pt_o = op(pl, "te", lambda e, ci=ci: e.transpose(
                        pt0[0:64, 128 + (ci % 4) * 128:128 + (ci % 4) * 128 + 128],
                        smb[:, 0:64], idB),
                        wait=[("V", v_o)] + (pl.take("pt0o") if ci % 4 == 0 else []),
                        inc="P")
                    if ci % 4 == 3:
                        v_oe = op(pl, "v", lambda e, ci=ci: e.tensor_copy(
                            AH[:, (ci - 3) * 128:(ci + 1) * 128], pt0[0:64, 128:640]),
                            wait=[("P", pt_o)], inc="V")
                        pl.free_at("pt0o", "V", v_oe)
                pl.free_at("kvst", "V", pl.n["V"])
                st_a = dma(pl, "sy", "Y", aTd[64 * h:64 * h + 64, :], AH,
                           wait=[("V", pl.n["V"])])
                pl.free_at("slot2", "Y", st_a)
            pl.barrier()
            for st_i in range(4):
                t0 = st_i * TS
                lx = dma(pl, "sy", "Y",
                         xs.rearrange("p (c t) -> p c t", t=1024)[:, :, 0:TS],
                         r3(src, S)[:, :, t0:t0 + TS], wait=pl.take("xs"))
                la = dma(pl, "sy", "Y",
                         hb.rearrange("p (c t) -> p c t", t=1024)[:, 0:8, 0:TS],
                         r3(aTd, S)[:, :, t0:t0 + TS], wait=pl.take("hb"))
                v_last = 0
                for oc in range(8):
                    for sub in range(2):
                        o = sub * 512
                        bk_ = "ps%d" % ((oc * 2 + sub) % 4)
                        p_p = mmg(pl, ps[(oc * 2 + sub) % 4],
                                  [(wsl[:, 24576 + c * 1024 + oc * 128:
                                        24576 + c * 1024 + oc * 128 + 128],
                                    hb[:, c * 1024 + o:c * 1024 + o + 512]) for c in range(8)],
                                  wait=([("Y", la), ("Y", lx), ("V", v_z)]
                                        if oc == 0 and sub == 0 else [])
                                  + pl.take(bk_))
                        v_last = op(pl, "v", lambda e, oc=oc, o=o,
                                    b_=ps[(oc * 2 + sub) % 4]: e.tensor_add(
                                        xs[:, oc * 1024 + o:oc * 1024 + o + 512],
                                        xs[:, oc * 1024 + o:oc * 1024 + o + 512], b_),
                                    wait=[("P", p_p)], inc="V")
                        pl.free_at(bk_, "V", v_last)
                st = store_resid(dst, t0, TS, v_last)
                pl.free_at("hb", "Y", st)
                pl.barrier()
            for i in range(3):
                slot_state[i] = None
                slot_freeP[i] = pl.n["P"]

        if "A" in phases:
            ffn_l0(xT, resid, load_router=True)
        if "B" in phases:
            attn_l0(resid if "A" in phases else xT, resid)
        if "C" in phases:
            ffn_l0(resid if ("A" in phases or "B" in phases) else xT, resid,
                   load_router="A" not in phases)
        ran = any(p in phases for p in "ABC")
        if "D" in phases:
            ffn_l1(resid if ran else xT, resid, load_gw=True)
            ran = True
        if "E" in phases:
            attn_l1(resid if ran else xT, resid)
            ran = True
        if "F" in phases:
            ffn_l1(resid if ran else xT, resid, load_gw="D" not in phases)

        ran_any = any(p in phases for p in "ABCDEF")
        src_final = resid if ran_any else xT
        done = dma(pl, "sy", "Y", y[:, :], src_final[:, :])

        @pl.on("sy")
        def _(eng, SM, done=done):
            eng.wait_ge(SM["Y"], done)

        with nc.Block() as block:
            @block.tensor
            def _(eng):
                for fn in pl.q["te"]:
                    fn(eng, SM)

            @block.scalar
            def _(eng):
                for fn in pl.q["sc"]:
                    fn(eng, SM)

            @block.vector
            def _(eng):
                for fn in pl.q["v"]:
                    fn(eng, SM)

            @block.gpsimd
            def _(eng):
                for fn in pl.q["g"]:
                    fn(eng, SM)

            @block.sync
            def _(eng):
                for fn in pl.q["sy"]:
                    fn(eng, SM)

    return nc


def _host_inputs(inputs):
    g = {k: np.asarray(v) for k, v in inputs.items()}
    T = lambda a: np.ascontiguousarray(np.asarray(a, np.float32).T)
    C = lambda a: np.ascontiguousarray(np.asarray(a, np.float32))
    common = {
        "wqT": T(g["L0_attn_wq"]), "wkT": T(g["L0_attn_wk"]),
        "wvT": T(g["L0_attn_wv"]), "woT": T(g["L0_attn_wo"]),
        "bq": C(g["L0_attn_bq"]), "bk": C(g["L0_attn_bk"]),
        "bv": C(g["L0_attn_bv"]), "bo": C(g["L0_attn_bo"]),
        "pw1T": np.ascontiguousarray(np.asarray(g["L0_attn_pw1"], np.float32).transpose(0, 2, 1)),
        "pb1": C(g["L0_attn_pb1"]),
        "pw2T": np.ascontiguousarray(np.asarray(g["L0_attn_pw2"], np.float32).transpose(0, 2, 1)),
        "pb2": C(g["L0_attn_pb2"]),
        "sw1T": T(g["L0_attn_sw1"]), "sb1": C(g["L0_attn_sb1"]),
        "sw2T": T(g["L0_attn_sw2"]), "sb2": C(g["L0_attn_sb2"]),
        "sgT": T(g["L0_ffn_shared_gate"]), "suT": T(g["L0_ffn_shared_up"]),
        "sdT": T(g["L0_ffn_shared_down"]),
        "gw": C(g["L0_ffn_gate_w"]), "uw": C(g["L0_ffn_up_w"]),
        "dw": C(g["L0_ffn_down_w"]),
        "rwT": T(g["L0_ffn_router_w"]), "rb": C(g["L0_ffn_routing_bias"]),
        "w2qT": T(g["L1_attn_Wq"]), "w2kT": T(g["L1_attn_Wk"]),
        "w2vT": T(g["L1_attn_Wv"]), "w2oT": T(g["L1_attn_Wo"]),
        "W1": C(g["L1_ffn_W1"]), "W3": C(g["L1_ffn_W3"]), "W2": C(g["L1_ffn_W2"]),
        "b1": C(g["L1_ffn_b1"]), "b3": C(g["L1_ffn_b3"]), "b2": C(g["L1_ffn_b2"]),
        "Wg": C(g["L1_ffn_Wg"]), "bg": C(g["L1_ffn_bg"]),
        "Wf": C(g["L1_ffn_Wf"]), "bf": C(g["L1_ffn_bf"]),
        "identF": np.eye(128, dtype=np.float32),
        "identB": np.eye(128, dtype=ml_dtypes.bfloat16),
        "mask128": np.tril(np.ones((128, 128))).astype(ml_dtypes.bfloat16),
    }
    common["maskTu"] = np.triu(np.ones((128, 128))).astype(ml_dtypes.bfloat16)
    x = np.asarray(g["L0_attn_x"], np.float32)
    maps = []
    for c in range(8):
        m = dict(common)
        m["xT"] = np.ascontiguousarray(x[c].T)
        maps.append(m)
    return maps


def kernel(**inputs):
    nc = build_program()
    maps = _host_inputs(inputs)
    decl = set(DECLARED)
    maps = [{k: v for k, v in m.items() if k in decl} for m in maps]
    res = run_bass_kernel_spmd(nc, maps, core_ids=list(range(8)))
    out = np.stack([np.asarray(res.results[c]["y"], np.float32).T for c in range(8)])
    return np.ascontiguousarray(out)
